# version 01
# baseline (speedup 1.0000x reference)
import contextlib
import numpy as np
import concourse.bass as bass
import concourse.mybir as mybir
from concourse.bass_utils import run_bass_kernel_spmd

F32 = mybir.dt.float32
BF16 = mybir.dt.bfloat16
AF = mybir.ActivationFunctionType
ALU = mybir.AluOpType
AX = mybir.AxisListType

B, T, DM = 4, 4096, 1024
TH = T // 2
DI = 2048
NS = 16
NCORES = 8

_BUILD_CACHE = {}


def _chunkvec(v, c):
    return np.ascontiguousarray(np.asarray(v, np.float32).reshape(c, 128).T)


def _fm(a):
    return np.ascontiguousarray(np.asarray(a, np.float32).T)


class Prog:

    ENGINES = ("sync", "scalar", "vector", "tensor", "gpsimd")

    def __init__(self, name):
        self.nc = bass.Bass("TRN2", target_bir_lowering=False)
        self.es = contextlib.ExitStack()
        self.name = name
        self.steps = []
        self._sem_total = 0
        self.dram = {}

    def din(self, name, shape, dt=F32):
        ap = self.nc.dram_tensor(name, list(shape), dt, kind="ExternalInput").ap()
        self.dram[name] = ap
        return ap

    def dout(self, name, shape, dt=F32):
        ap = self.nc.dram_tensor(name, list(shape), dt, kind="ExternalOutput").ap()
        self.dram[name] = ap
        return ap

    def sbuf(self, name, shape, dt=F32):
        return self.es.enter_context(self.nc.sbuf_tensor(name, list(shape), dt))

    def psum(self, name, shape, dt=F32):
        return self.es.enter_context(self.nc.psum_tensor(name, list(shape), dt))

    def step(self, engine, fn, dma=False):
        self.steps.append((engine, fn, 16 if dma else 1))

    def finalize(self):
        nc = self.nc
        chain = self.es.enter_context(nc.semaphore("chain"))
        block = self.es.enter_context(nc.Block())
        totals = []
        acc = 0
        for (_, _, inc) in self.steps:
            totals.append(acc)
            acc += inc
        final_total = acc
        steps = self.steps

        def make(engine_name):
            def emit(q):
                for i, (eng, fn, inc) in enumerate(steps):
                    if eng != engine_name:
                        continue
                    q.wait_ge(chain, totals[i])
                    last = fn(q)
                    assert last is not None, f"step {i} returned no instruction"
                    last.then_inc(chain, inc)
                q.wait_ge(chain, final_total)
            return emit

        for eng in self.ENGINES:
            getattr(block, eng)(make(eng))
        self.es.close()
        return nc



def emit_ln_stats(p, q_t, pst1, pst2, x_tiles, x2_tile, w, nchunk):
    for c in range(nchunk):
        q_t.matmul(out=pst1[0:1, 0:w], lhsT=p.ones_col[:, 0:1], rhs=x_tiles(c),
                   start=(c == 0), stop=(c == nchunk - 1))
    for c in range(nchunk):
        mm = q_t.matmul(out=pst2[0:1, 0:w], lhsT=p.ones_col[:, 0:1], rhs=x2_tile(c),
                        start=(c == 0), stop=(c == nchunk - 1))
    return mm


def steps_ln_smalls(p, pst1, pst2, w, nfeat, sm):
    inv = 1.0 / nfeat
    p.step("vector", lambda q: q.tensor_scalar_mul(
        out=sm["mu"][0:1, 0:w], in0=pst1[0:1, 0:w], scalar1=inv))
    p.step("vector", lambda q: q.tensor_scalar_mul(
        out=sm["ex2"][0:1, 0:w], in0=pst2[0:1, 0:w], scalar1=inv))
    p.step("vector", lambda q: q.tensor_mul(
        out=sm["var"][0:1, 0:w], in0=sm["mu"][0:1, 0:w], in1=sm["mu"][0:1, 0:w]))
    p.step("vector", lambda q: q.tensor_sub(
        out=sm["var"][0:1, 0:w], in0=sm["ex2"][0:1, 0:w], in1=sm["var"][0:1, 0:w]))
    p.step("vector", lambda q: q.tensor_scalar_add(
        out=sm["var"][0:1, 0:w], in0=sm["var"][0:1, 0:w], scalar1=1e-5))
    p.step("scalar", lambda q: q.activation(
        out=sm["lnv"][0:1, 0:w], in_=sm["var"][0:1, 0:w], func=AF.Ln))
    p.step("scalar", lambda q: q.activation(
        out=sm["rstd"][0:1, 0:w], in_=sm["lnv"][0:1, 0:w], func=AF.Exp, scale=-0.5))
    p.step("vector", lambda q: q.tensor_mul(
        out=sm["nmr"][0:1, 0:w], in0=sm["mu"][0:1, 0:w], in1=sm["rstd"][0:1, 0:w]))
    p.step("vector", lambda q: q.tensor_scalar_mul(
        out=sm["nmr"][0:1, 0:w], in0=sm["nmr"][0:1, 0:w], scalar1=-1.0))


def emit_replicate2(p, q_t, prepA, prepB, sm, w):
    q_t.matmul(out=prepA[:, 0:w], lhsT=p.ones_row[0:1, :], rhs=sm["rstd"][0:1, 0:w],
               start=True, stop=True)
    return q_t.matmul(out=prepB[:, 0:w], lhsT=p.ones_row[0:1, :],
                      rhs=sm["nmr"][0:1, 0:w], start=True, stop=True)



def build_A0():
    p = Prog("A0")
    nc = p.nc
    TE = TH + 8

    xT = p.din("xT", [DM, TE])
    lng1 = p.din("lng1", [128, 8])
    lnb1 = p.din("lnb1", [128, 8])
    w_in = p.din("w_in", [DM, 4 * DM])
    b_in = p.din("b_in", [128, 32])
    w_conv = p.din("w_conv", [128, 16 * 4])
    b_conv = p.din("b_conv", [128, 16])
    w_xp = p.din("w_xp", [DI, 96])
    w_dt = p.din("w_dt", [64, DI])
    b_dt = p.din("b_dt", [128, 16])
    hmask = p.din("hmask", [128, 8])

    z_o = p.dout("z", [DI, TH])
    xc_o = p.dout("xc", [DI, TH])
    dt_o = p.dout("dt", [DI, TH])
    bc_o = p.dout("bc", [32, TH])
    u0_o = p.dout("u0", [DM, TH])

    STE = 512 + 8
    xt = p.sbuf("xt", [128, 8 * STE])
    hn = p.sbuf("hn", [128, 8 * STE], BF16)
    xn = p.sbuf("xn", [128, 8 * 512])
    x2 = p.sbuf("x2", [128, 8 * 512])
    xi = p.sbuf("xi", [128, 16 * STE])
    xcs = p.sbuf("xcs", [128, 512])
    xcb = p.sbuf("xcb", [128, 16 * 512], BF16)
    zs = p.sbuf("zs", [128, 512])
    wst = [p.sbuf(f"wst{i}", [128, 8 * 256]) for i in range(2)]
    wbf = [p.sbuf(f"wbf{i}", [128, 8 * 256], BF16) for i in range(2)]
    lg = p.sbuf("lg", [128, 8])
    lb = p.sbuf("lb", [128, 8])
    bin_t = p.sbuf("bin_t", [128, 32])
    wcv = p.sbuf("wcv", [128, 16 * 4])
    bcv = p.sbuf("bcv", [128, 16])
    wxp = p.sbuf("wxp", [128, 16 * 96])
    wxpb = p.sbuf("wxpb", [128, 16 * 96], BF16)
    wdt = p.sbuf("wdt", [64, DI])
    wdtb = p.sbuf("wdtb", [64, DI], BF16)
    bdt = p.sbuf("bdt", [128, 16])
    hm = p.sbuf("hm", [128, 8])
    proj = p.sbuf("proj", [96, 512])
    projb = p.sbuf("projb", [64, 512], BF16)
    ddt = p.sbuf("ddt", [128, 512])
    dta = p.sbuf("dta", [128, 512])
    dtb_ = p.sbuf("dtb_", [128, 512])
    rep = p.sbuf("rep", [128, 2 * 512])
    p.ones_col = p.sbuf("ones_col", [128, 1])
    p.ones_row = p.sbuf("ones_row", [1, 128])
    sm = {k: p.sbuf("sm_" + k, [1, 512]) for k in
          ("mu", "ex2", "var", "lnv", "rstd", "nmr")}
    p.sm = sm

    pst1 = p.psum("pst1", [1, 512])
    pst2 = p.psum("pst2", [1, 512])
    prep = p.psum("prep", [128, 512])
    prep2 = p.psum("prep2", [128, 512])
    pmm = [p.psum(f"pmm{i}", [128, 512]) for i in range(2)]
    pxp = p.psum("pxp", [96, 512])

    p.step("vector", lambda q: q.memset(p.ones_col[:], 1.0))
    p.step("vector", lambda q: q.memset(p.ones_row[:], 1.0))

    def ld(dst, src):
        p.step("sync", lambda q, d=dst, s=src: q.dma_start(out=d, in_=s), dma=True)
    ld(lg[:], lng1)
    ld(lb[:], lnb1)
    ld(bin_t[:], b_in)
    ld(wcv[:], w_conv)
    ld(bcv[:], b_conv)
    ld(wxp[:], w_xp.rearrange("(c p) n -> p c n", p=128))
    ld(wdt[:], w_dt)
    ld(bdt[:], b_dt)
    ld(hm[:], hmask)
    p.step("scalar", lambda q: q.copy(out=wxpb[:], in_=wxp[:]))
    p.step("scalar", lambda q: q.copy(out=wdtb[:], in_=wdt[:]))

    for st in range(4):
        ebase = st * 512
        p.step("sync", lambda q, e=ebase: q.dma_start(
            out=xt[:].rearrange("p (c t) -> p c t", c=8),
            in_=xT.rearrange("(c p) t -> p c t", p=128)[:, :, e:e + STE]), dma=True)

        for (off, w) in ((0, 8), (8, 512)):
            xt_c = lambda c, o=off, ww=w: xt[:, c * STE + o: c * STE + o + ww]
            x2_c = lambda c, ww=w: x2[:, c * 512: c * 512 + ww]
            xn_c = lambda c, ww=w: xn[:, c * 512: c * 512 + ww]
            hn_c = lambda c, o=off, ww=w: hn[:, c * STE + o: c * STE + o + ww]

            def sq1(q, ww=w, xt_c=xt_c, x2_c=x2_c):
                for c in range(8):
                    last = q.activation(out=x2_c(c), in_=xt_c(c), func=AF.Square)
                return last
            p.step("scalar", sq1)
            p.step("tensor", lambda q, ww=w, xt_c=xt_c, x2_c=x2_c:
                   emit_ln_stats(p, q, pst1, pst2, xt_c, x2_c, ww, 8))
            steps_ln_smalls(p, pst1, pst2, w, DM, sm)
            p.step("tensor", lambda q, ww=w: emit_replicate2(p, q, prep, prep2, sm, ww))

            def apply1a(q, ww=w, xt_c=xt_c, xn_c=xn_c):
                for c in range(8):
                    last = q.tensor_tensor(out=xn_c(c), in0=xt_c(c), in1=prep[:, 0:ww],
                                           op=ALU.mult)
                return last
            p.step("vector", apply1a)

            def apply1b(q, ww=w, xn_c=xn_c):
                for c in range(8):
                    last = q.tensor_tensor(out=xn_c(c), in0=xn_c(c),
                                           in1=prep2[:, 0:ww], op=ALU.add)
                return last
            p.step("vector", apply1b)
            if off == 8:
                p.step("sync", lambda q, st=st, ww=w: q.dma_start(
                    out=u0_o.rearrange("(c p) t -> p c t", p=128)
                    [:, :, st * 512: st * 512 + ww],
                    in_=xn[:].rearrange("p (c t) -> p c t", c=8)[:, :, 0:ww]), dma=True)

            def sq2(q, ww=w, xn_c=xn_c, x2_c=x2_c):
                for c in range(8):
                    last = q.activation(out=x2_c(c), in_=xn_c(c), func=AF.Square)
                return last
            p.step("scalar", sq2)
            p.step("tensor", lambda q, ww=w, xn_c=xn_c, x2_c=x2_c:
                   emit_ln_stats(p, q, pst1, pst2, xn_c, x2_c, ww, 8))
            steps_ln_smalls(p, pst1, pst2, w, DM, sm)
            p.step("tensor", lambda q, ww=w: emit_replicate2(p, q, prep, prep2, sm, ww))

            def apply2a(q, ww=w, xn_c=xn_c):
                for c in range(8):
                    last = q.tensor_tensor(out=xn_c(c), in0=xn_c(c), in1=prep[:, 0:ww],
                                           op=ALU.mult)
                return last
            p.step("vector", apply2a)

            def apply2bb(q, ww=w, xn_c=xn_c):
                for c in range(8):
                    last = q.tensor_tensor(out=xn_c(c), in0=xn_c(c),
                                           in1=prep2[:, 0:ww], op=ALU.add)
                return last
            p.step("vector", apply2bb)

            def apply2b(q, ww=w, xn_c=xn_c, hn_c=hn_c):
                for c in range(8):
                    last = q.scalar_tensor_tensor(
                        out=hn_c(c), in0=xn_c(c), scalar=lg[:, c:c + 1],
                        in1=lb[:, c:c + 1].broadcast_to([128, ww]),
                        op0=ALU.mult, op1=ALU.add)
                return last
            p.step("vector", apply2b)

        for slab in range(16):
            ridx = slab % 2
            p.step("sync", lambda q, s=slab, r=ridx: q.dma_start(
                out=wst[r][:].rearrange("p (c n) -> p c n", c=8),
                in_=w_in.rearrange("(c p) n -> p c n", p=128)[:, :, s * 256:(s + 1) * 256]),
                dma=True)
            p.step("scalar", lambda q, r=ridx: q.copy(out=wbf[r][:], in_=wst[r][:]))
            for mm_i in range(2):
                m = slab * 2 + mm_i
                for (off, w) in ((0, 8), (8, 512)):
                    if m >= 16 and off == 0:
                        continue
                    pb = pmm[m % 2]

                    def mmf(q, m=m, mm_i=mm_i, off=off, w=w, pb=pb, r=ridx):
                        for k in range(8):
                            last = q.matmul(
                                out=pb[:, 0:w],
                                lhsT=wbf[r][:, k * 256 + mm_i * 128: k * 256 + mm_i * 128 + 128],
                                rhs=hn[:, k * STE + off: k * STE + off + w],
                                start=(k == 0), stop=(k == 7))
                        return last
                    p.step("tensor", mmf)

                    if m < 16:
                        def ev(q, m=m, off=off, w=w, pb=pb):
                            return q.tensor_tensor(
                                out=xi[:, m * STE + off: m * STE + off + w],
                                in0=pb[:, 0:w],
                                in1=bin_t[:, m:m + 1].broadcast_to([128, w]),
                                op=ALU.add)
                        p.step("vector", ev)
                    else:
                        def evz(q, m=m, off=off, w=w, pb=pb):
                            return q.tensor_tensor(
                                out=zs[:, 0:w], in0=pb[:, 0:w],
                                in1=bin_t[:, m:m + 1].broadcast_to([128, w]),
                                op=ALU.add)
                        p.step("vector", evz)

                        def stz(q, m=m, off=off, w=w, st=st):
                            return q.dma_start(
                                out=z_o.rearrange("(c p) t -> p c t", p=128)
                                [:, m - 16, st * 512 + off - 8: st * 512 + off - 8 + w],
                                in_=zs[:, 0:w])
                        p.step("sync", stz, dma=True)

        if st == 0:
            def hmul(q):
                for c in range(16):
                    last = q.tensor_tensor(out=xi[:, c * STE: c * STE + 8],
                                           in0=xi[:, c * STE: c * STE + 8],
                                           in1=hm[:], op=ALU.mult)
                return last
            p.step("vector", hmul)

        for c in range(16):
            def conv4(q, c=c):
                q.tensor_scalar(out=xcs[:, 0:512],
                                in0=xi[:, c * STE + 5: c * STE + 5 + 512],
                                scalar1=wcv[:, c * 4: c * 4 + 1], scalar2=None,
                                op0=ALU.mult)
                for k in (1, 2, 3):
                    last = q.scalar_tensor_tensor(
                        out=xcs[:, 0:512],
                        in0=xi[:, c * STE + 5 + k: c * STE + 5 + k + 512],
                        scalar=wcv[:, c * 4 + k: c * 4 + k + 1],
                        in1=xcs[:, 0:512],
                        op0=ALU.mult, op1=ALU.add)
                return last
            p.step("vector", conv4)

            def sil(q, c=c):
                return q.activation(out=xcs[:, 0:512], in_=xcs[:, 0:512], func=AF.Silu,
                                    bias=bcv[:, c:c + 1])
            p.step("scalar", sil)
            def silb(q, c=c):
                return q.tensor_copy(out=xcb[:, c * 512:(c + 1) * 512], in_=xcs[:, 0:512])
            p.step("vector", silb)
            def stxc(q, c=c, st=st):
                return q.dma_start(
                    out=xc_o.rearrange("(c p) t -> p c t", p=128)[:, c, st * 512:(st + 1) * 512],
                    in_=xcs[:, 0:512])
            p.step("sync", stxc, dma=True)

        def mmxp(q):
            for k in range(16):
                last = q.matmul(out=pxp[:, 0:512],
                                lhsT=wxpb[:, k * 96:(k + 1) * 96],
                                rhs=xcb[:, k * 512: (k + 1) * 512],
                                start=(k == 0), stop=(k == 15))
            return last
        p.step("tensor", mmxp)
        p.step("vector", lambda q: q.tensor_copy(out=proj[:], in_=pxp[:, 0:512]))
        p.step("scalar", lambda q: q.copy(out=projb[:], in_=proj[0:64, :]))
        p.step("sync", lambda q, st=st: q.dma_start(
            out=bc_o[:, st * 512:(st + 1) * 512], in_=proj[64:96, :]), dma=True)

        for m in range(16):
            def mmdt(q, m=m):
                return q.matmul(out=pmm[m % 2][:, 0:512],
                                lhsT=wdtb[:, m * 128:(m + 1) * 128],
                                rhs=projb[:, 0:512], start=True, stop=True)
            p.step("tensor", mmdt)
            p.step("vector", lambda q, m=m: q.tensor_tensor(
                out=ddt[:, 0:512], in0=pmm[m % 2][:, 0:512],
                in1=bdt[:, m:m + 1].broadcast_to([128, 512]), op=ALU.add))

            p.step("scalar", lambda q, m=m: q.activation(out=dta[:], in_=ddt[:],
                                                         func=AF.Abs))
            p.step("scalar", lambda q, m=m: q.activation(out=dta[:], in_=dta[:],
                                                         func=AF.Exp, scale=-1.0))
            p.step("scalar", lambda q, m=m: q.activation(out=dta[:], in_=dta[:],
                                                         func=AF.Ln, bias=1.0))
            p.step("scalar", lambda q, m=m: q.activation(out=dtb_[:], in_=ddt[:],
                                                         func=AF.Relu))
            p.step("vector", lambda q, m=m: q.tensor_add(out=dta[:], in0=dta[:], in1=dtb_[:]))
            p.step("sync", lambda q, m=m, st=st: q.dma_start(
                out=dt_o.rearrange("(c p) t -> p c t", p=128)[:, m, st * 512:(st + 1) * 512],
                in_=dta[:]), dma=True)

    return p.finalize()


def run_A0(inputs, X):
    nc = _BUILD_CACHE.setdefault("A0", build_A0())
    w_in = np.ascontiguousarray(inputs["L0_attn_in_proj_w"].T)
    w_xp = np.ascontiguousarray(inputs["L0_attn_x_proj_w"].T)
    w_dt = np.ascontiguousarray(inputs["L0_attn_dt_proj_w"].T)
    base = {
        "lng1": _chunkvec(inputs["L0_attn_ln_mamba_g"], 8),
        "lnb1": _chunkvec(inputs["L0_attn_ln_mamba_b"], 8),
        "w_in": w_in,
        "b_in": _chunkvec(inputs["L0_attn_in_proj_b"], 32),
        "w_conv": np.ascontiguousarray(
            np.asarray(inputs["L0_attn_conv_w"], np.float32).reshape(16, 128, 4)
            .transpose(1, 0, 2).reshape(128, 64)),
        "b_conv": _chunkvec(inputs["L0_attn_conv_b"], 16),
        "w_xp": w_xp,
        "w_dt": w_dt,
        "b_dt": _chunkvec(inputs["L0_attn_dt_proj_b"], 16),
    }
    in_maps = []
    for c in range(NCORES):
        b, s = c // 2, c % 2
        if s == 0:
            seg = np.concatenate([np.zeros((8, DM), np.float32), X[b, 0:TH]], axis=0)
            hmask = np.zeros((128, 8), np.float32)
        else:
            seg = X[b, TH - 8: T]
            hmask = np.ones((128, 8), np.float32)
        m = dict(base)
        m["xT"] = np.ascontiguousarray(seg.T.astype(np.float32))
        m["hmask"] = hmask
        in_maps.append(m)
    res = run_bass_kernel_spmd(nc, in_maps, core_ids=list(range(NCORES)))
    U0 = np.empty((B, DM, T), np.float32)
    Z = np.empty((B, DI, T), np.float32)
    XC = np.empty((B, DI, T), np.float32)
    DT = np.empty((B, DI, T), np.float32)
    BC = np.empty((B, 32, T), np.float32)
    for c in range(NCORES):
        b, s = c // 2, c % 2
        r = res.results[c]
        U0[b, :, s * TH:(s + 1) * TH] = r["u0"]
        Z[b, :, s * TH:(s + 1) * TH] = r["z"]
        XC[b, :, s * TH:(s + 1) * TH] = r["xc"]
        DT[b, :, s * TH:(s + 1) * TH] = r["dt"]
        BC[b, :, s * TH:(s + 1) * TH] = r["bc"]
    return U0, Z, XC, DT, BC


def kernel(**inputs):
    raise NotImplementedError("phases under construction")



def build_A1(NT_=4, NN_=NS):
    p = Prog("A1")
    TT_ = 1024

    dts = p.din("dts", [1024, T])
    xcs = p.din("xcs", [1024, T])
    alog = p.din("alog", [128, 128])
    sel = p.din("sel", [16, 16 * 128])
    Bh = p.din("Bh", [16, T])
    Ch = p.din("Ch", [16, T])
    ys_o = p.dout("ys", [1024, T])

    dtb = p.sbuf("dtb", [128, 8 * TT_])
    xcb = p.sbuf("xcb", [128, 8 * TT_])
    ub = p.sbuf("ub", [128, 8 * TT_])
    yb = p.sbuf("yb", [128, 8 * TT_])
    brep = p.sbuf("brep", [128, TT_])
    crep = p.sbuf("crep", [128, TT_])
    dab = p.sbuf("dab", [128, TT_])
    tmpb = p.sbuf("tmpb", [128, TT_])
    hb = p.sbuf("hb", [128, TT_])
    acol = p.sbuf("acol", [128, 128])
    state = p.sbuf("state", [128, 128])
    bsml = p.sbuf("bsml", [16, TT_])
    csml = p.sbuf("csml", [16, TT_])
    selt = p.sbuf("selt", [16, 16 * 128])
    p.ones_row = p.sbuf("ones_row", [1, 128])
    prep1 = p.psum("prep1", [128, TT_])
    prep2 = p.psum("prep2", [128, TT_])

    p.step("vector", lambda q: q.memset(p.ones_row[:], 1.0))
    p.step("vector", lambda q: q.memset(state[:], 0.0))
    p.step("sync", lambda q: q.dma_start(out=acol[:], in_=alog), dma=True)
    p.step("sync", lambda q: q.dma_start(out=selt[:], in_=sel), dma=True)

    p.step("scalar", lambda q: q.activation(out=acol[:], in_=acol[:], func=AF.Exp))
    p.step("vector", lambda q: q.tensor_scalar_mul(out=acol[:], in0=acol[:], scalar1=-1.0))

    for it in range(NT_):
        t0 = it * TT_
        p.step("sync", lambda q, t0=t0: q.dma_start(
            out=dtb[:].rearrange("p (c t) -> p c t", c=8),
            in_=dts.rearrange("(c p) t -> p c t", p=128)[:, :, t0:t0 + TT_]), dma=True)
        p.step("sync", lambda q, t0=t0: q.dma_start(
            out=xcb[:].rearrange("p (c t) -> p c t", c=8),
            in_=xcs.rearrange("(c p) t -> p c t", p=128)[:, :, t0:t0 + TT_]), dma=True)
        p.step("sync", lambda q, t0=t0: q.dma_start(
            out=bsml[:], in_=Bh[:, t0:t0 + TT_]), dma=True)
        p.step("sync", lambda q, t0=t0: q.dma_start(
            out=csml[:], in_=Ch[:, t0:t0 + TT_]), dma=True)

        def uf(q):
            for g in range(8):
                last = q.tensor_mul(out=ub[:, g * TT_:(g + 1) * TT_],
                                    in0=dtb[:, g * TT_:(g + 1) * TT_],
                                    in1=xcb[:, g * TT_:(g + 1) * TT_])
            return last
        p.step("vector", uf)

        for n in range(NN_):
            def repmm(q, n=n):
                lh = selt[:, n * 128:(n + 1) * 128]
                q.matmul(out=prep1[:, 0:512], lhsT=lh, rhs=bsml[:, 0:512],
                         start=True, stop=True)
                q.matmul(out=prep1[:, 512:1024], lhsT=lh, rhs=bsml[:, 512:1024],
                         start=True, stop=True)
                q.matmul(out=prep2[:, 0:512], lhsT=lh, rhs=csml[:, 0:512],
                         start=True, stop=True)
                return q.matmul(out=prep2[:, 512:1024], lhsT=lh, rhs=csml[:, 512:1024],
                                start=True, stop=True)
            p.step("tensor", repmm)
            p.step("vector", lambda q: q.tensor_copy(out=brep[:], in_=prep1[:]))
            p.step("vector", lambda q: q.tensor_copy(out=crep[:], in_=prep2[:]))

            for g in range(8):
                col = g * 16 + n
                p.step("scalar", lambda q, g=g, col=col: q.activation(
                    out=dab[:], in_=dtb[:, g * TT_:(g + 1) * TT_], func=AF.Exp,
                    scale=acol[:, col:col + 1]))
                p.step("vector", lambda q, g=g: q.tensor_mul(
                    out=tmpb[:], in0=ub[:, g * TT_:(g + 1) * TT_], in1=brep[:]))
                p.step("vector", lambda q, col=col: q.tensor_tensor_scan(
                    out=hb[:], data0=dab[:], data1=tmpb[:],
                    initial=state[:, col:col + 1], op0=ALU.mult, op1=ALU.add))
                p.step("vector", lambda q, col=col: q.tensor_copy(
                    out=state[:, col:col + 1], in_=hb[:, TT_ - 1:TT_]))
                p.step("gpsimd", lambda q: q.tensor_mul(out=tmpb[:], in0=hb[:], in1=crep[:]))
                if n == 0:
                    p.step("vector", lambda q, g=g: q.tensor_copy(
                        out=yb[:, g * TT_:(g + 1) * TT_], in_=tmpb[:]))
                else:
                    p.step("vector", lambda q, g=g: q.tensor_add(
                        out=yb[:, g * TT_:(g + 1) * TT_],
                        in0=yb[:, g * TT_:(g + 1) * TT_], in1=tmpb[:]))

        p.step("sync", lambda q, t0=t0: q.dma_start(
            out=ys_o.rearrange("(c p) t -> p c t", p=128)[:, :, t0:t0 + TT_],
            in_=yb[:].rearrange("p (c t) -> p c t", c=8)), dma=True)

    return p.finalize()


def run_A1(inputs, DT, XC, BC):
    nc = _BUILD_CACHE.setdefault("A1", build_A1())
    A_log = np.asarray(inputs["L0_attn_A_log"], np.float32)
    in_maps = []
    for c in range(NCORES):
        b, s = c // 2, c % 2
        sl = slice(s * 1024, (s + 1) * 1024)
        sel = np.zeros((16, 16 * 128), np.float32)
        for n in range(16):
            sel[n, n * 128:(n + 1) * 128] = 1.0
        in_maps.append({
            "sel": sel,
            "dts": np.ascontiguousarray(DT[b, sl]),
            "xcs": np.ascontiguousarray(XC[b, sl]),
            "alog": np.ascontiguousarray(
                A_log[sl].reshape(8, 128, 16).transpose(1, 0, 2).reshape(128, 128)),
            "Bh": np.ascontiguousarray(BC[b, 0:16]),
            "Ch": np.ascontiguousarray(BC[b, 16:32]),
        })
    res = run_bass_kernel_spmd(nc, in_maps, core_ids=list(range(NCORES)))
    YS = np.empty((B, DI, T), np.float32)
    for c in range(NCORES):
        b, s = c // 2, c % 2
        YS[b, s * 1024:(s + 1) * 1024] = res.results[c]["ys"]
    return YS



def ln_setup(p, maxw=512, share=None):
    p.ones_col = p.sbuf("ln_ones_col", [128, 1])
    p.ones_row = p.sbuf("ln_ones_row", [1, 128])
    p.ln_x2 = p.sbuf("ln_x2", [128, 8 * maxw])
    p.ln_sm = {k: p.sbuf("ln_sm_" + k, [1, 512]) for k in
               ("mu", "ex2", "var", "lnv", "rstd", "nmr")}
    p.ln_maxw = maxw
    if share is not None:
        p.ln_pst1, p.ln_pst2, p.ln_prepA, p.ln_prepB = share
    else:
        p.ln_pst1 = p.psum("ln_pst1", [1, 512])
        p.ln_pst2 = p.psum("ln_pst2", [1, 512])
        p.ln_prepA = p.psum("ln_prepA", [128, 512])
        p.ln_prepB = p.psum("ln_prepB", [128, 512])
    p.step("vector", lambda q: q.memset(p.ones_col[:], 1.0))
    p.step("vector", lambda q: q.memset(p.ones_row[:], 1.0))


def steps_ln(p, src_fn, dst_fn, w, nchunk=8, nfeat=DM, gain=None, bias=None,
             dst2_fn=None):
    sm = p.ln_sm
    x2c = lambda c: p.ln_x2[:, c * p.ln_maxw: c * p.ln_maxw + w]

    def sq(q):
        for c in range(nchunk):
            last = q.activation(out=x2c(c), in_=src_fn(c), func=AF.Square)
        return last
    p.step("scalar", sq)
    p.step("tensor", lambda q: emit_ln_stats(p, q, p.ln_pst1, p.ln_pst2,
                                             src_fn, x2c, w, nchunk))
    steps_ln_smalls(p, p.ln_pst1, p.ln_pst2, w, nfeat, sm)
    p.step("tensor", lambda q: emit_replicate2(p, q, p.ln_prepA, p.ln_prepB, sm, w))

    def mul_(q):
        for c in range(nchunk):
            last = q.tensor_tensor(out=x2c(c), in0=src_fn(c), in1=p.ln_prepA[:, 0:w],
                                   op=ALU.mult)
        return last
    p.step("vector", mul_)

    if gain is None:
        def add_(q):
            for c in range(nchunk):
                last = q.tensor_tensor(out=dst_fn(c), in0=x2c(c),
                                       in1=p.ln_prepB[:, 0:w], op=ALU.add)
            return last
        p.step("vector", add_)
        if dst2_fn is not None:
            def cp2(q):
                for c in range(nchunk):
                    last = q.tensor_copy(out=dst2_fn(c), in_=dst_fn(c))
                return last
            p.step("vector", cp2)
    else:
        def add_(q):
            for c in range(nchunk):
                last = q.tensor_tensor(out=x2c(c), in0=x2c(c),
                                       in1=p.ln_prepB[:, 0:w], op=ALU.add)
            return last
        p.step("vector", add_)

        def gb(q):
            for c in range(nchunk):
                last = q.scalar_tensor_tensor(
                    out=dst_fn(c), in0=x2c(c), scalar=gain[:, c:c + 1],
                    in1=bias[:, c:c + 1].broadcast_to([128, w]),
                    op0=ALU.mult, op1=ALU.add)
            return last
        p.step("vector", gb)
        if dst2_fn is not None:
            def cp2(q):
                for c in range(nchunk):
                    last = q.tensor_copy(out=dst2_fn(c), in_=dst_fn(c))
                return last
            p.step("vector", cp2)


def steps_stream_matmul(p, wdram, KC, MC, rhs_fn, evac_fn, tws, wst, wbf, pss,
                        mgroup=2):
    slabw = 128 * mgroup
    for si in range(0, MC, mgroup):
        r = (si // mgroup) % len(wst)
        p.step("sync", lambda q, si=si, r=r: q.dma_start(
            out=wst[r][:, 0:KC * slabw].rearrange("p (c n) -> p c n", c=KC),
            in_=wdram.rearrange("(c p) n -> p c n", p=128)
            [:, :, si * 128: si * 128 + slabw]), dma=True)
        p.step("scalar", lambda q, r=r: q.copy(out=wbf[r][:, 0:KC * slabw],
                                               in_=wst[r][:, 0:KC * slabw]))
        for mi in range(mgroup):
            m = si + mi
            if m >= MC:
                break
            for (t0, w) in tws:
                pb = pss[(m * len(tws) + tws.index((t0, w))) % len(pss)]

                def mmf(q, m=m, mi=mi, t0=t0, w=w, pb=pb, r=r):
                    for k in range(KC):
                        last = q.matmul(
                            out=pb[:, 0:w],
                            lhsT=wbf[r][:, k * slabw + mi * 128: k * slabw + mi * 128 + 128],
                            rhs=rhs_fn(k, t0, w),
                            start=(k == 0), stop=(k == KC - 1))
                    return last
                p.step("tensor", mmf)
                evac_fn(m, t0, w, pb)



def build_A2a():
    p = Prog("A2a")
    xs = p.din("xs", [DM, TH])
    ysd = p.din("ysd", [DI, TH])
    xcd = p.din("xcd", [DI, TH])
    zd = p.din("zd", [DI, TH])
    dsk = p.din("dsk", [128, 16])
    w_op = p.din("w_op", [DI, DM])
    lng = p.din("lng", [128, 8])
    lnb = p.din("lnb", [128, 8])
    w_rt = p.din("w_rt", [DM, 8])
    ident = p.din("ident", [128, 128])

    hres_o = p.dout("hres", [DM, TH])
    min_o = p.dout("min", [DM, TH], BF16)
    cfm_o = p.dout("cfm", [8, TH])

    hres = p.sbuf("hres_s", [128, 8 * TH])
    xst = p.sbuf("xst", [128, 8 * 256])
    ysb = p.sbuf("ysb", [128, 16 * 256])
    xcb = p.sbuf("xcb", [128, 16 * 256])
    ybf = p.sbuf("ybf", [128, 16 * 256], BF16)
    dskt = p.sbuf("dskt", [128, 16])
    wst = [p.sbuf(f"wst{i}", [128, 16 * 128]) for i in range(1)]
    wbf = [p.sbuf(f"wbf{i}", [128, 16 * 128], BF16) for i in range(1)]
    lg = p.sbuf("lg", [128, 8])
    lb = p.sbuf("lb", [128, 8])
    minb = p.sbuf("minb", [128, 8 * 512], BF16)
    minf = p.sbuf("minf", [128, 8 * 512])
    wrt = p.sbuf("wrt", [128, 8 * 8])
    identt = p.sbuf("identt", [128, 128])
    lgts = p.sbuf("lgts", [8, TH])
    lgtm = p.sbuf("lgtm", [128, 8])
    rmax = p.sbuf("rmax", [128, 1])
    rsum = p.sbuf("rsum", [128, 1])
    rrec = p.sbuf("rrec", [128, 1])
    probs = p.sbuf("probs", [128, 8])
    gtc = p.sbuf("gtc", [128, 64])
    cnt = p.sbuf("cnt", [128, 8])
    ind = p.sbuf("ind", [128, 8])
    wsum = p.sbuf("wsum", [128, 1])
    wrec = p.sbuf("wrec", [128, 1])
    ctm = p.sbuf("ctm", [128, 8])
    cfm = p.sbuf("cfm_t", [8, 128])
    pmm = [p.psum(f"pmm{i}", [128, 512]) for i in range(2)]
    prt = p.psum("prt", [8, 512])
    ptr = p.psum("ptr", [128, 128])
    ln_setup(p)

    TWS = [(t0, 512) for t0 in range(0, TH, 512)]

    def ld(dst, src):
        p.step("sync", lambda q, d=dst, s=src: q.dma_start(out=d, in_=s), dma=True)
    ld(dskt[:], dsk)
    ld(lg[:], lng)
    ld(lb[:], lnb)
    ld(wrt[:].rearrange("p (c n) -> p c n", c=8), w_rt.rearrange("(c p) n -> p c n", p=128))
    ld(identt[:], ident)

    for t0 in range(0, TH, 256):
        p.step("sync", lambda q, t0=t0: q.dma_start(
            out=ysb[:].rearrange("p (c t) -> p c t", c=16),
            in_=ysd.rearrange("(c p) t -> p c t", p=128)[:, :, t0:t0 + 256]), dma=True)
        p.step("sync", lambda q, t0=t0: q.dma_start(
            out=xcb[:].rearrange("p (c t) -> p c t", c=16),
            in_=xcd.rearrange("(c p) t -> p c t", p=128)[:, :, t0:t0 + 256]), dma=True)

        def ytail(q):
            for c in range(16):
                last = q.scalar_tensor_tensor(
                    out=ysb[:, c * 256:(c + 1) * 256],
                    in0=xcb[:, c * 256:(c + 1) * 256], scalar=dskt[:, c:c + 1],
                    in1=ysb[:, c * 256:(c + 1) * 256], op0=ALU.mult, op1=ALU.add)
            return last
        p.step("vector", ytail)
        p.step("sync", lambda q, t0=t0: q.dma_start(
            out=xcb[:].rearrange("p (c t) -> p c t", c=16),
            in_=zd.rearrange("(c p) t -> p c t", p=128)[:, :, t0:t0 + 256]), dma=True)

        def sz(q):
            for c in range(16):
                last = q.activation(out=xcb[:, c * 256:(c + 1) * 256],
                                    in_=xcb[:, c * 256:(c + 1) * 256], func=AF.Silu)
            return last
        p.step("scalar", sz)

        def ymul(q):
            for c in range(16):
                last = q.tensor_tensor(out=ybf[:, c * 256:(c + 1) * 256],
                                       in0=ysb[:, c * 256:(c + 1) * 256],
                                       in1=xcb[:, c * 256:(c + 1) * 256], op=ALU.mult)
            return last
        p.step("vector", ymul)
        p.step("sync", lambda q, t0=t0: q.dma_start(
            out=xst[:].rearrange("p (c t) -> p c t", c=8),
            in_=xs.rearrange("(c p) t -> p c t", p=128)[:, :, t0:t0 + 256]), dma=True)

        def evac(m, tt0, w_, pb, t0=t0):
            def f(q, m=m, pb=pb):
                return q.tensor_tensor(
                    out=hres[:, m * TH + t0: m * TH + t0 + 256],
                    in0=xst[:, m * 256:(m + 1) * 256], in1=pb[:, 0:256], op=ALU.add)
            p.step("vector", f)
        steps_stream_matmul(p, w_op, 16, 8,
                            lambda k, tt0, w_: ybf[:, k * 256: k * 256 + 256],
                            evac, [(0, 256)], wst, wbf, pmm, mgroup=1)

    for (t0, w) in TWS:
        src = lambda c, t0=t0: hres[:, c * TH + t0: c * TH + t0 + 512]
        dstb = lambda c: minb[:, c * 512:(c + 1) * 512]
        dstf = lambda c: minf[:, c * 512:(c + 1) * 512]
        steps_ln(p, src, dstb, 512, gain=lg, bias=lb, dst2_fn=dstf)

        def rt(q, t0=t0):
            for c in range(8):
                last = q.matmul(out=prt[:, 0:512], lhsT=wrt[:, c * 8:(c + 1) * 8],
                                rhs=minf[:, c * 512:(c + 1) * 512],
                                start=(c == 0), stop=(c == 7))
            return last
        p.step("tensor", rt)
        p.step("vector", lambda q, t0=t0: q.tensor_copy(out=lgts[:, t0:t0 + 512],
                                                        in_=prt[:, 0:512]))
        p.step("sync", lambda q, t0=t0: q.dma_start(
            out=min_o.rearrange("(c p) t -> p c t", p=128)[:, :, t0:t0 + 512],
            in_=minb[:].rearrange("p (c t) -> p c t", c=8)), dma=True)
    p.step("sync", lambda q: q.dma_start(
        out=hres_o.rearrange("(c p) t -> p c t", p=128),
        in_=hres[:].rearrange("p (c t) -> p c t", c=8)), dma=True)

    for ti in range(TH // 128):
        t0 = ti * 128
        p.step("tensor", lambda q, t0=t0: q.transpose(
            out=ptr[:, 0:8], in_=lgts[:, t0:t0 + 128], identity=identt[0:8, 0:8]))
        p.step("vector", lambda q: q.tensor_copy(out=lgtm[:], in_=ptr[0:128, 0:8]))
        p.step("vector", lambda q: q.reduce_max(out=rmax[:], in_=lgtm[:], axis=AX.X))
        p.step("vector", lambda q: q.tensor_scalar_mul(out=rmax[:], in0=rmax[:],
                                                       scalar1=-1.0))
        p.step("scalar", lambda q: q.activation(out=probs[:], in_=lgtm[:], func=AF.Exp,
                                                bias=rmax[:], accum_out=rsum[:]))
        p.step("vector", lambda q: q.reciprocal(out=rrec[:], in_=rsum[:]))
        p.step("scalar", lambda q: q.activation(out=probs[:], in_=probs[:],
                                                func=AF.Copy, scale=rrec[:]))
        p.step("vector", lambda q: q.tensor_tensor(
            out=gtc[:].rearrange("p (a b) -> p a b", a=8),
            in0=probs[:].unsqueeze(1).broadcast_to([128, 8, 8]),
            in1=probs[:].unsqueeze(2).broadcast_to([128, 8, 8]), op=ALU.is_gt))
        p.step("vector", lambda q: q.tensor_reduce(
            out=cnt[:], in_=gtc[:].rearrange("p (a b) -> p a b", a=8), axis=AX.X,
            op=ALU.add))
        p.step("vector", lambda q: q.tensor_single_scalar(
            out=ind[:], in_=cnt[:], scalar=1.5, op=ALU.is_lt))
        p.step("vector", lambda q: q.tensor_mul(out=ind[:], in0=ind[:], in1=probs[:]))
        p.step("vector", lambda q: q.tensor_reduce(out=wsum[:], in_=ind[:], axis=AX.X,
                                                   op=ALU.add))
        p.step("vector", lambda q: q.reciprocal(out=wrec[:], in_=wsum[:]))
        p.step("scalar", lambda q: q.activation(out=ctm[:], in_=ind[:], func=AF.Copy,
                                                scale=wrec[:]))
        p.step("tensor", lambda q: q.transpose(out=ptr[0:8, 0:128], in_=ctm[:],
                                               identity=identt[:]))
        p.step("vector", lambda q: q.tensor_copy(out=cfm[:], in_=ptr[0:8, 0:128]))
        p.step("sync", lambda q, t0=t0: q.dma_start(out=cfm_o[:, t0:t0 + 128],
                                                    in_=cfm[:]), dma=True)
    return p.finalize()


def run_A2a(inputs, X, YS, XC, Z):
    nc = _BUILD_CACHE.setdefault("A2a", build_A2a())
    base = {
        "dsk": _chunkvec(inputs["L0_attn_D_skip"], 16),
        "w_op": np.ascontiguousarray(np.asarray(inputs["L0_attn_out_proj_w"], np.float32).T),
        "lng": _chunkvec(inputs["L0_attn_ln_moe_g"], 8),
        "lnb": _chunkvec(inputs["L0_attn_ln_moe_b"], 8),
        "w_rt": np.ascontiguousarray(np.asarray(inputs["L0_attn_router_w"], np.float32).T),
        "ident": np.eye(128, dtype=np.float32),
    }
    in_maps = []
    for c in range(NCORES):
        b, s = c // 2, c % 2
        sl = slice(s * TH, (s + 1) * TH)
        m = dict(base)
        m["xs"] = np.ascontiguousarray(X[b][:, sl]) if X.shape[1] == DM else None
        m["ysd"] = np.ascontiguousarray(YS[b][:, sl])
        m["xcd"] = np.ascontiguousarray(XC[b][:, sl])
        m["zd"] = np.ascontiguousarray(Z[b][:, sl])
        in_maps.append(m)
    res = run_bass_kernel_spmd(nc, in_maps, core_ids=list(range(NCORES)))
    HR = np.empty((B, DM, T), np.float32)
    import ml_dtypes
    MIN = np.empty((B, DM, T), ml_dtypes.bfloat16)
    CFM = np.empty((B, 8, T), np.float32)
    for c in range(NCORES):
        b, s = c // 2, c % 2
        r = res.results[c]
        HR[b, :, s * TH:(s + 1) * TH] = r["hres"]
        MIN[b, :, s * TH:(s + 1) * TH] = r["min"]
        CFM[b, :, s * TH:(s + 1) * TH] = r["cfm"]
    return HR, MIN, CFM



def build_A2b():
    p = Prog("A2b")
    hres_i = p.din("hres", [DM, TH])
    min_i = p.din("min", [DM, TH], BF16)
    cfm_i = p.din("cfm", [8, TH])
    w1e = p.din("w1e", [8, DM, DI])
    b1e = p.din("b1e", [128, 8 * 16])
    w2e = p.din("w2e", [8, DI, DM])
    b2e = p.din("b2e", [128, 8 * 8])
    sel8 = p.din("sel8", [8, 8 * 128])
    xres_i = p.din("xres", [DM, TH])
    xo = p.dout("xo", [DM, TH])

    hres = p.sbuf("hres_s", [128, 8 * TH])
    minb = p.sbuf("minb", [128, 8 * TH], BF16)
    cfmt = p.sbuf("cfmt", [8, TH])
    selt = p.sbuf("selt", [8, 8 * 128])
    crep = p.sbuf("crep", [128, TH])
    h1cb = p.sbuf("h1cb", [128, 16 * TH], BF16)
    tmp5 = p.sbuf("tmp5", [128, 512])
    b1t = p.sbuf("b1t", [128, 8 * 16])
    b2t = p.sbuf("b2t", [128, 8 * 8])
    wst = [p.sbuf("wst0", [128, 16 * 128])]
    wbf = [p.sbuf("wbf0", [128, 16 * 128], BF16)]
    pmm = [p.psum(f"pmm{i}", [128, 512]) for i in range(2)]
    prep_c = p.psum("prep_c", [128, 512])

    def ld(dst, src):
        p.step("sync", lambda q, d=dst, s=src: q.dma_start(out=d, in_=s), dma=True)
    ld(hres[:].rearrange("p (c t) -> p c t", c=8),
       hres_i.rearrange("(c p) t -> p c t", p=128))
    ld(minb[:].rearrange("p (c t) -> p c t", c=8),
       min_i.rearrange("(c p) t -> p c t", p=128))
    ld(cfmt[:], cfm_i)
    ld(selt[:], sel8)
    for c8 in range(8):
        for tc in range(TH // 512):
            p.step("sync", lambda q, c8=c8, tc=tc: q.dma_start(
                out=tmp5[:],
                in_=xres_i.rearrange("(c p) t -> p c t", p=128)
                [:, c8, tc * 512:(tc + 1) * 512]), dma=True)
            p.step("vector", lambda q, c8=c8, tc=tc: q.tensor_tensor(
                out=hres[:, c8 * TH + tc * 512: c8 * TH + (tc + 1) * 512],
                in0=hres[:, c8 * TH + tc * 512: c8 * TH + (tc + 1) * 512],
                in1=tmp5[:], op=ALU.add))
    ld(b1t[:], b1e)
    ld(b2t[:], b2e)

    for e in range(8):
        for tc in range(TH // 512):
            p.step("tensor", lambda q, e=e, tc=tc: q.matmul(
                out=prep_c[:], lhsT=selt[:, e * 128:(e + 1) * 128],
                rhs=cfmt[:, tc * 512:(tc + 1) * 512], start=True, stop=True))
            p.step("vector", lambda q, tc=tc: q.tensor_copy(
                out=crep[:, tc * 512:(tc + 1) * 512], in_=prep_c[:]))

        def ev1(m, t0, w, pb, e=e):
            p.step("scalar", lambda q, m=m, pb=pb, e=e: q.activation(
                out=tmp5[:], in_=pb[:, 0:512], func=AF.Silu,
                bias=b1t[:, e * 16 + m: e * 16 + m + 1]))
            p.step("vector", lambda q, m=m, t0=t0: q.tensor_tensor(
                out=h1cb[:, m * TH + t0: m * TH + t0 + 512], in0=tmp5[:],
                in1=crep[:, t0:t0 + 512], op=ALU.mult))
        steps_stream_matmul(
            p, w1e[e], 8, 16,
            lambda k, t0, w: minb[:, k * TH + t0: k * TH + t0 + 512],
            ev1, [(t0, 512) for t0 in range(0, TH, 512)], wst, wbf, pmm, mgroup=1)

        def ev2(m, t0, w, pb, e=e):
            p.step("vector", lambda q, m=m, t0=t0, e=e: q.tensor_scalar(
                out=tmp5[:], in0=crep[:, t0:t0 + 512],
                scalar1=b2t[:, e * 8 + m: e * 8 + m + 1], scalar2=None, op0=ALU.mult))
            p.step("vector", lambda q, m=m, t0=t0, pb=pb: q.tensor_tensor(
                out=tmp5[:], in0=tmp5[:], in1=pb[:, 0:512], op=ALU.add))
            p.step("vector", lambda q, m=m, t0=t0: q.tensor_tensor(
                out=hres[:, m * TH + t0: m * TH + t0 + 512],
                in0=hres[:, m * TH + t0: m * TH + t0 + 512], in1=tmp5[:], op=ALU.add))
        steps_stream_matmul(
            p, w2e[e], 16, 8,
            lambda k, t0, w: h1cb[:, k * TH + t0: k * TH + t0 + 512],
            ev2, [(t0, 512) for t0 in range(0, TH, 512)], wst, wbf, pmm, mgroup=1)

    p.step("sync", lambda q: q.dma_start(
        out=xo.rearrange("(c p) t -> p c t", p=128),
        in_=hres[:].rearrange("p (c t) -> p c t", c=8)), dma=True)
    return p.finalize()


def run_A2b(inputs, HR, MIN, CFM, XRES):
    nc = _BUILD_CACHE.setdefault("A2b", build_A2b())
    w1 = np.asarray(inputs["L0_attn_w1"], np.float32)
    w2 = np.asarray(inputs["L0_attn_w2"], np.float32)
    b1 = np.asarray(inputs["L0_attn_b1"], np.float32)
    b2 = np.asarray(inputs["L0_attn_b2"], np.float32)
    sel8 = np.zeros((8, 8 * 128), np.float32)
    for e in range(8):
        sel8[e, e * 128:(e + 1) * 128] = 1.0
    base = {
        "w1e": np.ascontiguousarray(w1.transpose(0, 2, 1)),
        "w2e": np.ascontiguousarray(w2.transpose(0, 2, 1)),
        "b1e": np.ascontiguousarray(
            b1.reshape(8, 16, 128).transpose(2, 0, 1).reshape(128, 128)),
        "b2e": np.ascontiguousarray(
            b2.reshape(8, 8, 128).transpose(2, 0, 1).reshape(128, 64)),
        "sel8": sel8,
    }
    in_maps = []
    for c in range(NCORES):
        b, s = c // 2, c % 2
        sl = slice(s * TH, (s + 1) * TH)
        m = dict(base)
        m["hres"] = np.ascontiguousarray(HR[b][:, sl])
        m["min"] = np.ascontiguousarray(MIN[b][:, sl])
        m["cfm"] = np.ascontiguousarray(CFM[b][:, sl])
        m["xres"] = np.ascontiguousarray(XRES[b][:, sl])
        in_maps.append(m)
    res = run_bass_kernel_spmd(nc, in_maps, core_ids=list(range(NCORES)))
    XO = np.empty((B, DM, T), np.float32)
    for c in range(NCORES):
        b, s = c // 2, c % 2
        XO[b, :, s * TH:(s + 1) * TH] = res.results[c]["xo"]
    return XO



C_GELU = 0.7978845608028654
SQ_GELU = 0.21146087051789464


def build_F0():
    p = Prog("F0")
    xin = p.din("xin", [DM, TH])
    w_rt = p.din("w_rt", [DM, 8])
    w1 = p.din("w1", [DM, 4 * DM])
    b1c = p.din("b1c", [128, 32])
    w2 = p.din("w2", [4 * DM, DM])
    b2c = p.din("b2c", [128, 8])
    ident = p.din("ident", [128, 128])
    xo = p.dout("xo", [DM, TH])

    hres = p.sbuf("hres_s", [128, 8 * TH])
    ubf = p.sbuf("ubf", [128, 8 * 512], BF16)
    uf = p.sbuf("uf", [128, 8 * 512])
    h1cb = p.sbuf("h1cb", [128, 32 * 512], BF16)
    gx = p.sbuf("gx", [128, 512])
    gt = p.sbuf("gt", [128, 512])
    srow = p.sbuf("srow", [1, 512])
    srep = p.sbuf("srep", [128, 512])
    wrt = p.sbuf("wrt", [128, 8 * 8])
    b1t = p.sbuf("b1t", [128, 32])
    b2t = p.sbuf("b2t", [128, 8])
    identt = p.sbuf("identt", [128, 128])
    lgts = p.sbuf("lgts", [8, 512])
    lgtm = p.sbuf("lgtm", [128, 8])
    rmax = p.sbuf("rmax", [128, 1])
    rsum = p.sbuf("rsum", [128, 1])
    rrec = p.sbuf("rrec", [128, 1])
    probs = p.sbuf("probs", [128, 8])
    gtc = p.sbuf("gtc", [128, 64])
    cnt = p.sbuf("cnt", [128, 8])
    ind = p.sbuf("ind", [128, 8])
    wsum = p.sbuf("wsum", [128, 1])
    wrec = p.sbuf("wrec", [128, 1])
    stm = p.sbuf("stm", [128, 1])
    wst = [p.sbuf("wst0", [128, 32 * 128])]
    wbf = [p.sbuf("wbf0", [128, 32 * 128], BF16)]
    pmm = [p.psum(f"pmm{i}", [128, 512]) for i in range(2)]
    prt = p.psum("prt", [8, 512])
    ptr = p.psum("ptr", [128, 128])
    ln_setup(p)

    def ld(dst, src):
        p.step("sync", lambda q, d=dst, s=src: q.dma_start(out=d, in_=s), dma=True)
    ld(hres[:].rearrange("p (c t) -> p c t", c=8),
       xin.rearrange("(c p) t -> p c t", p=128))
    ld(wrt[:].rearrange("p (c n) -> p c n", c=8), w_rt.rearrange("(c p) n -> p c n", p=128))
    ld(b1t[:], b1c)
    ld(b2t[:], b2c)
    ld(identt[:], ident)

    for qtr in range(4):
        t0 = qtr * 512
        src_fn = lambda c, t0=t0: hres[:, c * TH + t0: c * TH + t0 + 512]
        dst_fn = lambda c: uf[:, c * 512:(c + 1) * 512]
        steps_ln(p, src_fn, dst_fn, 512)

        def castu(q):
            for c in range(8):
                last = q.tensor_copy(out=ubf[:, c * 512:(c + 1) * 512],
                                     in_=uf[:, c * 512:(c + 1) * 512])
            return last
        p.step("vector", castu)

        def rt(q):
            for c in range(8):
                last = q.matmul(out=prt[:, 0:512], lhsT=wrt[:, c * 8:(c + 1) * 8],
                                rhs=uf[:, c * 512:(c + 1) * 512],
                                start=(c == 0), stop=(c == 7))
            return last
        p.step("tensor", rt)
        p.step("vector", lambda q: q.tensor_copy(out=lgts[:], in_=prt[:, 0:512]))

        for ti in range(4):
            tt = ti * 128
            p.step("tensor", lambda q, tt=tt: q.transpose(
                out=ptr[:, 0:8], in_=lgts[:, tt:tt + 128], identity=identt[0:8, 0:8]))
            p.step("vector", lambda q: q.tensor_copy(out=lgtm[:], in_=ptr[0:128, 0:8]))
            p.step("vector", lambda q: q.reduce_max(out=rmax[:], in_=lgtm[:], axis=AX.X))
            p.step("vector", lambda q: q.tensor_scalar_mul(out=rmax[:], in0=rmax[:],
                                                           scalar1=-1.0))
            p.step("scalar", lambda q: q.activation(out=probs[:], in_=lgtm[:],
                                                    func=AF.Exp, bias=rmax[:],
                                                    accum_out=rsum[:]))
            p.step("vector", lambda q: q.reciprocal(out=rrec[:], in_=rsum[:]))
            p.step("scalar", lambda q: q.activation(out=probs[:], in_=probs[:],
                                                    func=AF.Copy, scale=rrec[:]))
            p.step("vector", lambda q: q.tensor_tensor(
                out=gtc[:].rearrange("p (a b) -> p a b", a=8),
                in0=probs[:].unsqueeze(1).broadcast_to([128, 8, 8]),
                in1=probs[:].unsqueeze(2).broadcast_to([128, 8, 8]), op=ALU.is_gt))
            p.step("vector", lambda q: q.tensor_reduce(
                out=cnt[:], in_=gtc[:].rearrange("p (a b) -> p a b", a=8), axis=AX.X,
                op=ALU.add))
            p.step("vector", lambda q: q.tensor_single_scalar(
                out=ind[:], in_=cnt[:], scalar=1.5, op=ALU.is_lt))
            p.step("vector", lambda q: q.tensor_mul(out=ind[:], in0=ind[:], in1=probs[:]))
            p.step("vector", lambda q: q.tensor_reduce(out=wsum[:], in_=ind[:],
                                                       axis=AX.X, op=ALU.add))
            p.step("vector", lambda q: q.reciprocal(out=wrec[:], in_=wsum[:]))
            p.step("vector", lambda q: q.tensor_mul(out=stm[:], in0=wsum[:], in1=wrec[:]))
            p.step("tensor", lambda q: q.transpose(out=ptr[0:1, 0:128], in_=stm[:],
                                                   identity=identt[:]))
            p.step("vector", lambda q, tt=tt: q.tensor_copy(out=srow[0:1, tt:tt + 128],
                                                            in_=ptr[0:1, 0:128]))
        p.step("tensor", lambda q: q.matmul(out=ptr[:, 0:128] if False else pmm[0][:, 0:512],
                                            lhsT=p.ones_row[0:1, :], rhs=srow[0:1, 0:512],
                                            start=True, stop=True))
        p.step("vector", lambda q: q.tensor_copy(out=srep[:], in_=pmm[0][:, 0:512]))

        def ev1(m, tt0, w, pb):
            p.step("vector", lambda q, m=m, pb=pb: q.tensor_tensor(
                out=gx[:], in0=pb[:, 0:512],
                in1=b1t[:, m:m + 1].broadcast_to([128, 512]), op=ALU.add))
            p.step("scalar", lambda q: q.activation(out=gt[:], in_=gx[:],
                                                    func=AF.Square, scale=SQ_GELU))
            p.step("vector", lambda q: q.scalar_tensor_tensor(
                out=gt[:], in0=gt[:], scalar=1.0, in1=gx[:], op0=ALU.add, op1=ALU.mult))
            p.step("scalar", lambda q: q.activation(out=gt[:], in_=gt[:],
                                                    func=AF.Sigmoid, scale=2 * C_GELU))
            p.step("vector", lambda q, m=m: q.tensor_mul(
                out=h1cb[:, m * 512:(m + 1) * 512], in0=gt[:], in1=gx[:]))
        steps_stream_matmul(p, w1, 8, 32,
                            lambda k, tt0, w: ubf[:, k * 512: k * 512 + 512],
                            ev1, [(0, 512)], wst, wbf, pmm, mgroup=1)

        def ev2(m, tt0, w, pb, t0=t0):
            p.step("vector", lambda q, m=m, pb=pb: q.tensor_tensor(
                out=gx[:], in0=pb[:, 0:512],
                in1=b2t[:, m:m + 1].broadcast_to([128, 512]), op=ALU.add))
            p.step("vector", lambda q: q.tensor_mul(out=gx[:], in0=gx[:], in1=srep[:]))
            p.step("vector", lambda q, m=m, t0=t0: q.tensor_tensor(
                out=hres[:, m * TH + t0: m * TH + t0 + 512],
                in0=hres[:, m * TH + t0: m * TH + t0 + 512], in1=gx[:], op=ALU.add))
        steps_stream_matmul(p, w2, 32, 8,
                            lambda k, tt0, w: h1cb[:, k * 512: k * 512 + 512],
                            ev2, [(0, 512)], wst, wbf, pmm, mgroup=1)

    p.step("sync", lambda q: q.dma_start(
        out=xo.rearrange("(c p) t -> p c t", p=128),
        in_=hres[:].rearrange("p (c t) -> p c t", c=8)), dma=True)
    return p.finalize()


def run_F0(inputs, X1):
    nc = _BUILD_CACHE.setdefault("F0", build_F0())
    base = {
        "w_rt": np.ascontiguousarray(np.asarray(inputs["L0_ffn_router_w"], np.float32)),
        "w1": np.ascontiguousarray(np.asarray(inputs["L0_ffn_w1"], np.float32)),
        "b1c": _chunkvec(inputs["L0_ffn_b1"], 32),
        "w2": np.ascontiguousarray(np.asarray(inputs["L0_ffn_w2"], np.float32)),
        "b2c": _chunkvec(inputs["L0_ffn_b2"], 8),
        "ident": np.eye(128, dtype=np.float32),
    }
    in_maps = []
    for c in range(NCORES):
        b, s = c // 2, c % 2
        m = dict(base)
        m["xin"] = np.ascontiguousarray(X1[b][:, s * TH:(s + 1) * TH])
        in_maps.append(m)
    res = run_bass_kernel_spmd(nc, in_maps, core_ids=list(range(NCORES)))
    XO = np.empty((B, DM, T), np.float32)
    for c in range(NCORES):
        b, s = c // 2, c % 2
        XO[b, :, s * TH:(s + 1) * TH] = res.results[c]["xo"]
    return XO



def build_B0():
    p = Prog("B0")
    xin = p.din("xin", [DM, TH])
    wqkv = p.din("wqkv", [DM, 3 * DM])
    bqkv = p.din("bqkv", [128, 24])
    wg = p.din("wg", [128, 8])
    bg = p.din("bg", [1, 1])
    qkv_o = p.dout("qkv", [3 * DM, TH], BF16)
    gate_o = p.dout("gate", [1, TH])

    hres = p.sbuf("hres_s", [128, 8 * TH])
    ubf = p.sbuf("ubf", [128, 8 * TH], BF16)
    uf = p.sbuf("uf", [128, 8 * 512])
    bq = p.sbuf("bq", [128, 24])
    wgt = p.sbuf("wgt", [128, 8])
    bgt = p.sbuf("bgt", [1, 1])
    ev = p.sbuf("ev", [128, 512], BF16)
    gat = p.sbuf("gat", [1, TH])
    wst = [p.sbuf("wst0", [128, 8 * 128])]
    wbf = [p.sbuf("wbf0", [128, 8 * 128], BF16)]
    pmm = [p.psum(f"pmm{i}", [128, 512]) for i in range(2)]
    pg = p.psum("pg", [1, 512])
    ln_setup(p)

    def ld(dst, src):
        p.step("sync", lambda q, d=dst, s=src: q.dma_start(out=d, in_=s), dma=True)
    ld(hres[:].rearrange("p (c t) -> p c t", c=8),
       xin.rearrange("(c p) t -> p c t", p=128))
    ld(bq[:], bqkv)
    ld(wgt[:], wg)
    ld(bgt[:], bg)

    TWS = [(t0, 512) for t0 in range(0, TH, 512)]
    for (t0, w) in TWS:
        src_fn = lambda c, t0=t0: hres[:, c * TH + t0: c * TH + t0 + 512]
        dst_fn = lambda c: uf[:, c * 512:(c + 1) * 512]
        steps_ln(p, src_fn, dst_fn, 512)

        def castu(q, t0=t0):
            for c in range(8):
                last = q.tensor_copy(out=ubf[:, c * TH + t0: c * TH + t0 + 512],
                                     in_=uf[:, c * 512:(c + 1) * 512])
            return last
        p.step("vector", castu)
        def gmm(q):
            for c in range(8):
                last = q.matmul(out=pg[:, 0:512], lhsT=wgt[:, c:c + 1],
                                rhs=uf[:, c * 512:(c + 1) * 512],
                                start=(c == 0), stop=(c == 7))
            return last
        p.step("tensor", gmm)
        p.step("scalar", lambda q, t0=t0: q.activation(
            out=gat[0:1, t0:t0 + 512], in_=pg[0:1, 0:512], func=AF.Sigmoid,
            bias=bgt[0:1, 0:1]))
    p.step("sync", lambda q: q.dma_start(out=gate_o, in_=gat[:]), dma=True)

    def ev1(m, t0, w, pb):
        p.step("vector", lambda q, m=m, pb=pb: q.tensor_tensor(
            out=ev[:], in0=pb[:, 0:512],
            in1=bq[:, m:m + 1].broadcast_to([128, 512]), op=ALU.add))
        p.step("sync", lambda q, m=m, t0=t0: q.dma_start(
            out=qkv_o.rearrange("(c p) t -> p c t", p=128)[:, m, t0:t0 + 512],
            in_=ev[:]), dma=True)
    steps_stream_matmul(p, wqkv, 8, 24,
                        lambda k, t0, w: ubf[:, k * TH + t0: k * TH + t0 + 512],
                        ev1, TWS, wst, wbf, pmm, mgroup=1)
    return p.finalize()


def run_B0(inputs, X2):
    nc = _BUILD_CACHE.setdefault("B0", build_B0())
    base = {
        "wqkv": np.ascontiguousarray(np.asarray(inputs["L1_attn_Wqkv"], np.float32)),
        "bqkv": _chunkvec(inputs["L1_attn_bqkv"], 24),
        "wg": _chunkvec(np.asarray(inputs["L1_attn_Wg"], np.float32)[:, 0], 8),
        "bg": np.asarray(inputs["L1_attn_bg"], np.float32).reshape(1, 1),
    }
    in_maps = []
    for c in range(NCORES):
        b, s = c // 2, c % 2
        m = dict(base)
        m["xin"] = np.ascontiguousarray(X2[b][:, s * TH:(s + 1) * TH])
        in_maps.append(m)
    res = run_bass_kernel_spmd(nc, in_maps, core_ids=list(range(NCORES)))
    import ml_dtypes
    QKV = np.empty((B, 3 * DM, T), ml_dtypes.bfloat16)
    GATE = np.empty((B, 1, T), np.float32)
    for c in range(NCORES):
        b, s = c // 2, c % 2
        QKV[b, :, s * TH:(s + 1) * TH] = res.results[c]["qkv"]
        GATE[b, :, s * TH:(s + 1) * TH] = res.results[c]["gate"]
    return QKV, GATE



def build_B1():
    p = Prog("B1")
    NH_ = 8
    NQB = T // 128
    SCALE = 0.125
    QF = p.din("QF", [NH_, 64, T], BF16)
    KF = p.din("KF", [NH_, 64, T], BF16)
    VT = p.din("VT", [NH_, T, 64], BF16)
    GT = p.din("GT", [1, T])
    mA2 = p.din("mA2", [128, 2 * 128])
    mB2 = p.din("mB2", [128, 2 * 128])
    mFull = p.din("mFull", [128, 128])
    ident = p.din("ident", [128, 128])
    augr = p.din("augr", [2, 64], BF16)
    at_o = p.dout("at", [NH_, 64, T], BF16)

    qf = p.sbuf("qf", [64, T], BF16)
    kaug = p.sbuf("kaug", [65, T], BF16)
    qaug = p.sbuf("qaug", [66, T], BF16)
    kgat = p.sbuf("kgat", [66, 64], BF16)
    vt = p.sbuf("vt", [128, 32 * 64], BF16)
    att = p.sbuf("att", [64, T], BF16)
    grep = p.sbuf("grep", [64, T])
    gtr = p.sbuf("gtr", [1, T])
    sa = p.sbuf("sa", [128, 640])
    sb = p.sbuf("sb", [128, 128])
    pt = p.sbuf("pt", [128, 5 * 128], BF16)
    ptg = p.sbuf("ptg", [64, 128], BF16)
    mcol = p.sbuf("mcol", [128, 32])
    mgcol = p.sbuf("mgcol", [128, 32])
    nm32 = p.sbuf("nm32", [32, 128], BF16)
    nmg32 = p.sbuf("nmg32", [32, 128], BF16)
    m32f = p.sbuf("m32f", [32, 128])
    srec = p.sbuf("srec", [1, 128])
    sgrec = p.sbuf("sgrec", [1, 128])
    rl = p.sbuf("rl", [64, 128])
    rg = p.sbuf("rg", [64, 128])
    ol = p.sbuf("ol", [64, 128])
    og = p.sbuf("og", [64, 128])
    dd = p.sbuf("dd", [64, 128])
    mskA = p.sbuf("mskA", [128, 2 * 128])
    mskB = p.sbuf("mskB", [128, 2 * 128])
    mskF = p.sbuf("mskF", [128, 128])
    onesb = p.sbuf("onesb", [128, 1], BF16)
    identt = p.sbuf("identt", [128, 128])
    p.ones_row = p.sbuf("ones_row", [1, 128])
    psA = p.psum("psA", [128, 640])
    psS = p.psum("psS", [1, 128])
    psSg = p.psum("psSg", [1, 128])
    psO = p.psum("psO", [64, 128])
    psOg = p.psum("psOg", [64, 128])
    psR = p.psum("psR", [64, 128])
    pstr = p.psum("pstr", [32, 128])

    p.step("vector", lambda q: q.memset(onesb[:], 1.0))
    p.step("vector", lambda q: q.memset(p.ones_row[:], 1.0))
    p.step("vector", lambda q: q.memset(qaug[:], 0.0))
    p.step("vector", lambda q: q.memset(kgat[:], 0.0))

    def ld(dst, src):
        p.step("sync", lambda q, d=dst, s=src: q.dma_start(out=d, in_=s), dma=True)
    ld(mskA[:], mA2)
    ld(mskB[:], mB2)
    ld(mskF[:], mFull)
    ld(identt[:], ident)
    ld(gtr[:], GT)
    for tc in range(T // 512):
        p.step("tensor", lambda q, tc=tc: q.matmul(
            out=psA[0:64, 0:512], lhsT=p.ones_row[0:1, 0:64],
            rhs=gtr[0:1, tc * 512:(tc + 1) * 512], start=True, stop=True))
        p.step("vector", lambda q, tc=tc: q.tensor_copy(
            out=grep[:, tc * 512:(tc + 1) * 512], in_=psA[0:64, 0:512]))

    for h in range(NH_):
        ld(qf[:], QF[h])
        ld(kaug[0:64, :], KF[h])
        p.step("vector", lambda q: q.memset(kaug[64:65, :], 1.0))
        ld(vt[:].rearrange("p (c n) -> p c n", c=32),
           VT[h].rearrange("(c p) n -> p c n", p=128))
        ld(qaug[0:64, :], QF[h])
        p.step("vector", lambda q: q.tensor_copy(out=kgat[0:64, :], in_=kaug[0:64, 0:64]))
        p.step("sync", lambda q: q.dma_start(out=kgat[64:66, :], in_=augr), dma=True)

        for qb in range(NQB):
            kbs = [min(max(qb - 2 + j, 0), NQB - 1) for j in range(5)]

            def mmA(q, qb=qb, kbs=kbs):
                for j, kb in enumerate(kbs):
                    last = q.matmul(out=psA[:, j * 128:(j + 1) * 128],
                                    lhsT=qf[:, qb * 128:(qb + 1) * 128],
                                    rhs=kaug[0:64, kb * 128:(kb + 1) * 128],
                                    start=True, stop=True)
                return last
            p.step("tensor", mmA)

            def mskev(q, qb=qb, kbs=kbs):
                for j, kb in enumerate(kbs):
                    if kb != qb - 2 + j:
                        msk = mskF[:]
                    elif j == 0:
                        msk = mskA[:, 0:128]
                    elif j == 4:
                        msk = mskA[:, 128:256]
                    else:
                        msk = None
                    if msk is None:
                        last = q.tensor_copy(out=sa[:, j * 128:(j + 1) * 128],
                                             in_=psA[:, j * 128:(j + 1) * 128])
                    else:
                        last = q.tensor_tensor(out=sa[:, j * 128:(j + 1) * 128],
                                               in0=psA[:, j * 128:(j + 1) * 128],
                                               in1=msk, op=ALU.add)
                return last
            p.step("vector", mskev)
            p.step("vector", lambda q, qb=qb: q.reduce_max(
                out=mcol[:, qb:qb + 1], in_=sa[:], axis=AX.X))
            p.step("tensor", lambda q, qb=qb: q.matmul(
                out=psA[:, 0:64], lhsT=qf[:, qb * 128:(qb + 1) * 128],
                rhs=kgat[0:64, 0:64], start=True, stop=True))
            p.step("vector", lambda q, qb=qb: q.reduce_max(
                out=mgcol[:, qb:qb + 1], in_=psA[:, 0:64], axis=AX.X))

        p.step("tensor", lambda q: q.transpose(out=pstr[:], in_=mcol[:],
                                               identity=identt[:]))
        p.step("vector", lambda q: q.tensor_copy(out=m32f[:], in_=pstr[:]))
        p.step("vector", lambda q: q.tensor_scalar_mul(out=nm32[:], in0=m32f[:],
                                                       scalar1=-1.0))
        p.step("tensor", lambda q: q.transpose(out=pstr[:], in_=mgcol[:],
                                               identity=identt[:]))
        p.step("vector", lambda q: q.tensor_copy(out=m32f[:], in_=pstr[:]))
        p.step("vector", lambda q: q.tensor_scalar_mul(out=nmg32[:], in0=m32f[:],
                                                       scalar1=-1.0))
        p.step("sync", lambda q: q.dma_start(
            out=qaug[64:65, :].rearrange("p (b t) -> p b t", b=32),
            in_=nm32[:].unsqueeze(1)), dma=True)
        p.step("sync", lambda q: q.dma_start(
            out=qaug[65:66, :].rearrange("p (b t) -> p b t", b=32),
            in_=nmg32[:].unsqueeze(1)), dma=True)

        for qb in range(NQB):
            kbs = [min(max(qb - 2 + j, 0), NQB - 1) for j in range(5)]

            def mmB(q, qb=qb, kbs=kbs):
                for j, kb in enumerate(kbs):
                    last = q.matmul(out=psA[:, j * 128:(j + 1) * 128],
                                    lhsT=kaug[0:65, kb * 128:(kb + 1) * 128],
                                    rhs=qaug[0:65, qb * 128:(qb + 1) * 128],
                                    start=True, stop=True)
                return last
            p.step("tensor", mmB)

            for j, kb in enumerate(kbs):
                if kb != qb - 2 + j:
                    msk = mskF[:]
                elif j == 0:
                    msk = mskB[:, 0:128]
                elif j == 4:
                    msk = mskB[:, 128:256]
                else:
                    msk = None
                if msk is None:
                    p.step("scalar", lambda q, j=j: q.activation(
                        out=pt[:, j * 128:(j + 1) * 128],
                        in_=psA[:, j * 128:(j + 1) * 128], func=AF.Exp, scale=SCALE))
                else:
                    p.step("vector", lambda q, j=j, msk=msk: q.tensor_tensor(
                        out=sb[:], in0=psA[:, j * 128:(j + 1) * 128], in1=msk,
                        op=ALU.add))
                    p.step("scalar", lambda q, j=j: q.activation(
                        out=pt[:, j * 128:(j + 1) * 128], in_=sb[:], func=AF.Exp,
                        scale=SCALE))

            def sums(q):
                for j in range(5):
                    last = q.matmul(out=psS[:], lhsT=onesb[:, 0:1],
                                    rhs=pt[:, j * 128:(j + 1) * 128],
                                    start=(j == 0), stop=(j == 4))
                return last
            p.step("tensor", sums)
            p.step("vector", lambda q: q.reciprocal(out=srec[:], in_=psS[0:1, :]))

            def pv(q, kbs=kbs):
                for j, kb in enumerate(kbs):
                    last = q.matmul(out=psO[:], lhsT=vt[:, kb * 64:(kb + 1) * 64],
                                    rhs=pt[:, j * 128:(j + 1) * 128],
                                    start=(j == 0), stop=(j == 4))
                return last
            p.step("tensor", pv)

            def mmBg(q, qb=qb):
                return q.matmul(out=psA[:, 512:640][0:64, :],
                                lhsT=kgat[:, :], rhs=qaug[:, qb * 128:(qb + 1) * 128],
                                start=True, stop=True)
            p.step("tensor", mmBg)
            p.step("scalar", lambda q: q.activation(out=ptg[:], in_=psA[0:64, 512:640],
                                                    func=AF.Exp, scale=SCALE))
            p.step("tensor", lambda q: q.matmul(out=psSg[:], lhsT=onesb[0:64, 0:1],
                                                rhs=ptg[:], start=True, stop=True))
            p.step("vector", lambda q: q.reciprocal(out=sgrec[:], in_=psSg[0:1, :]))
            p.step("tensor", lambda q: q.matmul(out=psOg[:], lhsT=vt[0:64, 0:64],
                                                rhs=ptg[:], start=True, stop=True))

            p.step("tensor", lambda q: q.matmul(out=psR[:], lhsT=p.ones_row[0:1, 0:64],
                                                rhs=srec[:], start=True, stop=True))
            p.step("vector", lambda q: q.tensor_copy(out=rl[:], in_=psR[:]))
            p.step("tensor", lambda q: q.matmul(out=psR[:], lhsT=p.ones_row[0:1, 0:64],
                                                rhs=sgrec[:], start=True, stop=True))
            p.step("vector", lambda q: q.tensor_copy(out=rg[:], in_=psR[:]))
            p.step("vector", lambda q: q.tensor_mul(out=ol[:], in0=psO[:], in1=rl[:]))
            p.step("vector", lambda q: q.tensor_mul(out=og[:], in0=psOg[:], in1=rg[:]))
            p.step("vector", lambda q: q.tensor_sub(out=dd[:], in0=og[:], in1=ol[:]))
            p.step("vector", lambda q, qb=qb: q.tensor_mul(
                out=dd[:], in0=dd[:], in1=grep[:, qb * 128:(qb + 1) * 128]))
            p.step("vector", lambda q, qb=qb: q.tensor_tensor(
                out=att[:, qb * 128:(qb + 1) * 128], in0=ol[:], in1=dd[:], op=ALU.add))

        p.step("sync", lambda q, h=h: q.dma_start(out=at_o[h], in_=att[:]), dma=True)
    return p.finalize()


def run_B1(inputs, QKV, GATE):
    nc = _BUILD_CACHE.setdefault("B1", build_B1())
    import ml_dtypes
    i0, i1 = np.tril_indices(128, -1)
    mA = np.zeros((128, 256), np.float32)
    m_j0 = np.zeros((128, 128), np.float32)
    m_j0[i0, i1] = -1e30
    m_j4 = np.ascontiguousarray(m_j0.T)
    mA[:, 0:128] = m_j0
    mA[:, 128:256] = m_j4
    mB = np.zeros((128, 256), np.float32)
    mB[:, 0:128] = m_j0.T
    mB[:, 128:256] = m_j4.T
    base = {
        "mA2": mA, "mB2": mB,
        "mFull": np.full((128, 128), -1e30, np.float32),
        "ident": np.eye(128, dtype=np.float32),
        "augr": np.concatenate([np.zeros((1, 64)), np.ones((1, 64))]).astype(ml_dtypes.bfloat16),
    }
    in_maps = []
    for c in range(NCORES):
        b, hh = c // 2, c % 2
        heads = range(hh * 8, hh * 8 + 8)
        qf = np.stack([QKV[b, h * 64:(h + 1) * 64, :] for h in heads])
        kf = np.stack([QKV[b, DM + h * 64: DM + (h + 1) * 64, :] for h in heads])
        vtm = np.stack([np.ascontiguousarray(
            QKV[b, 2 * DM + h * 64: 2 * DM + (h + 1) * 64, :].T) for h in heads])
        m = dict(base)
        m["QF"] = np.ascontiguousarray(qf)
        m["KF"] = np.ascontiguousarray(kf)
        m["VT"] = vtm
        m["GT"] = np.ascontiguousarray(GATE[b])
        in_maps.append(m)
    res = run_bass_kernel_spmd(nc, in_maps, core_ids=list(range(NCORES)))
    import ml_dtypes
    ATT = np.empty((B, DM, T), ml_dtypes.bfloat16)
    for c in range(NCORES):
        b, hh = c // 2, c % 2
        for j, h in enumerate(range(hh * 8, hh * 8 + 8)):
            ATT[b, h * 64:(h + 1) * 64, :] = res.results[c]["at"][j]
    return ATT



def build_B2():
    p = Prog("B2")
    xin = p.din("xin", [DM, TH])
    at_i = p.din("at", [DM, TH], BF16)
    wout = p.din("wout", [DM, DM])
    bout = p.din("bout", [128, 8])
    xo = p.dout("xo", [DM, TH])

    hres = p.sbuf("hres_s", [128, 8 * TH])
    atb = p.sbuf("atb", [128, 8 * TH], BF16)
    bo = p.sbuf("bo", [128, 8])
    wst = [p.sbuf("wst0", [128, 8 * 128])]
    wbf = [p.sbuf("wbf0", [128, 8 * 128], BF16)]
    pmm = [p.psum(f"pmm{i}", [128, 512]) for i in range(2)]

    def ld(dst, src):
        p.step("sync", lambda q, d=dst, s=src: q.dma_start(out=d, in_=s), dma=True)
    ld(hres[:].rearrange("p (c t) -> p c t", c=8), xin.rearrange("(c p) t -> p c t", p=128))
    ld(atb[:].rearrange("p (c t) -> p c t", c=8), at_i.rearrange("(c p) t -> p c t", p=128))
    ld(bo[:], bout)

    def ev(m, t0, w, pb):
        p.step("vector", lambda q, m=m, t0=t0, pb=pb: q.scalar_tensor_tensor(
            out=hres[:, m * TH + t0: m * TH + t0 + 512], in0=pb[:, 0:512],
            scalar=bo[:, m:m + 1], in1=hres[:, m * TH + t0: m * TH + t0 + 512],
            op0=ALU.add, op1=ALU.add))
    steps_stream_matmul(p, wout, 8, 8,
                        lambda k, t0, w: atb[:, k * TH + t0: k * TH + t0 + 512],
                        ev, [(t0, 512) for t0 in range(0, TH, 512)],
                        wst, wbf, pmm, mgroup=1)
    p.step("sync", lambda q: q.dma_start(
        out=xo.rearrange("(c p) t -> p c t", p=128),
        in_=hres[:].rearrange("p (c t) -> p c t", c=8)), dma=True)
    return p.finalize()


def run_B2(inputs, X2, ATT):
    nc = _BUILD_CACHE.setdefault("B2", build_B2())
    base = {
        "wout": np.ascontiguousarray(np.asarray(inputs["L1_attn_Wout"], np.float32)),
        "bout": _chunkvec(inputs["L1_attn_bout"], 8),
    }
    import ml_dtypes
    in_maps = []
    for c in range(NCORES):
        b, s = c // 2, c % 2
        m = dict(base)
        m["xin"] = np.ascontiguousarray(X2[b][:, s * TH:(s + 1) * TH])
        m["at"] = np.ascontiguousarray(np.asarray(ATT[b][:, s * TH:(s + 1) * TH],
                                                  ml_dtypes.bfloat16))
        in_maps.append(m)
    res = run_bass_kernel_spmd(nc, in_maps, core_ids=list(range(NCORES)))
    XO = np.empty((B, DM, T), np.float32)
    for c in range(NCORES):
        b, s = c // 2, c % 2
        XO[b, :, s * TH:(s + 1) * TH] = res.results[c]["xo"]
    return XO



def build_C0():
    p = Prog("C0")
    xin = p.din("xin", [DM, TH])
    w_gt = p.din("w_gt", [DM, 16])
    gbrep = p.din("gbrep", [128, 16])
    w13 = p.din("w13", [16, DM, DM])
    w2e = p.din("w2e", [16, 512, DM])
    w13s = p.din("w13s", [DM, DM])
    w2s = p.din("w2s", [512, DM])
    sel16 = p.din("sel16", [16, 16 * 128])
    ident = p.din("ident", [128, 128])
    xo = p.dout("xo", [DM, TH])

    hres = p.sbuf("hres_s", [128, 8 * TH])
    ubf = p.sbuf("ubf", [128, 8 * TH], BF16)
    uf = p.sbuf("uf", [128, 8 * 256])
    gsb = p.sbuf("gsb", [128, 4 * TH], BF16)
    h1cb = p.sbuf("h1cb", [128, 4 * TH], BF16)
    crep = p.sbuf("crep", [128, TH])
    scfm = p.sbuf("scfm", [16, TH])
    cfm = p.sbuf("cfm", [16, TH])
    selt = p.sbuf("selt", [16, 16 * 128])
    gbt = p.sbuf("gbt", [128, 16])
    identt = p.sbuf("identt", [128, 128])
    stm = p.sbuf("stm", [128, 16])
    btm = p.sbuf("btm", [128, 16])
    gtc = p.sbuf("gtc", [128, 256])
    cnt = p.sbuf("cnt", [128, 16])
    ind = p.sbuf("ind", [128, 16])
    wsum = p.sbuf("wsum", [128, 1])
    wrec = p.sbuf("wrec", [128, 1])
    ctm = p.sbuf("ctm", [128, 16])
    tmp5 = p.sbuf("tmp5", [128, 512])
    wst = [p.sbuf("wst0", [128, 8 * 128])]
    wbf = [p.sbuf("wbf0", [128, 8 * 128], BF16)]
    pmm = [p.psum(f"pmm{i}", [128, 512]) for i in range(2)]
    prt = p.psum("prt", [16, 512])
    prc = p.psum("prc", [128, 512])
    ptr = prc
    ln_setup(p, maxw=256)

    def ld(dst, src):
        p.step("sync", lambda q, d=dst, s=src: q.dma_start(out=d, in_=s), dma=True)
    ld(hres[:].rearrange("p (c t) -> p c t", c=8), xin.rearrange("(c p) t -> p c t", p=128))
    ld(selt[:], sel16)
    ld(gbt[:], gbrep)
    ld(identt[:], ident)
    wgt = p.sbuf("wgt", [128, 8 * 16])
    ld(wgt[:].rearrange("p (c n) -> p c n", c=8), w_gt.rearrange("(c p) n -> p c n", p=128))

    for (t0, w) in [(t0, 256) for t0 in range(0, TH, 256)]:
        src_fn = lambda c, t0=t0: hres[:, c * TH + t0: c * TH + t0 + 256]
        dst_fn = lambda c: uf[:, c * 256:(c + 1) * 256]
        steps_ln(p, src_fn, dst_fn, 256)

        def castu(q, t0=t0):
            for c in range(8):
                last = q.tensor_copy(out=ubf[:, c * TH + t0: c * TH + t0 + 256],
                                     in_=uf[:, c * 256:(c + 1) * 256])
            return last
        p.step("vector", castu)

        def gmm(q):
            for c in range(8):
                last = q.matmul(out=prt[:, 0:256], lhsT=wgt[:, c * 16:(c + 1) * 16],
                                rhs=uf[:, c * 256:(c + 1) * 256],
                                start=(c == 0), stop=(c == 7))
            return last
        p.step("tensor", gmm)
        p.step("scalar", lambda q, t0=t0: q.activation(
            out=scfm[:, t0:t0 + 256], in_=prt[0:16, 0:256], func=AF.Sigmoid))

    for ti in range(TH // 128):
        t0 = ti * 128
        p.step("tensor", lambda q, t0=t0: q.transpose(
            out=ptr[:, 0:16], in_=scfm[:, t0:t0 + 128], identity=identt[0:16, 0:16]))
        p.step("vector", lambda q: q.tensor_copy(out=stm[:], in_=ptr[0:128, 0:16]))
        p.step("vector", lambda q: q.tensor_add(out=btm[:], in0=stm[:], in1=gbt[:]))
        p.step("vector", lambda q: q.tensor_tensor(
            out=gtc[:].rearrange("p (a b) -> p a b", a=16),
            in0=btm[:].unsqueeze(1).broadcast_to([128, 16, 16]),
            in1=btm[:].unsqueeze(2).broadcast_to([128, 16, 16]), op=ALU.is_gt))
        p.step("vector", lambda q: q.tensor_reduce(
            out=cnt[:], in_=gtc[:].rearrange("p (a b) -> p a b", a=16), axis=AX.X,
            op=ALU.add))
        p.step("vector", lambda q: q.tensor_single_scalar(
            out=ind[:], in_=cnt[:], scalar=3.5, op=ALU.is_lt))
        p.step("vector", lambda q: q.tensor_mul(out=ind[:], in0=ind[:], in1=stm[:]))
        p.step("vector", lambda q: q.tensor_reduce(out=wsum[:], in_=ind[:],
                                                   axis=AX.X, op=ALU.add))
        p.step("vector", lambda q: q.reciprocal(out=wrec[:], in_=wsum[:]))
        p.step("scalar", lambda q: q.activation(out=ctm[:], in_=ind[:], func=AF.Copy,
                                                scale=wrec[:]))
        p.step("vector", lambda q: q.tensor_scalar_mul(out=ctm[:], in0=ctm[:],
                                                       scalar1=2.5))
        p.step("tensor", lambda q: q.transpose(out=ptr[0:16, 0:128], in_=ctm[:],
                                               identity=identt[:]))
        p.step("vector", lambda q, t0=t0: q.tensor_copy(out=cfm[:, t0:t0 + 128],
                                                        in_=ptr[0:16, 0:128]))

    TWS = [(t0, 512) for t0 in range(0, TH, 512)]
    for e in range(17):
        shared = (e == 16)
        if not shared:
            for tc in range(TH // 512):
                p.step("tensor", lambda q, e=e, tc=tc: q.matmul(
                    out=prc[:], lhsT=selt[:, e * 128:(e + 1) * 128],
                    rhs=cfm[:, tc * 512:(tc + 1) * 512], start=True, stop=True))
                p.step("vector", lambda q, tc=tc: q.tensor_copy(
                    out=crep[:, tc * 512:(tc + 1) * 512], in_=prc[:]))

        wsrc = w13s if shared else w13[e]

        def ev1(m, t0, w, pb, shared=shared):
            if m < 4:
                p.step("scalar", lambda q, m=m, t0=t0, pb=pb: q.activation(
                    out=gsb[:, m * TH + t0: m * TH + t0 + 512], in_=pb[:, 0:512],
                    func=AF.Silu))
            else:
                if shared:
                    p.step("vector", lambda q, m=m, t0=t0, pb=pb: q.tensor_tensor(
                        out=h1cb[:, (m - 4) * TH + t0: (m - 4) * TH + t0 + 512],
                        in0=gsb[:, (m - 4) * TH + t0: (m - 4) * TH + t0 + 512],
                        in1=pb[:, 0:512], op=ALU.mult))
                else:
                    p.step("vector", lambda q, m=m, t0=t0, pb=pb: q.tensor_tensor(
                        out=tmp5[:], in0=gsb[:, (m - 4) * TH + t0: (m - 4) * TH + t0 + 512],
                        in1=pb[:, 0:512], op=ALU.mult))
                    p.step("vector", lambda q, m=m, t0=t0: q.tensor_tensor(
                        out=h1cb[:, (m - 4) * TH + t0: (m - 4) * TH + t0 + 512],
                        in0=tmp5[:], in1=crep[:, t0:t0 + 512], op=ALU.mult))
        steps_stream_matmul(p, wsrc, 8, 8,
                            lambda k, t0, w: ubf[:, k * TH + t0: k * TH + t0 + 512],
                            ev1, TWS, wst, wbf, pmm, mgroup=1)

        wsrc2 = w2s if shared else w2e[e]

        def ev2(m, t0, w, pb):
            p.step("vector", lambda q, m=m, t0=t0, pb=pb: q.tensor_tensor(
                out=hres[:, m * TH + t0: m * TH + t0 + 512],
                in0=hres[:, m * TH + t0: m * TH + t0 + 512], in1=pb[:, 0:512],
                op=ALU.add))
        steps_stream_matmul(p, wsrc2, 4, 8,
                            lambda k, t0, w: h1cb[:, k * TH + t0: k * TH + t0 + 512],
                            ev2, TWS, wst, wbf, pmm, mgroup=1)

    p.step("sync", lambda q: q.dma_start(
        out=xo.rearrange("(c p) t -> p c t", p=128),
        in_=hres[:].rearrange("p (c t) -> p c t", c=8)), dma=True)
    return p.finalize()


def run_C0(inputs, X3):
    nc = _BUILD_CACHE.setdefault("C0", build_C0())
    w13 = np.asarray(inputs["L1_ffn_w13"], np.float32)
    w2 = np.asarray(inputs["L1_ffn_w2"], np.float32)
    sel16 = np.zeros((16, 16 * 128), np.float32)
    for e in range(16):
        sel16[e, e * 128:(e + 1) * 128] = 1.0
    base = {
        "w_gt": np.ascontiguousarray(np.asarray(inputs["L1_ffn_gate_w"], np.float32).T),
        "gbrep": np.ascontiguousarray(np.broadcast_to(
            np.asarray(inputs["L1_ffn_gate_b"], np.float32), (128, 16))),
        "w13": np.ascontiguousarray(w13.transpose(0, 2, 1)),
        "w2e": np.ascontiguousarray(w2.transpose(0, 2, 1)),
        "w13s": np.ascontiguousarray(np.asarray(inputs["L1_ffn_w13_s"], np.float32).T),
        "w2s": np.ascontiguousarray(np.asarray(inputs["L1_ffn_w2_s"], np.float32).T),
        "sel16": sel16,
        "ident": np.eye(128, dtype=np.float32),
    }
    in_maps = []
    for c in range(NCORES):
        b, s = c // 2, c % 2
        m = dict(base)
        m["xin"] = np.ascontiguousarray(X3[b][:, s * TH:(s + 1) * TH])
        in_maps.append(m)
    res = run_bass_kernel_spmd(nc, in_maps, core_ids=list(range(NCORES)))
    XO = np.empty((B, DM, T), np.float32)
    for c in range(NCORES):
        b, s = c // 2, c % 2
        XO[b, :, s * TH:(s + 1) * TH] = res.results[c]["xo"]
    return XO



def build_D0a():
    p = Prog("D0a")
    TE2 = TH + 256
    NTI = TE2 // 128
    xin = p.din("xin", [DM, TE2])
    w_ret = p.din("w_ret", [DM, DM])
    brep = p.din("brep", [128, DM])
    ltb = p.din("ltb", [128, 128], BF16)
    dcol = p.din("dcol", [128, 1], BF16)
    hmask = p.din("hmask", [128, 2])
    ret_o = p.dout("ret", [DM, TH])

    uf = p.sbuf("uf", [128, 8 * 512])
    ubf = p.sbuf("ubf", [128, 8 * TE2], BF16)
    sigf = p.sbuf("sigf", [128, 1024])
    sigb = p.sbuf("sigb", [128, 1024], BF16)
    stater = p.sbuf("stater", [1, 1024])
    bret = p.sbuf("bret", [128, DM])
    ltbt = p.sbuf("ltbt", [128, 128], BF16)
    dcolt = p.sbuf("dcolt", [128, 1], BF16)
    hmt = p.sbuf("hmt", [128, 2])
    retf = p.sbuf("retf", [128, 8 * 128])
    wst = [p.sbuf("wst0", [128, 8 * 128])]
    wbf = [p.sbuf("wbf0", [128, 8 * 128], BF16)]
    psg = [p.psum(f"psg{i}", [128, 512]) for i in range(2)]
    psr = p.psum("psr", [128, 128])
    ln_setup(p)

    def ld(dst, src):
        p.step("sync", lambda q, d=dst, s=src: q.dma_start(out=d, in_=s), dma=True)
    ld(bret[:], brep)
    ld(ltbt[:], ltb)
    ld(dcolt[:], dcol)
    ld(hmt[:], hmask)
    p.step("vector", lambda q: q.memset(stater[:], 0.0))

    for t0 in range(0, TE2, 512):
        w = min(512, TE2 - t0)
        src_fn = lambda c, t0=t0, w=w: None
        p.step("sync", lambda q, t0=t0, w=w: q.dma_start(
            out=uf[:].rearrange("p (c t) -> p c t", c=8)[:, :, 0:w],
            in_=xin.rearrange("(c p) t -> p c t", p=128)[:, :, t0:t0 + w]), dma=True)
        srcf = lambda c, w=w: uf[:, c * 512: c * 512 + w]
        dstf = lambda c, t0=t0, w=w: ubf[:, c * TE2 + t0: c * TE2 + t0 + w]
        steps_ln(p, srcf, dstf, w)

    wrb = p.sbuf("wrb", [128, 8 * DM], BF16)
    for c8 in range(8):
        p.step("sync", lambda q, c8=c8: q.dma_start(
            out=wst[0][:].rearrange("p (a n) -> p a n", a=1),
            in_=w_ret.rearrange("(c p) n -> p c n", p=128)[:, c8:c8 + 1, :]), dma=True)
        p.step("scalar", lambda q, c8=c8: q.copy(
            out=wrb[:, c8 * DM:(c8 + 1) * DM], in_=wst[0][:, 0:DM]))

    for ti in range(NTI):
        t0 = ti * 128
        for half in range(2):
            def mmsig(q, t0=t0, half=half):
                for c in range(8):
                    last = q.matmul(
                        out=psg[half][:, 0:512],
                        lhsT=ubf[:, c * TE2 + t0: c * TE2 + t0 + 128],
                        rhs=wrb[:, c * DM + half * 512: c * DM + half * 512 + 512],
                        start=(c == 0), stop=(c == 7))
                return last
            p.step("tensor", mmsig)
            p.step("vector", lambda q, half=half: q.tensor_tensor(
                out=sigf[:, half * 512:(half + 1) * 512], in0=psg[half][:, 0:512],
                in1=bret[:, half * 512:(half + 1) * 512], op=ALU.add))
        p.step("scalar", lambda q: q.activation(out=sigf[:], in_=sigf[:],
                                                func=AF.Sigmoid))
        if ti < 2:
            p.step("vector", lambda q, ti=ti: q.tensor_scalar(
                out=sigf[:], in0=sigf[:], scalar1=hmt[:, ti:ti + 1], scalar2=None,
                op0=ALU.mult))
        p.step("vector", lambda q: q.tensor_tensor(
            out=sigf[0:1, :], in0=sigf[0:1, :], in1=stater[:], op=ALU.add))
        p.step("vector", lambda q: q.tensor_copy(out=sigb[:], in_=sigf[:]))
        for fc in range(8):
            p.step("tensor", lambda q, fc=fc: q.matmul(
                out=psr[:], lhsT=sigb[:, fc * 128:(fc + 1) * 128], rhs=ltbt[:],
                start=True, stop=True))
            p.step("vector", lambda q, fc=fc: q.tensor_copy(
                out=retf[:, fc * 128:(fc + 1) * 128], in_=psr[:]))
        def mmst(q):
            q.matmul(out=psg[0][0:1, 0:512], lhsT=dcolt[:], rhs=sigb[:, 0:512],
                     start=True, stop=True)
            return q.matmul(out=psg[1][0:1, 0:512], lhsT=dcolt[:],
                            rhs=sigb[:, 512:1024], start=True, stop=True)
        p.step("tensor", mmst)
        p.step("vector", lambda q: q.tensor_copy(out=stater[0:1, 0:512],
                                                 in_=psg[0][0:1, 0:512]))
        p.step("vector", lambda q: q.tensor_copy(out=stater[0:1, 512:1024],
                                                 in_=psg[1][0:1, 0:512]))
        if ti >= 2:
            p.step("sync", lambda q, ti=ti: q.dma_start(
                out=ret_o.rearrange("(c p) t -> p c t", p=128)
                [:, :, (ti - 2) * 128:(ti - 1) * 128],
                in_=retf[:].rearrange("p (c t) -> p c t", c=8)), dma=True)
    return p.finalize()


def run_D0a(inputs, X4):
    nc = _BUILD_CACHE.setdefault("D0a", build_D0a())
    lt = np.zeros((128, 128), np.float32)
    for s_ in range(128):
        for t_ in range(s_, 128):
            lt[s_, t_] = 0.5 ** (t_ - s_ + 1)
    import ml_dtypes
    dcol = (0.5 ** (128 - np.arange(128, dtype=np.float32))).reshape(128, 1)
    base = {
        "w_ret": np.ascontiguousarray(np.asarray(inputs["L2_attn_W_ret"], np.float32)),
        "brep": np.ascontiguousarray(np.broadcast_to(
            np.asarray(inputs["L2_attn_b_ret"], np.float32), (128, DM))),
        "ltb": lt.astype(ml_dtypes.bfloat16),
        "dcol": dcol.astype(ml_dtypes.bfloat16),
    }
    in_maps = []
    for c in range(NCORES):
        b, s = c // 2, c % 2
        m = dict(base)
        if s == 0:
            seg = np.concatenate([np.zeros((DM, 256), np.float32),
                                  X4[b][:, 0:TH]], axis=1)
            m["hmask"] = np.zeros((128, 2), np.float32)
        else:
            seg = X4[b][:, TH - 256: T]
            m["hmask"] = np.ones((128, 2), np.float32)
        m["xin"] = np.ascontiguousarray(seg)
        in_maps.append(m)
    res = run_bass_kernel_spmd(nc, in_maps, core_ids=list(range(NCORES)))
    RET = np.empty((B, DM, T), np.float32)
    for c in range(NCORES):
        b, s = c // 2, c % 2
        RET[b, :, s * TH:(s + 1) * TH] = res.results[c]["ret"]
    return RET



def build_D0b():
    p = Prog("D0b")
    xin = p.din("xin", [DM, TH])
    ret_i = p.din("ret", [DM, TH])
    l1g = p.din("l1g", [128, 8])
    l1b = p.din("l1b", [128, 8])
    w1 = p.din("w1", [DM, 4 * DM])
    b1c = p.din("b1c", [128, 32])
    w2 = p.din("w2", [4 * DM, DM])
    b2c = p.din("b2c", [128, 8])
    l2g = p.din("l2g", [128, 8])
    l2b = p.din("l2b", [128, 8])
    xo = p.dout("xo", [DM, TH])

    xt4 = p.sbuf("xt4", [128, 8 * 512])
    h2f = p.sbuf("h2f", [128, 8 * TH])
    h2b = p.sbuf("h2b", [128, 8 * 512], BF16)
    uf = p.sbuf("uf", [128, 8 * 512])
    h1cb = p.sbuf("h1cb", [128, 32 * 512], BF16)
    gx = p.sbuf("gx", [128, 512])
    gt = p.sbuf("gt", [128, 512])
    lg1 = p.sbuf("lg1", [128, 8])
    lb1 = p.sbuf("lb1", [128, 8])
    lg2 = p.sbuf("lg2", [128, 8])
    lb2 = p.sbuf("lb2", [128, 8])
    b1t = p.sbuf("b1t", [128, 32])
    b2t = p.sbuf("b2t", [128, 8])
    wst = [p.sbuf("wst0", [128, 32 * 128])]
    wbf = [p.sbuf("wbf0", [128, 32 * 128], BF16)]
    pmm = [p.psum(f"pmm{i}", [128, 512]) for i in range(2)]
    ln_setup(p)

    def ld(dst, src):
        p.step("sync", lambda q, d=dst, s=src: q.dma_start(out=d, in_=s), dma=True)
    ld(lg1[:], l1g)
    ld(lb1[:], l1b)
    ld(lg2[:], l2g)
    ld(lb2[:], l2b)
    ld(b1t[:], b1c)
    ld(b2t[:], b2c)

    for t0 in range(0, TH, 512):
        p.step("sync", lambda q, t0=t0: q.dma_start(
            out=xt4[:].rearrange("p (c t) -> p c t", c=8),
            in_=xin.rearrange("(c p) t -> p c t", p=128)[:, :, t0:t0 + 512]), dma=True)
        srcf = lambda c: xt4[:, c * 512:(c + 1) * 512]
        dstf = lambda c: uf[:, c * 512:(c + 1) * 512]
        steps_ln(p, srcf, dstf, 512)
        def addret(q, t0=t0):
            return q.dma_start(
                out=p.ln_x2[:].rearrange("p (c t) -> p c t", c=8),
                in_=ret_i.rearrange("(c p) t -> p c t", p=128)[:, :, t0:t0 + 512])
        p.step("sync", addret, dma=True)

        def vsum(q, t0=t0):
            for c in range(8):
                last = q.tensor_tensor(out=uf[:, c * 512:(c + 1) * 512],
                                       in0=uf[:, c * 512:(c + 1) * 512],
                                       in1=p.ln_x2[:, c * 512:(c + 1) * 512],
                                       op=ALU.add)
            return last
        p.step("vector", vsum)
        srcf2 = lambda c: uf[:, c * 512:(c + 1) * 512]
        dstf2 = lambda c, t0=t0: h2f[:, c * TH + t0: c * TH + t0 + 512]
        steps_ln(p, srcf2, dstf2, 512, gain=lg1, bias=lb1)

    for qtr in range(4):
        t0 = qtr * 512

        def casth(q, t0=t0):
            for c in range(8):
                last = q.tensor_copy(out=h2b[:, c * 512:(c + 1) * 512],
                                     in_=h2f[:, c * TH + t0: c * TH + t0 + 512])
            return last
        p.step("vector", casth)

        def ev1(m, tt0, w, pb):
            p.step("vector", lambda q, m=m, pb=pb: q.tensor_tensor(
                out=gx[:], in0=pb[:, 0:512],
                in1=b1t[:, m:m + 1].broadcast_to([128, 512]), op=ALU.add))
            p.step("scalar", lambda q: q.activation(out=gt[:], in_=gx[:],
                                                    func=AF.Square, scale=SQ_GELU))
            p.step("vector", lambda q: q.scalar_tensor_tensor(
                out=gt[:], in0=gt[:], scalar=1.0, in1=gx[:], op0=ALU.add, op1=ALU.mult))
            p.step("scalar", lambda q: q.activation(out=gt[:], in_=gt[:],
                                                    func=AF.Sigmoid, scale=2 * C_GELU))
            p.step("vector", lambda q, m=m: q.tensor_mul(
                out=h1cb[:, m * 512:(m + 1) * 512], in0=gt[:], in1=gx[:]))
        steps_stream_matmul(p, w1, 8, 32,
                            lambda k, tt0, w: h2b[:, k * 512: k * 512 + 512],
                            ev1, [(0, 512)], wst, wbf, pmm, mgroup=1)

        def ev2(m, tt0, w, pb, t0=t0):
            p.step("vector", lambda q, m=m, t0=t0, pb=pb: q.scalar_tensor_tensor(
                out=h2f[:, m * TH + t0: m * TH + t0 + 512], in0=pb[:, 0:512],
                scalar=b2t[:, m:m + 1], in1=h2f[:, m * TH + t0: m * TH + t0 + 512],
                op0=ALU.add, op1=ALU.add))
        steps_stream_matmul(p, w2, 32, 8,
                            lambda k, tt0, w: h1cb[:, k * 512: k * 512 + 512],
                            ev2, [(0, 512)], wst, wbf, pmm, mgroup=1)

    for t0 in range(0, TH, 512):
        srcf = lambda c, t0=t0: h2f[:, c * TH + t0: c * TH + t0 + 512]
        dstf = lambda c: uf[:, c * 512:(c + 1) * 512]
        steps_ln(p, srcf, dstf, 512, gain=lg2, bias=lb2)
        p.step("sync", lambda q, t0=t0: q.dma_start(
            out=xt4[:].rearrange("p (c t) -> p c t", c=8),
            in_=xin.rearrange("(c p) t -> p c t", p=128)[:, :, t0:t0 + 512]), dma=True)

        def fin(q, t0=t0):
            for c in range(8):
                last = q.tensor_tensor(out=xt4[:, c * 512:(c + 1) * 512],
                                       in0=xt4[:, c * 512:(c + 1) * 512],
                                       in1=uf[:, c * 512:(c + 1) * 512], op=ALU.add)
            return last
        p.step("vector", fin)
        p.step("sync", lambda q, t0=t0: q.dma_start(
            out=xo.rearrange("(c p) t -> p c t", p=128)[:, :, t0:t0 + 512],
            in_=xt4[:].rearrange("p (c t) -> p c t", c=8)), dma=True)
    return p.finalize()


def run_D0b(inputs, X4, RET):
    nc = _BUILD_CACHE.setdefault("D0b", build_D0b())
    base = {
        "l1g": _chunkvec(inputs["L2_attn_ln1_scale"], 8),
        "l1b": _chunkvec(inputs["L2_attn_ln1_bias"], 8),
        "w1": np.ascontiguousarray(np.asarray(inputs["L2_attn_W1"], np.float32)),
        "b1c": _chunkvec(inputs["L2_attn_b1"], 32),
        "w2": np.ascontiguousarray(np.asarray(inputs["L2_attn_W2"], np.float32)),
        "b2c": _chunkvec(inputs["L2_attn_b2"], 8),
        "l2g": _chunkvec(inputs["L2_attn_ln2_scale"], 8),
        "l2b": _chunkvec(inputs["L2_attn_ln2_bias"], 8),
    }
    in_maps = []
    for c in range(NCORES):
        b, s = c // 2, c % 2
        m = dict(base)
        m["xin"] = np.ascontiguousarray(X4[b][:, s * TH:(s + 1) * TH])
        m["ret"] = np.ascontiguousarray(RET[b][:, s * TH:(s + 1) * TH])
        in_maps.append(m)
    res = run_bass_kernel_spmd(nc, in_maps, core_ids=list(range(NCORES)))
    XO = np.empty((B, DM, T), np.float32)
    for c in range(NCORES):
        b, s = c // 2, c % 2
        XO[b, :, s * TH:(s + 1) * TH] = res.results[c]["xo"]
    return XO



def build_E0():
    p = Prog("E0")
    xin = p.din("xin", [DM, TH])
    wsim = p.din("wsim", [DM, 16])
    gates = p.din("gates", [1, 16])
    wq_all = p.din("wq_all", [DM, DM])
    wk_all = p.din("wk_all", [DM, DM])
    wv_all = p.din("wv_all", [DM, DM])
    pat = p.din("pat", [16, 8 * 128])
    fpat = p.din("fpat", [128, 64], BF16)
    ident = p.din("ident", [128, 128])
    qm_o = p.dout("qm", [64, TH], BF16)
    km_o = p.dout("km", [64, TH], BF16)
    vm_o = p.dout("vm", [64, TH], BF16)
    rw_o = p.dout("rw", [16, TH])

    vbf = p.sbuf("vbf", [128, 8 * TH], BF16)
    uf = p.sbuf("uf", [128, 8 * 512])
    acc = p.sbuf("acc", [64, TH])
    accb = p.sbuf("accb", [64, TH], BF16)
    rwr = p.sbuf("rwr", [128, 512])
    lgts = p.sbuf("lgts", [16, TH])
    rwf = p.sbuf("rwf", [16, TH])
    wsimt = p.sbuf("wsimt", [128, 8 * 16])
    wsq = p.sbuf("wsq", [128, 8 * 16])
    patt = p.sbuf("patt", [16, 8 * 128])
    identt = p.sbuf("identt", [128, 128])
    ssrow = p.sbuf("ssrow", [1, 512])
    rn = p.sbuf("rn", [1, 512])
    rsimr = p.sbuf("rsimr", [1, 16])
    rsimc = p.sbuf("rsimc", [16, 1])
    sgr = p.sbuf("sgr", [1, 16])
    sgc = p.sbuf("sgc", [16, 1])
    lgf = p.sbuf("lgf", [16, 512])
    lgtm = p.sbuf("lgtm", [128, 16])
    pos = p.sbuf("pos", [128, 16])
    gated = p.sbuf("gated", [128, 16])
    expg = p.sbuf("expg", [128, 16])
    num = p.sbuf("num", [128, 16])
    den = p.sbuf("den", [128, 1])
    inact = p.sbuf("inact", [128, 1])
    rac = p.sbuf("rac", [128, 1])
    rwact = p.sbuf("rwact", [128, 16])
    gtc = p.sbuf("gtc", [128, 256])
    cnt = p.sbuf("cnt", [128, 16])
    ind2 = p.sbuf("ind2", [128, 16])
    rwtm = p.sbuf("rwtm", [128, 16])
    tmp6 = p.sbuf("tmp6", [128, 512], BF16)
    fpt = p.sbuf("fpt", [128, 64], BF16)
    wst = [p.sbuf("wst0", [128, 8 * 128])]
    wbf = [p.sbuf("wbf0", [128, 8 * 128], BF16)]
    pmm0 = p.psum("pmm0", [128, 512])
    pmm = [pmm0, pmm0]
    prt = p.psum("prt", [16, 512])
    prw = p.psum("prw", [128, 512])
    psX = [p.psum(f"psX{i}", [128, 512]) for i in range(4)]
    ln_setup(p, share=(psX[0][0:1, 0:512], psX[1][0:1, 0:512], psX[2], psX[3]))

    def ld(dst, src):
        p.step("sync", lambda q, d=dst, s=src: q.dma_start(out=d, in_=s), dma=True)
    ld(wsimt[:].rearrange("p (c n) -> p c n", c=8),
       wsim.rearrange("(c p) n -> p c n", p=128))
    ld(patt[:], pat)
    ld(fpt[:], fpat)
    ld(identt[:], ident)
    ld(sgr[:], gates)
    p.step("scalar", lambda q: q.activation(out=sgr[:], in_=sgr[:], func=AF.Sigmoid))
    p.step("tensor", lambda q: q.transpose(out=prt[0:16, 0:1], in_=sgr[:],
                                           identity=identt[0:1, 0:1]))
    p.step("vector", lambda q: q.tensor_copy(out=sgc[:], in_=prt[0:16, 0:1]))
    p.step("scalar", lambda q: q.activation(out=wsq[:], in_=wsimt[:], func=AF.Square))
    def simn(q):
        for c in range(8):
            last = q.matmul(out=prt[0:1, 0:16], lhsT=p.ones_col[:, 0:1],
                            rhs=wsq[:, c * 16:(c + 1) * 16],
                            start=(c == 0), stop=(c == 7))
        return last
    p.step("tensor", simn)
    p.step("scalar", lambda q: q.activation(out=rsimr[:], in_=prt[0:1, 0:16], func=AF.Ln))
    p.step("scalar", lambda q: q.activation(out=rsimr[:], in_=rsimr[:], func=AF.Exp,
                                            scale=0.5))
    p.step("vector", lambda q: q.tensor_scalar_max(out=rsimr[:], in0=rsimr[:],
                                                   scalar1=1e-12))
    p.step("vector", lambda q: q.reciprocal(out=rsimr[:], in_=rsimr[:]))
    p.step("tensor", lambda q: q.transpose(out=prt[0:16, 0:1], in_=rsimr[:],
                                           identity=identt[0:1, 0:1]))
    p.step("vector", lambda q: q.tensor_copy(out=rsimc[:], in_=prt[0:16, 0:1]))

    uf2 = p.sbuf("uf2", [128, 8 * 512])
    for t0 in range(0, TH, 512):
        p.step("sync", lambda q, t0=t0: q.dma_start(
            out=uf[:].rearrange("p (c t) -> p c t", c=8),
            in_=xin.rearrange("(c p) t -> p c t", p=128)[:, :, t0:t0 + 512]), dma=True)
        srcf = lambda c: uf[:, c * 512:(c + 1) * 512]
        dstf = lambda c: uf2[:, c * 512:(c + 1) * 512]
        steps_ln(p, srcf, dstf, 512)

        def castv(q, t0=t0):
            for c in range(8):
                last = q.tensor_copy(out=vbf[:, c * TH + t0: c * TH + t0 + 512],
                                     in_=uf2[:, c * 512:(c + 1) * 512])
            return last
        p.step("vector", castv)
        def sqv(q):
            for c in range(8):
                last = q.activation(out=uf[:, c * 512:(c + 1) * 512],
                                    in_=uf2[:, c * 512:(c + 1) * 512], func=AF.Square)
            return last
        p.step("scalar", sqv)

        def ssum(q):
            for c in range(8):
                last = q.matmul(out=prw[0:1, 0:512], lhsT=p.ones_col[:, 0:1],
                                rhs=uf[:, c * 512:(c + 1) * 512],
                                start=(c == 0), stop=(c == 7))
            return last
        p.step("tensor", ssum)
        p.step("vector", lambda q: q.tensor_copy(out=ssrow[:], in_=prw[0:1, 0:512]))
        p.step("scalar", lambda q: q.activation(out=rn[:], in_=ssrow[:], func=AF.Ln))
        p.step("scalar", lambda q: q.activation(out=rn[:], in_=rn[:], func=AF.Exp,
                                                scale=0.5))
        p.step("vector", lambda q: q.tensor_scalar_max(out=rn[:], in0=rn[:],
                                                       scalar1=1e-12))
        p.step("vector", lambda q: q.reciprocal(out=rn[:], in_=rn[:]))
        def lmm(q):
            for c in range(8):
                last = q.matmul(out=prt[:, 0:512], lhsT=wsimt[:, c * 16:(c + 1) * 16],
                                rhs=uf2[:, c * 512:(c + 1) * 512],
                                start=(c == 0), stop=(c == 7))
            return last
        p.step("tensor", lmm)
        p.step("vector", lambda q: q.tensor_copy(out=lgf[:], in_=prt[0:16, 0:512]))
        p.step("tensor", lambda q: q.matmul(out=prw[0:16, 0:512],
                                            lhsT=p.ones_row[0:1, 0:16], rhs=rn[:],
                                            start=True, stop=True))
        p.step("vector", lambda q: q.tensor_tensor(out=lgf[:], in0=lgf[:],
                                                   in1=prw[0:16, 0:512], op=ALU.mult))
        p.step("vector", lambda q, t0=t0: q.scalar_tensor_tensor(
            out=lgts[:, t0:t0 + 512], in0=lgf[:], scalar=rsimc[:],
            in1=sgc[:].broadcast_to([16, 512]), op0=ALU.mult, op1=ALU.subtract))

    for ti in range(TH // 128):
        t0 = ti * 128
        p.step("tensor", lambda q, t0=t0: q.transpose(
            out=prw[:, 0:16], in_=lgts[:, t0:t0 + 128], identity=identt[0:16, 0:16]))
        p.step("vector", lambda q: q.tensor_copy(out=lgtm[:], in_=prw[0:128, 0:16]))
        p.step("vector", lambda q: q.tensor_single_scalar(out=pos[:], in_=lgtm[:],
                                                          scalar=0.0, op=ALU.is_gt))
        p.step("vector", lambda q: q.tensor_scalar_max(out=gated[:], in0=lgtm[:],
                                                       scalar1=0.0))
        p.step("scalar", lambda q: q.activation(out=expg[:], in_=gated[:], func=AF.Exp))
        p.step("vector", lambda q: q.tensor_mul(out=num[:], in0=pos[:], in1=expg[:]))
        p.step("vector", lambda q: q.tensor_reduce(out=den[:], in_=num[:], axis=AX.X,
                                                   op=ALU.add))
        p.step("vector", lambda q: q.tensor_single_scalar(out=inact[:], in_=den[:],
                                                          scalar=0.5, op=ALU.is_lt))
        p.step("vector", lambda q: q.tensor_add(out=den[:], in0=den[:], in1=inact[:]))
        p.step("vector", lambda q: q.reciprocal(out=rac[:], in_=den[:]))
        p.step("scalar", lambda q: q.activation(out=rwact[:], in_=num[:], func=AF.Copy,
                                                scale=rac[:]))
        p.step("vector", lambda q: q.tensor_tensor(
            out=gtc[:].rearrange("p (a b) -> p a b", a=16),
            in0=lgtm[:].unsqueeze(1).broadcast_to([128, 16, 16]),
            in1=lgtm[:].unsqueeze(2).broadcast_to([128, 16, 16]), op=ALU.is_gt))
        p.step("vector", lambda q: q.tensor_reduce(
            out=cnt[:], in_=gtc[:].rearrange("p (a b) -> p a b", a=16), axis=AX.X,
            op=ALU.add))
        p.step("vector", lambda q: q.tensor_single_scalar(out=ind2[:], in_=cnt[:],
                                                          scalar=1.5, op=ALU.is_lt))
        p.step("vector", lambda q: q.tensor_scalar_mul(out=ind2[:], in0=ind2[:],
                                                       scalar1=0.5))
        p.step("vector", lambda q: q.scalar_tensor_tensor(
            out=rwtm[:], in0=ind2[:], scalar=inact[:], in1=rwact[:],
            op0=ALU.mult, op1=ALU.add))
        p.step("tensor", lambda q: q.transpose(out=prw[0:16, 0:128], in_=rwtm[:],
                                               identity=identt[:]))
        p.step("vector", lambda q, t0=t0: q.tensor_copy(out=rwf[:, t0:t0 + 128],
                                                        in_=prw[0:16, 0:128]))
    p.step("sync", lambda q: q.dma_start(out=rw_o, in_=rwf[:]), dma=True)

    for wsrc, dsto in ((wq_all, qm_o), (wk_all, km_o), (wv_all, vm_o)):
        def ev2(m, t0, w, pb):
            p.step("tensor", lambda q, m=m, t0=t0: q.matmul(
                out=prw[:], lhsT=patt[:, m * 128:(m + 1) * 128],
                rhs=rwf[:, t0:t0 + 512], start=True, stop=True))
            p.step("vector", lambda q: q.tensor_copy(out=rwr[:], in_=prw[:]))
            p.step("vector", lambda q, pb=pb: q.tensor_tensor(
                out=tmp6[:], in0=pb[:, 0:512], in1=rwr[:], op=ALU.mult))
            p.step("tensor", lambda q, m=m, t0=t0: q.matmul(
                out=psX[t0 // 512][0:64, :], lhsT=fpt[:], rhs=tmp6[:],
                start=(m == 0), stop=(m == 7)))
            if m == 7:
                p.step("vector", lambda q, t0=t0: q.tensor_copy(
                    out=accb[:, t0:t0 + 512], in_=psX[t0 // 512][0:64, :]))
        steps_stream_matmul(p, wsrc, 8, 8,
                            lambda k, t0, w: vbf[:, k * TH + t0: k * TH + t0 + 512],
                            ev2, [(t0, 512) for t0 in range(0, TH, 512)],
                            wst, wbf, pmm, mgroup=1)
        p.step("sync", lambda q, dsto=dsto: q.dma_start(out=dsto, in_=accb[:]), dma=True)
    return p.finalize()


def run_E0(inputs, X5):
    nc = _BUILD_CACHE.setdefault("E0", build_E0())
    qp = np.asarray(inputs["L2_ffn_q_proj"], np.float32)
    kp = np.asarray(inputs["L2_ffn_k_proj"], np.float32)
    vp = np.asarray(inputs["L2_ffn_v_proj"], np.float32)
    pat = np.zeros((16, 8 * 128), np.float32)
    for ch in range(8):
        for j in range(128):
            pat[2 * ch + j // 64, ch * 128 + j] = 1.0
    base = {
        "wsim": np.ascontiguousarray(np.asarray(inputs["L2_ffn_sim_matrix"], np.float32)),
        "gates": np.asarray(inputs["L2_ffn_gates"], np.float32).reshape(1, 16),
        "wq_all": np.ascontiguousarray(qp.transpose(1, 0, 2).reshape(DM, DM)),
        "wk_all": np.ascontiguousarray(kp.transpose(1, 0, 2).reshape(DM, DM)),
        "wv_all": np.ascontiguousarray(vp.transpose(1, 0, 2).reshape(DM, DM)),
        "pat": pat,
        "ident": np.eye(128, dtype=np.float32),
    }
    import ml_dtypes
    fpat = np.zeros((128, 64), np.float32)
    for j in range(128):
        fpat[j, j % 64] = 1.0
    base["fpat"] = fpat.astype(ml_dtypes.bfloat16)
    in_maps = []
    for c in range(NCORES):
        b, s = c // 2, c % 2
        m = dict(base)
        m["xin"] = np.ascontiguousarray(X5[b][:, s * TH:(s + 1) * TH])
        in_maps.append(m)
    res = run_bass_kernel_spmd(nc, in_maps, core_ids=list(range(NCORES)))
    QM = np.empty((B, 64, T), ml_dtypes.bfloat16)
    KM = np.empty((B, 64, T), ml_dtypes.bfloat16)
    VM = np.empty((B, 64, T), ml_dtypes.bfloat16)
    RW = np.empty((B, 16, T), np.float32)
    for c in range(NCORES):
        b, s = c // 2, c % 2
        r = res.results[c]
        QM[b, :, s * TH:(s + 1) * TH] = r["qm"]
        KM[b, :, s * TH:(s + 1) * TH] = r["km"]
        VM[b, :, s * TH:(s + 1) * TH] = r["vm"]
        RW[b, :, s * TH:(s + 1) * TH] = r["rw"]
    return QM, KM, VM, RW



def build_E1():
    p = Prog("E1")
    NJ = 16
    SCALE = 0.125
    QF = p.din("QF", [64, NJ * 128], BF16)
    KM = p.din("KM", [64, T], BF16)
    VT = p.din("VT", [T, 64], BF16)
    mA1 = p.din("mA1", [128, 128])
    mA2_ = p.din("mA2x", [128, 128])
    mB1 = p.din("mB1", [128, 128])
    mB2_ = p.din("mB2x", [128, 128])
    ident = p.din("ident", [128, 128])
    o_o = p.dout("o", [64, NJ * 128])

    kaug = p.sbuf("kaug", [65, T], BF16)
    qaug = p.sbuf("qaug", [65, NJ * 128], BF16)
    vt = p.sbuf("vt", [128, 32 * 64], BF16)
    sa = p.sbuf("sa", [128, 34 * 128])
    pt = p.sbuf("pt", [128, 512], BF16)
    mcol = p.sbuf("mcol", [128, NJ])
    m16f = p.sbuf("m16f", [16, 128])
    nm16 = p.sbuf("nm16", [16, 128], BF16)
    srec = p.sbuf("srec", [1, 128])
    rep = p.sbuf("rep", [64, 128])
    oo = p.sbuf("oo", [64, NJ * 128])
    msA1 = p.sbuf("msA1", [128, 128])
    msA2 = p.sbuf("msA2", [128, 128])
    msB1 = p.sbuf("msB1", [128, 128])
    msB2 = p.sbuf("msB2", [128, 128])
    identt = p.sbuf("identt", [128, 128])
    onesb = p.sbuf("onesb", [128, 1], BF16)
    p.ones_row = p.sbuf("ones_row", [1, 128])
    psA = p.psum("psA", [128, 512])
    psB = p.psum("psB", [128, 512])
    psS = p.psum("psS", [1, 128])
    psO = p.psum("psO", [64, 128])
    psR = p.psum("psR", [64, 128])
    pstr = p.psum("pstr", [16, 128])

    p.step("vector", lambda q: q.memset(onesb[:], 1.0))
    p.step("vector", lambda q: q.memset(p.ones_row[:], 1.0))
    p.step("vector", lambda q: q.memset(qaug[:], 0.0))

    def ld(dst, src):
        p.step("sync", lambda q, d=dst, s=src: q.dma_start(out=d, in_=s), dma=True)
    ld(msA1[:], mA1)
    ld(msA2[:], mA2_)
    ld(msB1[:], mB1)
    ld(msB2[:], mB2_)
    ld(identt[:], ident)
    ld(qaug[0:64, :], QF)
    ld(kaug[0:64, :], KM)
    p.step("vector", lambda q: q.memset(kaug[64:65, :], 1.0))
    ld(vt[:].rearrange("p (c n) -> p c n", c=32), VT.rearrange("(c p) n -> p c n", p=128))

    for j in range(NJ):
        nkb = 2 * j + 2
        ngr = (nkb * 128 + 511) // 512

        for g in range(ngr):
            w = min(512, nkb * 128 - g * 512)
            p.step("tensor", lambda q, j=j, g=g, w=w: q.matmul(
                out=psA[:, 0:w], lhsT=qaug[0:64, j * 128:(j + 1) * 128],
                rhs=kaug[0:64, g * 512: g * 512 + w], start=True, stop=True))

            def evA(q, j=j, g=g, w=w, nkb=nkb):
                done = None
                for kb in range(g * 4, min(nkb, g * 4 + (w + 127) // 128)):
                    off = kb * 128 - g * 512
                    if kb == nkb - 2:
                        done = q.tensor_tensor(
                            out=sa[:, kb * 128:(kb + 1) * 128],
                            in0=psA[:, off:off + 128], in1=msA1[:], op=ALU.add)
                    elif kb == nkb - 1:
                        done = q.tensor_tensor(
                            out=sa[:, kb * 128:(kb + 1) * 128],
                            in0=psA[:, off:off + 128], in1=msA2[:], op=ALU.add)
                    else:
                        done = q.tensor_copy(out=sa[:, kb * 128:(kb + 1) * 128],
                                             in_=psA[:, off:off + 128])
                return done
            p.step("vector", evA)
        p.step("vector", lambda q, j=j, nkb=nkb: q.reduce_max(
            out=mcol[:, j:j + 1], in_=sa[:, 0:nkb * 128], axis=AX.X))

    p.step("tensor", lambda q: q.transpose(out=pstr[:], in_=mcol[:],
                                           identity=identt[:]))
    p.step("vector", lambda q: q.tensor_copy(out=m16f[:], in_=pstr[:]))
    p.step("vector", lambda q: q.tensor_scalar_mul(out=nm16[:], in0=m16f[:],
                                                   scalar1=-1.0))
    p.step("sync", lambda q: q.dma_start(
        out=qaug[64:65, :].rearrange("p (b t) -> p b t", b=NJ),
        in_=nm16[:].unsqueeze(1)), dma=True)

    for j in range(NJ):
        nkb = 2 * j + 2
        for g in range((nkb + 3) // 4):
            kbs = list(range(g * 4, min(nkb, g * 4 + 4)))

            def mmB(q, j=j, kbs=kbs):
                for i, kb in enumerate(kbs):
                    last = q.matmul(out=psB[:, i * 128:(i + 1) * 128],
                                    lhsT=kaug[0:65, kb * 128:(kb + 1) * 128],
                                    rhs=qaug[0:65, j * 128:(j + 1) * 128],
                                    start=True, stop=True)
                return last
            p.step("tensor", mmB)

            for i, kb in enumerate(kbs):
                if kb == nkb - 2:
                    p.step("vector", lambda q, i=i: q.tensor_tensor(
                        out=sa[:, 0:128], in0=psB[:, i * 128:(i + 1) * 128],
                        in1=msB1[:], op=ALU.add))
                    p.step("scalar", lambda q, i=i: q.activation(
                        out=pt[:, i * 128:(i + 1) * 128], in_=sa[:, 0:128],
                        func=AF.Exp, scale=SCALE))
                elif kb == nkb - 1:
                    p.step("vector", lambda q, i=i: q.tensor_tensor(
                        out=sa[:, 0:128], in0=psB[:, i * 128:(i + 1) * 128],
                        in1=msB2[:], op=ALU.add))
                    p.step("scalar", lambda q, i=i: q.activation(
                        out=pt[:, i * 128:(i + 1) * 128], in_=sa[:, 0:128],
                        func=AF.Exp, scale=SCALE))
                else:
                    p.step("scalar", lambda q, i=i: q.activation(
                        out=pt[:, i * 128:(i + 1) * 128],
                        in_=psB[:, i * 128:(i + 1) * 128], func=AF.Exp, scale=SCALE))

            def sums(q, g=g, kbs=kbs, nkb=nkb):
                for i, kb in enumerate(kbs):
                    last = q.matmul(out=psS[:], lhsT=onesb[:, 0:1],
                                    rhs=pt[:, i * 128:(i + 1) * 128],
                                    start=(kb == 0), stop=(kb == nkb - 1))
                return last
            p.step("tensor", sums)

            def pv(q, g=g, kbs=kbs, nkb=nkb):
                for i, kb in enumerate(kbs):
                    last = q.matmul(out=psO[:], lhsT=vt[:, kb * 64:(kb + 1) * 64],
                                    rhs=pt[:, i * 128:(i + 1) * 128],
                                    start=(kb == 0), stop=(kb == nkb - 1))
                return last
            p.step("tensor", pv)

        p.step("vector", lambda q: q.reciprocal(out=srec[:], in_=psS[0:1, :]))
        p.step("tensor", lambda q: q.matmul(out=psR[:], lhsT=p.ones_row[0:1, 0:64],
                                            rhs=srec[:], start=True, stop=True))
        p.step("vector", lambda q: q.tensor_copy(out=rep[:], in_=psR[:]))
        p.step("vector", lambda q, j=j: q.tensor_mul(
            out=oo[:, j * 128:(j + 1) * 128], in0=psO[:], in1=rep[:]))
    p.step("sync", lambda q: q.dma_start(out=o_o, in_=oo[:]), dma=True)
    return p.finalize()


def run_E1(inputs, QM, KM, VM):
    nc = _BUILD_CACHE.setdefault("E1", build_E1())
    import ml_dtypes
    i0, i1 = np.tril_indices(128, -1)
    causA = np.zeros((128, 128), np.float32)
    causA[np.triu_indices(128, 1)] = -1e30
    causB = np.ascontiguousarray(causA.T)
    full = np.full((128, 128), -1e30, np.float32)
    zero = np.zeros((128, 128), np.float32)
    in_maps = []
    for c in range(NCORES):
        b, par = c // 2, c % 2
        qblocks = [2 * j + par for j in range(16)]
        qf = np.concatenate([np.asarray(QM[b][:, qb * 128:(qb + 1) * 128])
                             for qb in qblocks], axis=1)
        if par == 0:
            m_a1, m_a2 = causA, full
            m_b1, m_b2 = causB, full
        else:
            m_a1, m_a2 = zero, causA
            m_b1, m_b2 = zero, causB
        in_maps.append({
            "QF": np.ascontiguousarray(qf.astype(ml_dtypes.bfloat16)),
            "KM": np.ascontiguousarray(np.asarray(KM[b], ml_dtypes.bfloat16)),
            "VT": np.ascontiguousarray(np.asarray(VM[b], np.float32).T.astype(ml_dtypes.bfloat16)),
            "mA1": m_a1, "mA2x": m_a2, "mB1": m_b1, "mB2x": m_b2,
            "ident": np.eye(128, dtype=np.float32),
        })
    res = run_bass_kernel_spmd(nc, in_maps, core_ids=list(range(NCORES)))
    AT2 = np.empty((B, 64, T), np.float32)
    for c in range(NCORES):
        b, par = c // 2, c % 2
        o = res.results[c]["o"]
        for j in range(16):
            qb = 2 * j + par
            AT2[b, :, qb * 128:(qb + 1) * 128] = o[:, j * 128:(j + 1) * 128]
    return AT2



def build_E2():
    p = Prog("E2")
    xin = p.din("xin", [DM, TH])
    at_i = p.din("at2", [64, TH], BF16)
    rw_i = p.din("rw", [16, TH])
    wo = p.din("wo", [DM, DM])
    pat = p.din("pat", [16, 8 * 128])
    pat2 = p.din("pat2", [64, 128], BF16)
    xo = p.dout("xo", [DM, TH])

    hres = p.sbuf("hres_s", [128, 8 * TH])
    atb = p.sbuf("atb", [64, TH], BF16)
    rwf = p.sbuf("rwf", [16, TH])
    patt = p.sbuf("patt", [16, 8 * 128])
    patt2 = p.sbuf("patt2", [64, 128], BF16)
    mixb = p.sbuf("mixb", [128, 8 * 512], BF16)
    rwr = p.sbuf("rwr", [128, 512])
    wst = [p.sbuf("wst0", [128, 8 * 128])]
    wbf = [p.sbuf("wbf0", [128, 8 * 128], BF16)]
    pmm = [p.psum(f"pmm{i}", [128, 512]) for i in range(2)]
    prw = p.psum("prw", [128, 512])
    par_ = p.psum("par", [128, 512])

    def ld(dst, src):
        p.step("sync", lambda q, d=dst, s=src: q.dma_start(out=d, in_=s), dma=True)
    ld(hres[:].rearrange("p (c t) -> p c t", c=8), xin.rearrange("(c p) t -> p c t", p=128))
    ld(atb[:], at_i)
    ld(rwf[:], rw_i)
    ld(patt[:], pat)
    ld(patt2[:], pat2)

    for tc in range(TH // 512):
        t0 = tc * 512
        for ch in range(8):
            p.step("tensor", lambda q, ch=ch, t0=t0: q.matmul(
                out=par_[:], lhsT=patt2[:], rhs=atb[:, t0:t0 + 512],
                start=True, stop=True))
            p.step("tensor", lambda q, ch=ch, t0=t0: q.matmul(
                out=prw[:], lhsT=patt[:, ch * 128:(ch + 1) * 128],
                rhs=rwf[:, t0:t0 + 512], start=True, stop=True))
            p.step("vector", lambda q: q.tensor_copy(out=rwr[:], in_=prw[:]))
            p.step("vector", lambda q, ch=ch: q.tensor_tensor(
                out=mixb[:, ch * 512:(ch + 1) * 512], in0=par_[:], in1=rwr[:],
                op=ALU.mult))

        def ev(m, tt0, w, pb, t0=t0):
            p.step("vector", lambda q, m=m, pb=pb, t0=t0: q.tensor_tensor(
                out=hres[:, m * TH + t0: m * TH + t0 + 512],
                in0=hres[:, m * TH + t0: m * TH + t0 + 512], in1=pb[:, 0:512],
                op=ALU.add))
        steps_stream_matmul(p, wo, 8, 8,
                            lambda k, tt0, w: mixb[:, k * 512: k * 512 + 512],
                            ev, [(0, 512)], wst, wbf, pmm, mgroup=1)

    p.step("sync", lambda q: q.dma_start(
        out=xo.rearrange("(c p) t -> p c t", p=128),
        in_=hres[:].rearrange("p (c t) -> p c t", c=8)), dma=True)
    return p.finalize()


def run_E2(inputs, X5, AT2, RW):
    nc = _BUILD_CACHE.setdefault("E2", build_E2())
    op = np.asarray(inputs["L2_ffn_o_proj"], np.float32)
    pat = np.zeros((16, 8 * 128), np.float32)
    for ch in range(8):
        for j in range(128):
            pat[2 * ch + j // 64, ch * 128 + j] = 1.0
    pat2 = np.zeros((64, 128), np.float32)
    for j in range(128):
        pat2[j % 64, j] = 1.0
    base = {
        "wo": np.ascontiguousarray(op.reshape(DM, DM)),
        "pat": pat,
    }
    import ml_dtypes
    base["pat2"] = pat2.astype(ml_dtypes.bfloat16)
    in_maps = []
    for c in range(NCORES):
        b, s = c // 2, c % 2
        m = dict(base)
        m["xin"] = np.ascontiguousarray(X5[b][:, s * TH:(s + 1) * TH])
        m["at2"] = np.ascontiguousarray(
            np.asarray(AT2[b][:, s * TH:(s + 1) * TH], ml_dtypes.bfloat16))
        m["rw"] = np.ascontiguousarray(RW[b][:, s * TH:(s + 1) * TH])
        in_maps.append(m)
    res = run_bass_kernel_spmd(nc, in_maps, core_ids=list(range(NCORES)))
    XO = np.empty((B, DM, T), np.float32)
    for c in range(NCORES):
        b, s = c // 2, c % 2
        XO[b, :, s * TH:(s + 1) * TH] = res.results[c]["xo"]
    return XO



def build_G0():
    p = Prog("G0")
    xin = p.din("xin", [DM, TH])
    wqkv = p.din("wqkv", [DM, 3 * DM])
    bqkv = p.din("bqkv", [128, 24])
    q2lp = p.din("q2lp", [64, 64], BF16)
    kv2lp = p.din("kv2lp", [64, 64], BF16)
    bq2l = p.din("bq2l", [64, 1])
    bkv2l = p.din("bkv2l", [64, 1])
    wqkp = p.din("wqkp", [64, 64], BF16)
    bwqk = p.din("bwqk", [64, 1])
    cos64 = p.din("cos64", [64, TH])
    sin64 = p.din("sin64", [64, TH])
    qfk_o = p.dout("qfk", [16, 64, TH], BF16)
    kl_o = p.dout("kl", [16, 64, TH], BF16)
    vl_o = p.dout("vl", [16, 64, TH], BF16)

    ubf = p.sbuf("ubf", [128, 8 * TH], BF16)
    uf = p.sbuf("uf", [128, 8 * 512])
    qkvb = p.sbuf("qkvb", [128, 512], BF16)
    bq = p.sbuf("bq", [128, 24])
    q2t = p.sbuf("q2t", [128, 64], BF16)
    kvt = p.sbuf("kvt", [128, 64], BF16)
    wqt = p.sbuf("wqt", [64, 64], BF16)
    bq2 = p.sbuf("bq2", [64, 1])
    bkv = p.sbuf("bkv", [64, 1])
    bwq = p.sbuf("bwq", [64, 1])
    cst = p.sbuf("cst", [64, TH])
    snt = p.sbuf("snt", [64, TH])
    lbuf = p.sbuf("lbuf", [64, 512])
    rb = p.sbuf("rb", [64, 512])
    scr = p.sbuf("scr", [64, 512])
    latb = p.sbuf("latb", [64, 512], BF16)
    wst = [p.sbuf("wst0", [128, 8 * 128])]
    wbf = [p.sbuf("wbf0", [128, 8 * 128], BF16)]
    pmm = [p.psum(f"pmm{i}", [128, 512]) for i in range(2)]
    pla = p.psum("pla", [64, 512])
    ln_setup(p)

    def ld(dst, src):
        p.step("sync", lambda q, d=dst, s=src: q.dma_start(out=d, in_=s), dma=True)
    ld(bq[:], bqkv)
    ld(q2t[0:64, :], q2lp)
    ld(q2t[64:128, :], q2lp)
    ld(kvt[0:64, :], kv2lp)
    ld(kvt[64:128, :], kv2lp)
    ld(wqt[:], wqkp)
    ld(bq2[:], bq2l)
    ld(bkv[:], bkv2l)
    ld(bwq[:], bwqk)
    ld(cst[:], cos64)
    ld(snt[:], sin64)

    TWS = [(t0, 512) for t0 in range(0, TH, 512)]
    for (t0, w) in TWS:
        p.step("sync", lambda q, t0=t0: q.dma_start(
            out=uf[:].rearrange("p (c t) -> p c t", c=8),
            in_=xin.rearrange("(c p) t -> p c t", p=128)[:, :, t0:t0 + 512]), dma=True)
        srcf = lambda c: uf[:, c * 512:(c + 1) * 512]
        dstf = lambda c, t0=t0: ubf[:, c * TH + t0: c * TH + t0 + 512]
        steps_ln(p, srcf, dstf, 512)

    def rope(which, t0):
        p.step("vector", lambda q, t0=t0: q.tensor_tensor(
            out=rb[0:16, :], in0=lbuf[0:16, :], in1=cst[0:16, t0:t0 + 512],
            op=ALU.mult))
        p.step("vector", lambda q, t0=t0: q.tensor_tensor(
            out=scr[0:16, :], in0=lbuf[32:48, :], in1=snt[32:48, t0:t0 + 512],
            op=ALU.mult))
        p.step("vector", lambda q: q.tensor_sub(out=rb[0:16, :], in0=rb[0:16, :],
                                                in1=scr[0:16, :]))
        p.step("vector", lambda q, t0=t0: q.tensor_tensor(
            out=rb[32:48, :], in0=lbuf[32:48, :], in1=cst[32:48, t0:t0 + 512],
            op=ALU.mult))
        p.step("vector", lambda q, t0=t0: q.tensor_tensor(
            out=scr[32:48, :], in0=lbuf[0:16, :], in1=snt[0:16, t0:t0 + 512],
            op=ALU.mult))
        p.step("vector", lambda q: q.tensor_tensor(out=rb[32:48, :], in0=rb[32:48, :],
                                                   in1=scr[32:48, :], op=ALU.add))

    def ev(m, t0, w, pb):
        kind = m // 8
        p.step("vector", lambda q, m=m, pb=pb: q.tensor_tensor(
            out=qkvb[:], in0=pb[:, 0:512],
            in1=bq[:, m:m + 1].broadcast_to([128, 512]), op=ALU.add))
        for sub in range(2):
            head = (m % 8) * 2 + sub
            lt = q2t if kind == 0 else kvt
            bias = bq2 if kind == 0 else bkv
            p.step("tensor", lambda q, sub=sub, lt=lt: q.matmul(
                out=pla[:], lhsT=lt[sub * 64:(sub + 1) * 64, :],
                rhs=qkvb[sub * 64:(sub + 1) * 64, :], start=True, stop=True))
            p.step("vector", lambda q, bias=bias: q.tensor_tensor(
                out=lbuf[:], in0=pla[:], in1=bias[:].broadcast_to([64, 512]),
                op=ALU.add))
            if kind == 2:
                p.step("vector", lambda q: q.tensor_copy(out=latb[:], in_=lbuf[:]))
                p.step("sync", lambda q, head=head, t0=t0: q.dma_start(
                    out=vl_o[head][:, t0:t0 + 512], in_=latb[:]), dma=True)
            else:
                rope(kind, t0)
                p.step("vector", lambda q: q.tensor_copy(out=latb[:], in_=rb[:]))
                if kind == 1:
                    p.step("sync", lambda q, head=head, t0=t0: q.dma_start(
                        out=kl_o[head][:, t0:t0 + 512], in_=latb[:]), dma=True)
                else:
                    p.step("tensor", lambda q: q.matmul(
                        out=pla[:], lhsT=wqt[:], rhs=latb[:], start=True, stop=True))
                    p.step("vector", lambda q: q.tensor_tensor(
                        out=lbuf[:], in0=pla[:], in1=bwq[:].broadcast_to([64, 512]),
                        op=ALU.add))
                    p.step("vector", lambda q: q.tensor_copy(out=latb[:], in_=lbuf[:]))
                    p.step("sync", lambda q, head=head, t0=t0: q.dma_start(
                        out=qfk_o[head][:, t0:t0 + 512], in_=latb[:]), dma=True)
    steps_stream_matmul(p, wqkv, 8, 24,
                        lambda k, t0, w: ubf[:, k * TH + t0: k * TH + t0 + 512],
                        ev, TWS, wst, wbf, pmm, mgroup=1)
    return p.finalize()


def _pad_lat(w32):
    raise NotImplementedError


def run_G0(inputs, X6):
    nc = _BUILD_CACHE.setdefault("G0", build_G0())
    import ml_dtypes

    def padmap(l):
        return (l // 2) if l % 2 == 0 else 32 + l // 2

    q2l = np.asarray(inputs["L3_attn_q2l_w"], np.float32)
    kv2l = np.asarray(inputs["L3_attn_kv2l_w"], np.float32)
    q2lp = np.zeros((64, 64), np.float32)
    kv2lp = np.zeros((64, 64), np.float32)
    for l in range(32):
        q2lp[:, padmap(l)] = q2l[l]
        kv2lp[:, padmap(l)] = kv2l[l]
    bq2 = np.zeros((64, 1), np.float32)
    bkv = np.zeros((64, 1), np.float32)
    for l in range(32):
        bq2[padmap(l), 0] = np.asarray(inputs["L3_attn_q2l_b"], np.float32)[l]
        bkv[padmap(l), 0] = np.asarray(inputs["L3_attn_kv2l_b"], np.float32)[l]
    wqk = np.asarray(inputs["L3_attn_wqk_w"], np.float32)
    wqkp = np.zeros((64, 64), np.float32)
    for li in range(32):
        for lo in range(32):
            wqkp[padmap(li), padmap(lo)] = wqk[lo, li]
    bwq = np.zeros((64, 1), np.float32)
    for l in range(32):
        bwq[padmap(l), 0] = np.asarray(inputs["L3_attn_wqk_b"], np.float32)[l]
    inv = 1.0 / 10000.0 ** (np.arange(0, 32, 2, dtype=np.float32) / 32.0)
    tt = np.arange(T, dtype=np.float32)
    fr = np.outer(inv, tt)
    cosf = np.cos(fr).astype(np.float32)
    sinf = np.sin(fr).astype(np.float32)
    base = {
        "wqkv": np.ascontiguousarray(np.asarray(inputs["L3_attn_c_attn_w"], np.float32).T),
        "bqkv": _chunkvec(inputs["L3_attn_c_attn_b"], 24),
        "q2lp": q2lp.astype(ml_dtypes.bfloat16),
        "kv2lp": kv2lp.astype(ml_dtypes.bfloat16),
        "bq2l": bq2, "bkv2l": bkv,
        "wqkp": wqkp.astype(ml_dtypes.bfloat16), "bwqk": bwq,
    }
    in_maps = []
    for c in range(NCORES):
        b, s = c // 2, c % 2
        m = dict(base)
        m["xin"] = np.ascontiguousarray(X6[b][:, s * TH:(s + 1) * TH])
        cos = np.zeros((64, TH), np.float32)
        sin = np.zeros((64, TH), np.float32)
        sl = slice(s * TH, (s + 1) * TH)
        cos[0:16] = cosf[:, sl]; cos[32:48] = cosf[:, sl]
        sin[0:16] = sinf[:, sl]; sin[32:48] = sinf[:, sl]
        m["cos64"] = cos
        m["sin64"] = sin
        in_maps.append(m)
    res = run_bass_kernel_spmd(nc, in_maps, core_ids=list(range(NCORES)))
    import ml_dtypes
    QFK = np.empty((B, 16, 64, T), ml_dtypes.bfloat16)
    KL = np.empty((B, 16, 64, T), ml_dtypes.bfloat16)
    VL = np.empty((B, 16, 64, T), ml_dtypes.bfloat16)
    for c in range(NCORES):
        b, s = c // 2, c % 2
        r = res.results[c]
        QFK[b, :, :, s * TH:(s + 1) * TH] = r["qfk"]
        KL[b, :, :, s * TH:(s + 1) * TH] = r["kl"]
        VL[b, :, :, s * TH:(s + 1) * TH] = r["vl"]
    return QFK, KL, VL



def build_G1():
    p = Prog("G1")
    NH_ = 8
    NQB = T // 128
    SCALE = float(32.0 ** -0.5)
    QF = p.din("QF", [NH_, 64, T], BF16)
    KF = p.din("KF", [NH_, 64, T], BF16)
    VT = p.din("VT", [NH_, T, 64], BF16)
    l2op = p.din("l2op", [64, 64], BF16)
    bl2o = p.din("bl2o", [64, 1])
    mA = p.din("mA", [128, 128])
    mB = p.din("mB", [128, 128])
    ident = p.din("ident", [128, 128])
    y_o = p.dout("y", [NH_, 64, T], BF16)

    kaug = p.sbuf("kaug", [65, T], BF16)
    qaug = p.sbuf("qaug", [65, T], BF16)
    vt = p.sbuf("vt", [128, 32 * 64], BF16)
    yh = p.sbuf("yh", [64, T], BF16)
    sa = p.sbuf("sa", [128, 32 * 128])
    sb = p.sbuf("sb", [128, 128])
    pt = p.sbuf("pt", [128, 512], BF16)
    mcol = p.sbuf("mcol", [128, 32])
    m32f = p.sbuf("m32f", [32, 128])
    nm32 = p.sbuf("nm32", [32, 128], BF16)
    srec = p.sbuf("srec", [1, 128])
    rep = p.sbuf("rep", [64, 128])
    ylatb = p.sbuf("ylatb", [64, 128], BF16)
    msA = p.sbuf("msA", [128, 128])
    msB = p.sbuf("msB", [128, 128])
    identt = p.sbuf("identt", [128, 128])
    onesb = p.sbuf("onesb", [128, 1], BF16)
    l2t = p.sbuf("l2t", [64, 64], BF16)
    bl2 = p.sbuf("bl2", [64, 1])
    p.ones_row = p.sbuf("ones_row", [1, 128])
    psA = p.psum("psA", [128, 512])
    psB = p.psum("psB", [128, 512])
    psS = p.psum("psS", [1, 128])
    psO = p.psum("psO", [64, 128])
    psR = p.psum("psR", [64, 128])
    pstr = p.psum("pstr", [32, 128])

    p.step("vector", lambda q: q.memset(onesb[:], 1.0))
    p.step("vector", lambda q: q.memset(p.ones_row[:], 1.0))
    p.step("vector", lambda q: q.memset(qaug[:], 0.0))

    def ld(dst, src):
        p.step("sync", lambda q, d=dst, s=src: q.dma_start(out=d, in_=s), dma=True)
    ld(msA[:], mA)
    ld(msB[:], mB)
    ld(identt[:], ident)
    ld(l2t[:], l2op)
    ld(bl2[:], bl2o)

    for h in range(NH_):
        ld(qaug[0:64, :], QF[h])
        ld(kaug[0:64, :], KF[h])
        p.step("vector", lambda q: q.memset(kaug[64:65, :], 1.0))
        ld(vt[:].rearrange("p (c n) -> p c n", c=32),
           VT[h].rearrange("(c p) n -> p c n", p=128))

        for qb in range(NQB):
            nkb = qb + 1
            ngr = (nkb * 128 + 511) // 512
            for g in range(ngr):
                w = min(512, nkb * 128 - g * 512)
                p.step("tensor", lambda q, qb=qb, g=g, w=w: q.matmul(
                    out=psA[:, 0:w], lhsT=qaug[0:64, qb * 128:(qb + 1) * 128],
                    rhs=kaug[0:64, g * 512: g * 512 + w], start=True, stop=True))

                def evA(q, g=g, w=w, nkb=nkb):
                    done = None
                    nk_here = (w + 127) // 128
                    for i in range(nk_here):
                        kb = g * 4 + i
                        if kb == nkb - 1:
                            done = q.tensor_tensor(
                                out=sa[:, kb * 128:(kb + 1) * 128],
                                in0=psA[:, i * 128:(i + 1) * 128], in1=msA[:],
                                op=ALU.add)
                        else:
                            done = q.tensor_copy(out=sa[:, kb * 128:(kb + 1) * 128],
                                                 in_=psA[:, i * 128:(i + 1) * 128])
                    return done
                p.step("vector", evA)
            p.step("vector", lambda q, qb=qb, nkb=nkb: q.reduce_max(
                out=mcol[:, qb:qb + 1], in_=sa[:, 0:nkb * 128], axis=AX.X))

        p.step("tensor", lambda q: q.transpose(out=pstr[:], in_=mcol[:],
                                               identity=identt[:]))
        p.step("vector", lambda q: q.tensor_copy(out=m32f[:], in_=pstr[:]))
        p.step("vector", lambda q: q.tensor_scalar_mul(out=nm32[:], in0=m32f[:],
                                                       scalar1=-1.0))
        p.step("sync", lambda q: q.dma_start(
            out=qaug[64:65, :].rearrange("p (b t) -> p b t", b=32),
            in_=nm32[:].unsqueeze(1)), dma=True)

        for qb in range(NQB):
            nkb = qb + 1
            for g in range((nkb + 3) // 4):
                kbs = list(range(g * 4, min(nkb, g * 4 + 4)))

                def mmB(q, qb=qb, kbs=kbs):
                    for i, kb in enumerate(kbs):
                        last = q.matmul(out=psB[:, i * 128:(i + 1) * 128],
                                        lhsT=kaug[0:65, kb * 128:(kb + 1) * 128],
                                        rhs=qaug[0:65, qb * 128:(qb + 1) * 128],
                                        start=True, stop=True)
                    return last
                p.step("tensor", mmB)

                for i, kb in enumerate(kbs):
                    if kb == nkb - 1:
                        p.step("vector", lambda q, i=i: q.tensor_tensor(
                            out=sb[:], in0=psB[:, i * 128:(i + 1) * 128],
                            in1=msB[:], op=ALU.add))
                        p.step("scalar", lambda q, i=i: q.activation(
                            out=pt[:, i * 128:(i + 1) * 128], in_=sb[:],
                            func=AF.Exp, scale=SCALE))
                    else:
                        p.step("scalar", lambda q, i=i: q.activation(
                            out=pt[:, i * 128:(i + 1) * 128],
                            in_=psB[:, i * 128:(i + 1) * 128], func=AF.Exp,
                            scale=SCALE))

                def sums(q, kbs=kbs, nkb=nkb):
                    for i, kb in enumerate(kbs):
                        last = q.matmul(out=psS[:], lhsT=onesb[:, 0:1],
                                        rhs=pt[:, i * 128:(i + 1) * 128],
                                        start=(kb == 0), stop=(kb == nkb - 1))
                    return last
                p.step("tensor", sums)

                def pv(q, kbs=kbs, nkb=nkb):
                    for i, kb in enumerate(kbs):
                        last = q.matmul(out=psO[:], lhsT=vt[:, kb * 64:(kb + 1) * 64],
                                        rhs=pt[:, i * 128:(i + 1) * 128],
                                        start=(kb == 0), stop=(kb == nkb - 1))
                    return last
                p.step("tensor", pv)

            p.step("vector", lambda q: q.reciprocal(out=srec[:], in_=psS[0:1, :]))
            p.step("tensor", lambda q: q.matmul(out=psR[:], lhsT=p.ones_row[0:1, 0:64],
                                                rhs=srec[:], start=True, stop=True))
            p.step("vector", lambda q: q.tensor_copy(out=rep[:], in_=psR[:]))
            p.step("vector", lambda q: q.tensor_mul(out=ylatb[:], in0=psO[:], in1=rep[:]))
            p.step("tensor", lambda q: q.matmul(out=psR[:], lhsT=l2t[:], rhs=ylatb[:],
                                                start=True, stop=True))
            p.step("vector", lambda q, qb=qb: q.tensor_tensor(
                out=yh[:, qb * 128:(qb + 1) * 128], in0=psR[:],
                in1=bl2[:].broadcast_to([64, 128]), op=ALU.add))
        p.step("sync", lambda q, h=h: q.dma_start(out=y_o[h], in_=yh[:]), dma=True)
    return p.finalize()


def run_G1(inputs, QFK, KL, VL):
    nc = _BUILD_CACHE.setdefault("G1", build_G1())
    import ml_dtypes

    def padmap(l):
        return (l // 2) if l % 2 == 0 else 32 + l // 2

    l2o = np.asarray(inputs["L3_attn_l2o_w"], np.float32)
    l2op = np.zeros((64, 64), np.float32)
    for l in range(32):
        l2op[padmap(l), :] = l2o[:, l]
    bl2 = np.asarray(inputs["L3_attn_l2o_b"], np.float32).reshape(64, 1)
    causA = np.zeros((128, 128), np.float32)
    causA[np.triu_indices(128, 1)] = -1e30
    causB = np.ascontiguousarray(causA.T)
    base = {
        "l2op": l2op.astype(ml_dtypes.bfloat16),
        "bl2o": bl2,
        "mA": causA, "mB": causB,
        "ident": np.eye(128, dtype=np.float32),
    }
    in_maps = []
    for c in range(NCORES):
        b, hh = c // 2, c % 2
        heads = slice(hh * 8, hh * 8 + 8)
        m = dict(base)
        m["QF"] = np.ascontiguousarray(QFK[b][heads])
        m["KF"] = np.ascontiguousarray(KL[b][heads])
        m["VT"] = np.ascontiguousarray(np.transpose(np.asarray(VL[b][heads]),
                                                    (0, 2, 1)))
        in_maps.append(m)
    res = run_bass_kernel_spmd(nc, in_maps, core_ids=list(range(NCORES)))
    Y3 = np.empty((B, DM, T), ml_dtypes.bfloat16)
    for c in range(NCORES):
        b, hh = c // 2, c % 2
        for j, h in enumerate(range(hh * 8, hh * 8 + 8)):
            Y3[b, h * 64:(h + 1) * 64, :] = res.results[c]["y"][j]
    return Y3


def run_G2(inputs, X6, Y3):
    nc = _BUILD_CACHE.setdefault("B2", build_B2())
    base = {
        "wout": np.ascontiguousarray(np.asarray(inputs["L3_attn_cproj_w"], np.float32).T),
        "bout": _chunkvec(inputs["L3_attn_cproj_b"], 8),
    }
    import ml_dtypes
    in_maps = []
    for c in range(NCORES):
        b, s = c // 2, c % 2
        m = dict(base)
        m["xin"] = np.ascontiguousarray(X6[b][:, s * TH:(s + 1) * TH])
        m["at"] = np.ascontiguousarray(np.asarray(Y3[b][:, s * TH:(s + 1) * TH],
                                                  ml_dtypes.bfloat16))
        in_maps.append(m)
    res = run_bass_kernel_spmd(nc, in_maps, core_ids=list(range(NCORES)))
    XO = np.empty((B, DM, T), np.float32)
    for c in range(NCORES):
        b, s = c // 2, c % 2
        XO[b, :, s * TH:(s + 1) * TH] = res.results[c]["xo"]
    return XO



RNE_C = 8388608.0

def build_H0():
    p = Prog("H0")
    TE3 = TH + 2
    xin = p.din("xin", [DM, TE3])
    w1T = p.din("w1T", [DM, 4 * DM])
    w2T = p.din("w2T", [4 * DM, DM])
    convw = p.din("convw", [128, 32 * 3])
    convb = p.din("convb", [128, 32])
    ident = p.din("ident", [128, 128])
    xo = p.dout("xo", [DM, TH])
    dbg = p.dout("dbg", [128, 80])

    W = 256
    RW_ = 254
    uf = p.sbuf("uf", [128, 8 * W])
    uf2 = p.sbuf("uf2", [128, 8 * W])
    xqb = p.sbuf("xqb", [128, 8 * W], BF16)
    y1buf = p.sbuf("y1buf", [128, 32 * W])
    hbuf = p.sbuf("hbuf", [128, 32 * RW_])
    htree = p.sbuf("htree", [128, 16 * RW_])
    hqb = p.sbuf("hqb", [128, 32 * RW_], BF16)
    xt7 = p.sbuf("xt7", [128, 8 * RW_])
    cwt = p.sbuf("cwt", [128, 32 * 3])
    cbt = p.sbuf("cbt", [128, 32])
    identt = p.sbuf("identt", [128, 128])
    wst = [p.sbuf("wst0", [128, 32 * 128])]
    wqb = [p.sbuf("wqb0", [128, 32 * 128], BF16)]
    wabs = p.sbuf("wabs", [128, 8 * 512])
    amaxt = p.sbuf("amaxt", [128, 2])
    sxt = p.sbuf("sxt", [128, 2])
    rmt = p.sbuf("rmt", [128, 2])
    trow = p.sbuf("trow", [2, 128])
    srow = p.sbuf("srow", [1, W])
    rmrow = p.sbuf("rmrow", [1, W])
    srep = p.sbuf("srep", [128, W])
    rmrep = p.sbuf("rmrep", [128, W])
    sm1 = p.sbuf("sm1", [1, 512])
    tot = p.sbuf("tot", [1, 2])
    swc = p.sbuf("swc", [128, 2])
    mvc = p.sbuf("mvc", [128, 2])
    p.ones_row = p.sbuf("ones_row", [1, 128])
    p.ones_col = p.sbuf("ones_col", [128, 1])
    pmm = [p.psum(f"pmm{i}", [128, 512]) for i in range(2)]
    ptr = p.psum("ptr", [128, 128])
    ln_setup(p, maxw=256,
             share=None)

    def ld(dst, src):
        p.step("sync", lambda q, d=dst, s=src: q.dma_start(out=d, in_=s), dma=True)
    ld(cwt[:], convw)
    ld(cbt[:], convb)
    ld(identt[:], ident)

    for wi, (wd, csz, ncols) in enumerate(((w1T, 8, 4 * DM), (w2T, 32, DM))):
        npieces = (csz * ncols) // (8 * 512)
        piece_cols = 8 * 512 // csz
        first = True
        for pc in range(npieces):
            p.step("sync", lambda q, wd=wd, csz=csz, pc=pc, piece_cols=piece_cols:
                   q.dma_start(
                       out=wabs[:].rearrange("p (c n) -> p c n", c=csz)[:, :, 0:piece_cols],
                       in_=wd.rearrange("(c p) n -> p c n", p=128)
                       [:, :, pc * piece_cols:(pc + 1) * piece_cols]), dma=True)
            p.step("scalar", lambda q: q.activation(out=wabs[:], in_=wabs[:],
                                                    func=AF.Abs))

            def accs(q, first=first):
                for c in range(8):
                    last = q.matmul(out=p.ln_pst1[0:1, 0:512], lhsT=p.ones_col[:, 0:1],
                                    rhs=wabs[:, c * 512:(c + 1) * 512],
                                    start=(first and c == 0), stop=False)
                return last
            p.step("tensor", accs)
            first = False
        p.step("vector", lambda q: q.tensor_copy(out=sm1[:], in_=p.ln_pst1[0:1, 0:512]))
        p.step("vector", lambda q, wi=wi: q.tensor_reduce(out=tot[0:1, wi:wi + 1],
                                                          in_=sm1[:], axis=AX.X,
                                                          op=ALU.add))
    p.step("vector", lambda q: q.tensor_scalar_mul(out=tot[0:1, 0:1], in0=tot[0:1, 0:1],
                                                   scalar1=1.0 / (4 * DM * DM)))
    p.step("vector", lambda q: q.tensor_scalar_mul(out=tot[0:1, 1:2], in0=tot[0:1, 1:2],
                                                   scalar1=1.0 / (4 * DM * DM)))
    p.step("vector", lambda q: q.tensor_scalar_max(out=tot[:], in0=tot[:], scalar1=1e-5))
    p.step("tensor", lambda q: q.matmul(out=ptr[:, 0:2], lhsT=p.ones_row[0:1, :],
                                        rhs=tot[:], start=True, stop=True))
    p.step("vector", lambda q: q.tensor_copy(out=mvc[:], in_=ptr[:, 0:2]))
    p.step("vector", lambda q: q.reciprocal(out=swc[:], in_=mvc[:]))

    def quant_slab(r, ncol, wi):
        p.step("vector", lambda q, r=r, ncol=ncol, wi=wi: q.tensor_scalar(
            out=wst[r][:, 0:ncol], in0=wst[r][:, 0:ncol],
            scalar1=swc[:, wi:wi + 1], scalar2=None, op0=ALU.mult))
        p.step("vector", lambda q, r=r, ncol=ncol: q.tensor_scalar_add(
            out=wst[r][:, 0:ncol], in0=wst[r][:, 0:ncol], scalar1=RNE_C))
        p.step("vector", lambda q, r=r, ncol=ncol: q.tensor_scalar_add(
            out=wst[r][:, 0:ncol], in0=wst[r][:, 0:ncol], scalar1=-RNE_C))
        p.step("vector", lambda q, r=r, ncol=ncol: q.tensor_scalar_min(
            out=wst[r][:, 0:ncol], in0=wst[r][:, 0:ncol], scalar1=1.0))
        p.step("vector", lambda q, r=r, ncol=ncol: q.tensor_scalar_max(
            out=wqb[r][:, 0:ncol], in0=wst[r][:, 0:ncol], scalar1=-1.0))

    starts = [254 * k for k in range(8)] + [TE3 - W]
    for k, S in enumerate(starts):
        outsl = (0, 254) if k < 8 else (238, 254)
        p.step("sync", lambda q, S=S: q.dma_start(
            out=uf[:].rearrange("p (c t) -> p c t", c=8),
            in_=xin.rearrange("(c p) t -> p c t", p=128)[:, :, S:S + W]), dma=True)
        srcf = lambda c: uf[:, c * W:(c + 1) * W]
        dstf = lambda c: uf2[:, c * W:(c + 1) * W]
        steps_ln(p, srcf, dstf, W)

        p.step("scalar", lambda q: q.activation(out=uf[:], in_=uf2[:], func=AF.Abs))
        p.step("vector", lambda q: q.tensor_tensor(out=uf[:, 0:4 * W], in0=uf[:, 0:4 * W],
                                                   in1=uf[:, 4 * W:8 * W], op=ALU.max))
        p.step("vector", lambda q: q.tensor_tensor(out=uf[:, 0:2 * W], in0=uf[:, 0:2 * W],
                                                   in1=uf[:, 2 * W:4 * W], op=ALU.max))
        p.step("vector", lambda q: q.tensor_tensor(out=uf[:, 0:W], in0=uf[:, 0:W],
                                                   in1=uf[:, W:2 * W], op=ALU.max))
        for tr in range(2):
            p.step("tensor", lambda q, tr=tr: q.transpose(
                out=ptr[:], in_=uf[:, tr * 128:(tr + 1) * 128], identity=identt[:]))
            p.step("vector", lambda q, tr=tr: q.reduce_max(
                out=amaxt[:, tr:tr + 1], in_=ptr[:], axis=AX.X))
        p.step("vector", lambda q: q.tensor_scalar_max(out=amaxt[:], in0=amaxt[:],
                                                       scalar1=1e-5))
        p.step("vector", lambda q: q.reciprocal(out=sxt[:], in_=amaxt[:]))
        p.step("vector", lambda q: q.tensor_scalar_mul(out=sxt[:], in0=sxt[:],
                                                       scalar1=127.0))
        p.step("vector", lambda q: q.reciprocal(out=rmt[:], in_=sxt[:]))
        p.step("vector", lambda q: q.tensor_scalar(out=rmt[:], in0=rmt[:],
                                                   scalar1=mvc[:, 0:1], scalar2=None,
                                                   op0=ALU.mult))
        def mkrow(src_, dstrow):
            p.step("tensor", lambda q, src_=src_: q.transpose(
                out=ptr[0:2, 0:128], in_=src_, identity=identt[:]))
            p.step("vector", lambda q: q.tensor_copy(out=trow[:], in_=ptr[0:2, 0:128]))
            p.step("sync", lambda q, dstrow=dstrow: q.dma_start(
                out=dstrow.rearrange("p (b t) -> p b t", b=2),
                in_=trow[:].unsqueeze(1)), dma=True)
        mkrow(sxt[:], srow[:])
        mkrow(rmt[:], rmrow[:])
        p.step("tensor", lambda q: q.matmul(out=pmm[0][:, 0:W], lhsT=p.ones_row[0:1, :],
                                            rhs=srow[:], start=True, stop=True))
        p.step("vector", lambda q: q.tensor_copy(out=srep[:], in_=pmm[0][:, 0:W]))
        p.step("tensor", lambda q: q.matmul(out=pmm[0][:, 0:W], lhsT=p.ones_row[0:1, :],
                                            rhs=rmrow[:], start=True, stop=True))
        p.step("vector", lambda q: q.tensor_copy(out=rmrep[:], in_=pmm[0][:, 0:W]))

        def qx(q):
            for c in range(8):
                last = q.tensor_tensor(out=uf[:, c * W:(c + 1) * W],
                                       in0=uf2[:, c * W:(c + 1) * W], in1=srep[:],
                                       op=ALU.mult)
            return last
        p.step("vector", qx)
        p.step("vector", lambda q: q.tensor_scalar_add(out=uf[:], in0=uf[:],
                                                       scalar1=RNE_C))
        p.step("vector", lambda q: q.tensor_scalar_add(out=uf[:], in0=uf[:],
                                                       scalar1=-RNE_C))
        p.step("vector", lambda q: q.tensor_scalar_min(out=uf[:], in0=uf[:],
                                                       scalar1=127.0))
        p.step("vector", lambda q: q.tensor_scalar_max(out=xqb[:], in0=uf[:],
                                                       scalar1=-128.0))

        if k == 0:
            def dmp1(q):
                q.tensor_copy(out=uf2[:, 0:2], in_=mvc[:])
                q.tensor_copy(out=uf2[:, 2:4], in_=amaxt[:])
                q.tensor_copy(out=uf2[:, 4:6], in_=sxt[:])
                q.tensor_copy(out=uf2[:, 6:8], in_=rmt[:])
                q.tensor_copy(out=uf2[:, 8:16], in_=srep[:, 0:8])
                return q.tensor_copy(out=uf2[:, 16:24], in_=rmrep[:, 0:8])
            def dmp1b(q):
                q.tensor_copy(out=xt7[:, 0:2], in_=mvc[:])
                q.tensor_copy(out=xt7[:, 2:4], in_=amaxt[:])
                q.tensor_copy(out=xt7[:, 4:6], in_=sxt[:])
                q.tensor_copy(out=xt7[:, 6:8], in_=rmt[:])
                q.tensor_copy(out=xt7[:, 8:16], in_=srep[:, 0:8])
                last = q.tensor_copy(out=xt7[:, 16:24], in_=rmrep[:, 0:8])
                return q.tensor_copy(out=xt7[:, 24:32], in_=xqb[:, 0:8])
            p.step("vector", dmp1b)
        for m in range(32):
            p.step("sync", lambda q, m=m: q.dma_start(
                out=wst[0][:, 0:8 * 128].rearrange("p (c n) -> p c n", c=8),
                in_=w1T.rearrange("(c p) n -> p c n", p=128)
                [:, :, m * 128:(m + 1) * 128]), dma=True)
            quant_slab(0, 8 * 128, 0)

            def mm1(q, m=m):
                for kk in range(8):
                    last = q.matmul(out=pmm[m % 2][:, 0:W],
                                    lhsT=wqb[0][:, kk * 128:(kk + 1) * 128],
                                    rhs=xqb[:, kk * W:(kk + 1) * W],
                                    start=(kk == 0), stop=(kk == 7))
                return last
            p.step("tensor", mm1)
            p.step("vector", lambda q, m=m: q.tensor_tensor(
                out=y1buf[:, m * W:(m + 1) * W], in0=pmm[m % 2][:, 0:W], in1=rmrep[:],
                op=ALU.mult))

        for c in range(32):
            p.step("vector", lambda q, c=c: q.tensor_scalar(
                out=hbuf[:, c * RW_:(c + 1) * RW_], in0=y1buf[:, c * W: c * W + RW_],
                scalar1=cwt[:, c * 3: c * 3 + 1], scalar2=None, op0=ALU.mult))
            def c2(q, c=c):
                q.scalar_tensor_tensor(
                    out=hbuf[:, c * RW_:(c + 1) * RW_],
                    in0=y1buf[:, c * W + 1: c * W + 1 + RW_],
                    scalar=cwt[:, c * 3 + 1: c * 3 + 2],
                    in1=hbuf[:, c * RW_:(c + 1) * RW_], op0=ALU.mult, op1=ALU.add)
                return q.scalar_tensor_tensor(
                    out=hbuf[:, c * RW_:(c + 1) * RW_],
                    in0=y1buf[:, c * W + 2: c * W + 2 + RW_],
                    scalar=cwt[:, c * 3 + 2: c * 3 + 3],
                    in1=hbuf[:, c * RW_:(c + 1) * RW_], op0=ALU.mult, op1=ALU.add)
            p.step("vector", c2)
            p.step("scalar", lambda q, c=c: q.activation(
                out=hbuf[:, c * RW_:(c + 1) * RW_], in_=hbuf[:, c * RW_:(c + 1) * RW_],
                func=AF.Silu, bias=cbt[:, c:c + 1]))

        if k == 0:
            def dmp2(q):
                q.tensor_copy(out=xt7[:, 32:40], in_=y1buf[:, 0:8])
                last = q.tensor_copy(out=xt7[:, 40:48], in_=hbuf[:, 0:8])
                return last
            p.step("vector", dmp2)
        p.step("scalar", lambda q: q.activation(out=y1buf[:, 0:32 * RW_], in_=hbuf[:],
                                                func=AF.Abs))
        p.step("vector", lambda q: q.tensor_tensor(
            out=htree[:], in0=y1buf[:, 0:16 * RW_], in1=y1buf[:, 16 * RW_:32 * RW_],
            op=ALU.max))
        p.step("vector", lambda q: q.tensor_tensor(
            out=htree[:, 0:8 * RW_], in0=htree[:, 0:8 * RW_],
            in1=htree[:, 8 * RW_:16 * RW_], op=ALU.max))
        p.step("vector", lambda q: q.tensor_tensor(
            out=htree[:, 0:4 * RW_], in0=htree[:, 0:4 * RW_],
            in1=htree[:, 4 * RW_:8 * RW_], op=ALU.max))
        p.step("vector", lambda q: q.tensor_tensor(
            out=htree[:, 0:2 * RW_], in0=htree[:, 0:2 * RW_],
            in1=htree[:, 2 * RW_:4 * RW_], op=ALU.max))
        p.step("vector", lambda q: q.tensor_tensor(
            out=htree[:, 0:RW_], in0=htree[:, 0:RW_], in1=htree[:, RW_:2 * RW_],
            op=ALU.max))
        for tr in range(2):
            wtr = min(128, RW_ - tr * 128)
            p.step("tensor", lambda q, tr=tr, wtr=wtr: q.transpose(
                out=ptr[0:wtr, 0:128], in_=htree[:, tr * 128: tr * 128 + wtr],
                identity=identt[:]))
            p.step("vector", lambda q, tr=tr, wtr=wtr: q.reduce_max(
                out=amaxt[0:wtr, tr:tr + 1], in_=ptr[0:wtr, 0:128], axis=AX.X))
        p.step("vector", lambda q: q.tensor_scalar_max(out=amaxt[:], in0=amaxt[:],
                                                       scalar1=1e-5))
        p.step("vector", lambda q: q.reciprocal(out=sxt[:], in_=amaxt[:]))
        p.step("vector", lambda q: q.tensor_scalar_mul(out=sxt[:], in0=sxt[:],
                                                       scalar1=127.0))
        p.step("vector", lambda q: q.reciprocal(out=rmt[:], in_=sxt[:]))
        p.step("vector", lambda q: q.tensor_scalar(out=rmt[:], in0=rmt[:],
                                                   scalar1=mvc[:, 1:2], scalar2=None,
                                                   op0=ALU.mult))
        mkrow(sxt[:], srow[:])
        mkrow(rmt[:], rmrow[:])
        p.step("tensor", lambda q: q.matmul(out=pmm[0][:, 0:RW_],
                                            lhsT=p.ones_row[0:1, :],
                                            rhs=srow[0:1, 0:RW_], start=True, stop=True))
        p.step("vector", lambda q: q.tensor_copy(out=srep[:, 0:RW_],
                                                 in_=pmm[0][:, 0:RW_]))
        p.step("tensor", lambda q: q.matmul(out=pmm[0][:, 0:RW_],
                                            lhsT=p.ones_row[0:1, :],
                                            rhs=rmrow[0:1, 0:RW_], start=True, stop=True))
        p.step("vector", lambda q: q.tensor_copy(out=rmrep[:, 0:RW_],
                                                 in_=pmm[0][:, 0:RW_]))

        def qh(q):
            for c in range(32):
                last = q.tensor_tensor(out=y1buf[:, c * RW_:(c + 1) * RW_],
                                       in0=hbuf[:, c * RW_:(c + 1) * RW_],
                                       in1=srep[:, 0:RW_], op=ALU.mult)
            return last
        p.step("vector", qh)
        p.step("vector", lambda q: q.tensor_scalar_add(
            out=y1buf[:, 0:32 * RW_], in0=y1buf[:, 0:32 * RW_], scalar1=RNE_C))
        p.step("vector", lambda q: q.tensor_scalar_add(
            out=y1buf[:, 0:32 * RW_], in0=y1buf[:, 0:32 * RW_], scalar1=-RNE_C))
        p.step("vector", lambda q: q.tensor_scalar_min(
            out=y1buf[:, 0:32 * RW_], in0=y1buf[:, 0:32 * RW_], scalar1=127.0))
        p.step("vector", lambda q: q.tensor_scalar_max(
            out=hqb[:], in0=y1buf[:, 0:32 * RW_], scalar1=-128.0))
        if k == 0:
            def dmp3(q):
                q.tensor_copy(out=xt7[:, 48:56], in_=hqb[:, 0:8])
                q.tensor_copy(out=xt7[:, 56:58], in_=sxt[:])
                last = q.tensor_copy(out=xt7[:, 58:60], in_=rmt[:])
                return last
            p.step("vector", dmp3)
            p.step("sync", lambda q: q.dma_start(out=dbg[:, 0:60], in_=xt7[:, 0:60]),
                   dma=True)

        p.step("sync", lambda q, S=S: q.dma_start(
            out=xt7[:].rearrange("p (c t) -> p c t", c=8),
            in_=xin.rearrange("(c p) t -> p c t", p=128)[:, :, S + 1:S + 1 + RW_]),
            dma=True)
        for m in range(8):
            p.step("sync", lambda q, m=m: q.dma_start(
                out=wst[0][:].rearrange("p (c n) -> p c n", c=32),
                in_=w2T.rearrange("(c p) n -> p c n", p=128)
                [:, :, m * 128:(m + 1) * 128]), dma=True)
            quant_slab(0, 32 * 128, 1)

            def mm2(q, m=m):
                for kk in range(32):
                    last = q.matmul(out=pmm[m % 2][:, 0:RW_],
                                    lhsT=wqb[0][:, kk * 128:(kk + 1) * 128],
                                    rhs=hqb[:, kk * RW_:(kk + 1) * RW_],
                                    start=(kk == 0), stop=(kk == 31))
                return last
            p.step("tensor", mm2)
            p.step("vector", lambda q, m=m: q.tensor_tensor(
                out=uf[:, 0:RW_], in0=pmm[m % 2][:, 0:RW_], in1=rmrep[:, 0:RW_],
                op=ALU.mult))
            p.step("vector", lambda q, m=m: q.tensor_tensor(
                out=uf[:, 0:RW_], in0=uf[:, 0:RW_],
                in1=xt7[:, m * RW_:(m + 1) * RW_], op=ALU.add))
            a, bnd = outsl
            p.step("sync", lambda q, m=m, S=S, a=a, bnd=bnd: q.dma_start(
                out=xo.rearrange("(c p) t -> p c t", p=128)
                [:, m, S + a: S + bnd],
                in_=uf[:, a:bnd]), dma=True)

    return p.finalize()


def run_H0(inputs, X7):
    nc = _BUILD_CACHE.setdefault("H0", build_H0())
    cw = np.asarray(inputs["L3_ffn_conv_w"], np.float32)[:, 0, :]
    base = {
        "w1T": np.ascontiguousarray(np.asarray(inputs["L3_ffn_w1"], np.float32).T),
        "w2T": np.ascontiguousarray(np.asarray(inputs["L3_ffn_w2"], np.float32).T),
        "convw": np.ascontiguousarray(
            cw.reshape(32, 128, 3).transpose(1, 0, 2).reshape(128, 96)),
        "convb": _chunkvec(inputs["L3_ffn_conv_b"], 32),
        "ident": np.eye(128, dtype=np.float32),
    }
    in_maps = []
    for c in range(NCORES):
        b, s = c // 2, c % 2
        m = dict(base)
        if s == 0:
            seg = np.concatenate([np.zeros((DM, 1), np.float32), X7[b][:, 0:TH + 1]],
                                 axis=1)
        else:
            seg = np.concatenate([X7[b][:, TH - 1: T],
                                  np.zeros((DM, 1), np.float32)], axis=1)
        m["xin"] = np.ascontiguousarray(seg)
        in_maps.append(m)
    res = run_bass_kernel_spmd(nc, in_maps, core_ids=list(range(NCORES)))
    XO = np.empty((B, DM, T), np.float32)
    for c in range(NCORES):
        b, s = c // 2, c % 2
        XO[b, :, s * TH:(s + 1) * TH] = res.results[c]["xo"]
    return XO



def kernel(**inputs):
    inputs = {k: np.asarray(v) for k, v in inputs.items()}
    X = np.ascontiguousarray(
        np.transpose(np.asarray(inputs["L0_attn_x"], np.float32), (0, 2, 1)))
    U0, Z, XC, DT, BC = run_A0(inputs, np.transpose(X, (0, 2, 1)))
    YS = run_A1(inputs, DT, XC, BC)
    HR, MIN, CFM = run_A2a(inputs, U0, YS, XC, Z)
    X1 = run_A2b(inputs, HR, MIN, CFM, X)
    del U0, Z, XC, DT, BC, YS, HR, MIN, CFM
    X2 = run_F0(inputs, X1)
    del X1
    QKV, GATE = run_B0(inputs, X2)
    ATT = run_B1(inputs, QKV, GATE)
    X3 = run_B2(inputs, X2, ATT)
    del QKV, GATE, ATT, X2
    X4 = run_C0(inputs, X3)
    del X3
    RET = run_D0a(inputs, X4)
    X5 = run_D0b(inputs, X4, RET)
    del RET, X4
    QM, KM, VM, RW = run_E0(inputs, X5)
    AT2 = run_E1(inputs, QM, KM, VM)
    X6 = run_E2(inputs, X5, AT2, RW)
    del QM, KM, VM, RW, AT2, X5
    QFK, KL, VL = run_G0(inputs, X6)
    Y3 = run_G1(inputs, QFK, KL, VL)
    X7 = run_G2(inputs, X6, Y3)
    del QFK, KL, VL, Y3, X6
    X8 = run_H0(inputs, X7)
    del X7
    return np.ascontiguousarray(np.transpose(X8, (0, 2, 1)))


# version 02
# speedup vs baseline: 1.0821x; 1.0821x over previous
import contextlib
import numpy as np
import concourse.bass as bass
import concourse.mybir as mybir
from concourse.bass_utils import run_bass_kernel_spmd

F32 = mybir.dt.float32
BF16 = mybir.dt.bfloat16
AF = mybir.ActivationFunctionType
ALU = mybir.AluOpType
AX = mybir.AxisListType

B, T, DM = 4, 4096, 1024
TH = T // 2
DI = 2048
NS = 16
NCORES = 8

_BUILD_CACHE = {}


def _chunkvec(v, c):
    return np.ascontiguousarray(np.asarray(v, np.float32).reshape(c, 128).T)


def _fm(a):
    return np.ascontiguousarray(np.asarray(a, np.float32).T)


class Prog:

    ENGINES = ("sync", "scalar", "vector", "tensor", "gpsimd")

    def __init__(self, name):
        self.nc = bass.Bass("TRN2", target_bir_lowering=False)
        self.es = contextlib.ExitStack()
        self.name = name
        self.steps = []
        self._sem_total = 0
        self.dram = {}

    def din(self, name, shape, dt=F32):
        ap = self.nc.dram_tensor(name, list(shape), dt, kind="ExternalInput").ap()
        self.dram[name] = ap
        return ap

    def dout(self, name, shape, dt=F32):
        ap = self.nc.dram_tensor(name, list(shape), dt, kind="ExternalOutput").ap()
        self.dram[name] = ap
        return ap

    def sbuf(self, name, shape, dt=F32):
        return self.es.enter_context(self.nc.sbuf_tensor(name, list(shape), dt))

    def psum(self, name, shape, dt=F32):
        return self.es.enter_context(self.nc.psum_tensor(name, list(shape), dt))

    def step(self, engine, fn, dma=False):
        self.steps.append((engine, fn, 16 if dma else 1))

    def finalize(self):
        nc = self.nc
        chain = self.es.enter_context(nc.semaphore("chain"))
        block = self.es.enter_context(nc.Block())
        totals = []
        acc = 0
        for (_, _, inc) in self.steps:
            totals.append(acc)
            acc += inc
        final_total = acc
        steps = self.steps

        def make(engine_name):
            def emit(q):
                for i, (eng, fn, inc) in enumerate(steps):
                    if eng != engine_name:
                        continue
                    q.wait_ge(chain, totals[i])
                    last = fn(q)
                    assert last is not None, f"step {i} returned no instruction"
                    last.then_inc(chain, inc)
                q.wait_ge(chain, final_total)
            return emit

        for eng in self.ENGINES:
            getattr(block, eng)(make(eng))
        self.es.close()
        return nc



def emit_ln_stats(p, q_t, pst1, pst2, x_tiles, x2_tile, w, nchunk):
    for c in range(nchunk):
        q_t.matmul(out=pst1[0:1, 0:w], lhsT=p.ones_col[:, 0:1], rhs=x_tiles(c),
                   start=(c == 0), stop=(c == nchunk - 1))
    for c in range(nchunk):
        mm = q_t.matmul(out=pst2[0:1, 0:w], lhsT=p.ones_col[:, 0:1], rhs=x2_tile(c),
                        start=(c == 0), stop=(c == nchunk - 1))
    return mm


def steps_ln_smalls(p, pst1, pst2, w, nfeat, sm):
    inv = 1.0 / nfeat
    p.step("vector", lambda q: q.tensor_scalar_mul(
        out=sm["mu"][0:1, 0:w], in0=pst1[0:1, 0:w], scalar1=inv))
    p.step("vector", lambda q: q.tensor_scalar_mul(
        out=sm["ex2"][0:1, 0:w], in0=pst2[0:1, 0:w], scalar1=inv))
    p.step("vector", lambda q: q.tensor_mul(
        out=sm["var"][0:1, 0:w], in0=sm["mu"][0:1, 0:w], in1=sm["mu"][0:1, 0:w]))
    p.step("vector", lambda q: q.tensor_sub(
        out=sm["var"][0:1, 0:w], in0=sm["ex2"][0:1, 0:w], in1=sm["var"][0:1, 0:w]))
    p.step("vector", lambda q: q.tensor_scalar_add(
        out=sm["var"][0:1, 0:w], in0=sm["var"][0:1, 0:w], scalar1=1e-5))
    p.step("scalar", lambda q: q.activation(
        out=sm["lnv"][0:1, 0:w], in_=sm["var"][0:1, 0:w], func=AF.Ln))
    p.step("scalar", lambda q: q.activation(
        out=sm["rstd"][0:1, 0:w], in_=sm["lnv"][0:1, 0:w], func=AF.Exp, scale=-0.5))
    p.step("vector", lambda q: q.tensor_mul(
        out=sm["nmr"][0:1, 0:w], in0=sm["mu"][0:1, 0:w], in1=sm["rstd"][0:1, 0:w]))
    p.step("vector", lambda q: q.tensor_scalar_mul(
        out=sm["nmr"][0:1, 0:w], in0=sm["nmr"][0:1, 0:w], scalar1=-1.0))


def emit_replicate2(p, q_t, prepA, prepB, sm, w):
    q_t.matmul(out=prepA[:, 0:w], lhsT=p.ones_row[0:1, :], rhs=sm["rstd"][0:1, 0:w],
               start=True, stop=True)
    return q_t.matmul(out=prepB[:, 0:w], lhsT=p.ones_row[0:1, :],
                      rhs=sm["nmr"][0:1, 0:w], start=True, stop=True)



def build_A0():
    p = Prog("A0")
    nc = p.nc
    TE = TH + 8

    xT = p.din("xT", [DM, TE])
    lng1 = p.din("lng1", [128, 8])
    lnb1 = p.din("lnb1", [128, 8])
    w_in = p.din("w_in", [DM, 4 * DM])
    b_in = p.din("b_in", [128, 32])
    w_conv = p.din("w_conv", [128, 16 * 4])
    b_conv = p.din("b_conv", [128, 16])
    w_xp = p.din("w_xp", [DI, 96])
    w_dt = p.din("w_dt", [64, DI])
    b_dt = p.din("b_dt", [128, 16])
    hmask = p.din("hmask", [128, 8])

    z_o = p.dout("z", [DI, TH])
    xc_o = p.dout("xc", [DI, TH])
    dt_o = p.dout("dt", [DI, TH])
    bc_o = p.dout("bc", [32, TH])
    u0_o = p.dout("u0", [DM, TH])

    STE = 512 + 8
    xt = p.sbuf("xt", [128, 8 * STE])
    hn = p.sbuf("hn", [128, 8 * STE], BF16)
    xn = p.sbuf("xn", [128, 8 * 512])
    x2 = p.sbuf("x2", [128, 8 * 512])
    xi = p.sbuf("xi", [128, 16 * STE])
    xcs = p.sbuf("xcs", [128, 512])
    xcb = p.sbuf("xcb", [128, 16 * 512], BF16)
    zs = p.sbuf("zs", [128, 512])
    wst = [p.sbuf(f"wst{i}", [128, 8 * 256]) for i in range(2)]
    wbf = [p.sbuf(f"wbf{i}", [128, 8 * 256], BF16) for i in range(2)]
    lg = p.sbuf("lg", [128, 8])
    lb = p.sbuf("lb", [128, 8])
    bin_t = p.sbuf("bin_t", [128, 32])
    wcv = p.sbuf("wcv", [128, 16 * 4])
    bcv = p.sbuf("bcv", [128, 16])
    wxp = p.sbuf("wxp", [128, 16 * 96])
    wxpb = p.sbuf("wxpb", [128, 16 * 96], BF16)
    wdt = p.sbuf("wdt", [64, DI])
    wdtb = p.sbuf("wdtb", [64, DI], BF16)
    bdt = p.sbuf("bdt", [128, 16])
    hm = p.sbuf("hm", [128, 8])
    proj = p.sbuf("proj", [96, 512])
    projb = p.sbuf("projb", [64, 512], BF16)
    ddt = p.sbuf("ddt", [128, 512])
    dta = p.sbuf("dta", [128, 512])
    dtb_ = p.sbuf("dtb_", [128, 512])
    rep = p.sbuf("rep", [128, 2 * 512])
    p.ones_col = p.sbuf("ones_col", [128, 1])
    p.ones_row = p.sbuf("ones_row", [1, 128])
    sm = {k: p.sbuf("sm_" + k, [1, 512]) for k in
          ("mu", "ex2", "var", "lnv", "rstd", "nmr")}
    p.sm = sm

    pst1 = p.psum("pst1", [1, 512])
    pst2 = p.psum("pst2", [1, 512])
    prep = p.psum("prep", [128, 512])
    prep2 = p.psum("prep2", [128, 512])
    pmm = [p.psum(f"pmm{i}", [128, 512]) for i in range(2)]
    pxp = p.psum("pxp", [96, 512])

    p.step("vector", lambda q: q.memset(p.ones_col[:], 1.0))
    p.step("vector", lambda q: q.memset(p.ones_row[:], 1.0))

    def ld(dst, src):
        p.step("sync", lambda q, d=dst, s=src: q.dma_start(out=d, in_=s), dma=True)
    ld(lg[:], lng1)
    ld(lb[:], lnb1)
    ld(bin_t[:], b_in)
    ld(wcv[:], w_conv)
    ld(bcv[:], b_conv)
    ld(wxp[:], w_xp.rearrange("(c p) n -> p c n", p=128))
    ld(wdt[:], w_dt)
    ld(bdt[:], b_dt)
    ld(hm[:], hmask)
    p.step("scalar", lambda q: q.copy(out=wxpb[:], in_=wxp[:]))
    p.step("scalar", lambda q: q.copy(out=wdtb[:], in_=wdt[:]))

    for st in range(4):
        ebase = st * 512
        p.step("sync", lambda q, e=ebase: q.dma_start(
            out=xt[:].rearrange("p (c t) -> p c t", c=8),
            in_=xT.rearrange("(c p) t -> p c t", p=128)[:, :, e:e + STE]), dma=True)

        for (off, w) in ((0, 8), (8, 512)):
            xt_c = lambda c, o=off, ww=w: xt[:, c * STE + o: c * STE + o + ww]
            x2_c = lambda c, ww=w: x2[:, c * 512: c * 512 + ww]
            xn_c = lambda c, ww=w: xn[:, c * 512: c * 512 + ww]
            hn_c = lambda c, o=off, ww=w: hn[:, c * STE + o: c * STE + o + ww]

            def sq1(q, ww=w, xt_c=xt_c, x2_c=x2_c):
                for c in range(8):
                    last = q.activation(out=x2_c(c), in_=xt_c(c), func=AF.Square)
                return last
            p.step("scalar", sq1)
            p.step("tensor", lambda q, ww=w, xt_c=xt_c, x2_c=x2_c:
                   emit_ln_stats(p, q, pst1, pst2, xt_c, x2_c, ww, 8))
            steps_ln_smalls(p, pst1, pst2, w, DM, sm)
            p.step("tensor", lambda q, ww=w: emit_replicate2(p, q, prep, prep2, sm, ww))

            def apply1a(q, ww=w, xt_c=xt_c, xn_c=xn_c):
                for c in range(8):
                    last = q.tensor_tensor(out=xn_c(c), in0=xt_c(c), in1=prep[:, 0:ww],
                                           op=ALU.mult)
                return last
            p.step("vector", apply1a)

            def apply1b(q, ww=w, xn_c=xn_c):
                for c in range(8):
                    last = q.tensor_tensor(out=xn_c(c), in0=xn_c(c),
                                           in1=prep2[:, 0:ww], op=ALU.add)
                return last
            p.step("vector", apply1b)
            if off == 8:
                p.step("sync", lambda q, st=st, ww=w: q.dma_start(
                    out=u0_o.rearrange("(c p) t -> p c t", p=128)
                    [:, :, st * 512: st * 512 + ww],
                    in_=xn[:].rearrange("p (c t) -> p c t", c=8)[:, :, 0:ww]), dma=True)

            def sq2(q, ww=w, xn_c=xn_c, x2_c=x2_c):
                for c in range(8):
                    last = q.activation(out=x2_c(c), in_=xn_c(c), func=AF.Square)
                return last
            p.step("scalar", sq2)
            p.step("tensor", lambda q, ww=w, xn_c=xn_c, x2_c=x2_c:
                   emit_ln_stats(p, q, pst1, pst2, xn_c, x2_c, ww, 8))
            steps_ln_smalls(p, pst1, pst2, w, DM, sm)
            p.step("tensor", lambda q, ww=w: emit_replicate2(p, q, prep, prep2, sm, ww))

            def apply2a(q, ww=w, xn_c=xn_c):
                for c in range(8):
                    last = q.tensor_tensor(out=xn_c(c), in0=xn_c(c), in1=prep[:, 0:ww],
                                           op=ALU.mult)
                return last
            p.step("vector", apply2a)

            def apply2bb(q, ww=w, xn_c=xn_c):
                for c in range(8):
                    last = q.tensor_tensor(out=xn_c(c), in0=xn_c(c),
                                           in1=prep2[:, 0:ww], op=ALU.add)
                return last
            p.step("vector", apply2bb)

            def apply2b(q, ww=w, xn_c=xn_c, hn_c=hn_c):
                for c in range(8):
                    last = q.scalar_tensor_tensor(
                        out=hn_c(c), in0=xn_c(c), scalar=lg[:, c:c + 1],
                        in1=lb[:, c:c + 1].broadcast_to([128, ww]),
                        op0=ALU.mult, op1=ALU.add)
                return last
            p.step("vector", apply2b)

        for slab in range(16):
            ridx = slab % 2
            p.step("sync", lambda q, s=slab, r=ridx: q.dma_start(
                out=wst[r][:].rearrange("p (c n) -> p c n", c=8),
                in_=w_in.rearrange("(c p) n -> p c n", p=128)[:, :, s * 256:(s + 1) * 256]),
                dma=True)
            p.step("scalar", lambda q, r=ridx: q.copy(out=wbf[r][:], in_=wst[r][:]))
            for mm_i in range(2):
                m = slab * 2 + mm_i
                for (off, w) in ((0, 8), (8, 512)):
                    if m >= 16 and off == 0:
                        continue
                    pb = pmm[m % 2]

                    def mmf(q, m=m, mm_i=mm_i, off=off, w=w, pb=pb, r=ridx):
                        for k in range(8):
                            last = q.matmul(
                                out=pb[:, 0:w],
                                lhsT=wbf[r][:, k * 256 + mm_i * 128: k * 256 + mm_i * 128 + 128],
                                rhs=hn[:, k * STE + off: k * STE + off + w],
                                start=(k == 0), stop=(k == 7))
                        return last
                    p.step("tensor", mmf)

                    if m < 16:
                        def ev(q, m=m, off=off, w=w, pb=pb):
                            return q.tensor_tensor(
                                out=xi[:, m * STE + off: m * STE + off + w],
                                in0=pb[:, 0:w],
                                in1=bin_t[:, m:m + 1].broadcast_to([128, w]),
                                op=ALU.add)
                        p.step("vector", ev)
                    else:
                        def evz(q, m=m, off=off, w=w, pb=pb):
                            return q.tensor_tensor(
                                out=zs[:, 0:w], in0=pb[:, 0:w],
                                in1=bin_t[:, m:m + 1].broadcast_to([128, w]),
                                op=ALU.add)
                        p.step("vector", evz)

                        def stz(q, m=m, off=off, w=w, st=st):
                            return q.dma_start(
                                out=z_o.rearrange("(c p) t -> p c t", p=128)
                                [:, m - 16, st * 512 + off - 8: st * 512 + off - 8 + w],
                                in_=zs[:, 0:w])
                        p.step("sync", stz, dma=True)

        if st == 0:
            def hmul(q):
                for c in range(16):
                    last = q.tensor_tensor(out=xi[:, c * STE: c * STE + 8],
                                           in0=xi[:, c * STE: c * STE + 8],
                                           in1=hm[:], op=ALU.mult)
                return last
            p.step("vector", hmul)

        for c in range(16):
            def conv4(q, c=c):
                q.tensor_scalar(out=xcs[:, 0:512],
                                in0=xi[:, c * STE + 5: c * STE + 5 + 512],
                                scalar1=wcv[:, c * 4: c * 4 + 1], scalar2=None,
                                op0=ALU.mult)
                for k in (1, 2, 3):
                    last = q.scalar_tensor_tensor(
                        out=xcs[:, 0:512],
                        in0=xi[:, c * STE + 5 + k: c * STE + 5 + k + 512],
                        scalar=wcv[:, c * 4 + k: c * 4 + k + 1],
                        in1=xcs[:, 0:512],
                        op0=ALU.mult, op1=ALU.add)
                return last
            p.step("vector", conv4)

            def sil(q, c=c):
                return q.activation(out=xcs[:, 0:512], in_=xcs[:, 0:512], func=AF.Silu,
                                    bias=bcv[:, c:c + 1])
            p.step("scalar", sil)
            def silb(q, c=c):
                return q.tensor_copy(out=xcb[:, c * 512:(c + 1) * 512], in_=xcs[:, 0:512])
            p.step("vector", silb)
            def stxc(q, c=c, st=st):
                return q.dma_start(
                    out=xc_o.rearrange("(c p) t -> p c t", p=128)[:, c, st * 512:(st + 1) * 512],
                    in_=xcs[:, 0:512])
            p.step("sync", stxc, dma=True)

        def mmxp(q):
            for k in range(16):
                last = q.matmul(out=pxp[:, 0:512],
                                lhsT=wxpb[:, k * 96:(k + 1) * 96],
                                rhs=xcb[:, k * 512: (k + 1) * 512],
                                start=(k == 0), stop=(k == 15))
            return last
        p.step("tensor", mmxp)
        p.step("vector", lambda q: q.tensor_copy(out=proj[:], in_=pxp[:, 0:512]))
        p.step("scalar", lambda q: q.copy(out=projb[:], in_=proj[0:64, :]))
        p.step("sync", lambda q, st=st: q.dma_start(
            out=bc_o[:, st * 512:(st + 1) * 512], in_=proj[64:96, :]), dma=True)

        for m in range(16):
            def mmdt(q, m=m):
                return q.matmul(out=pmm[m % 2][:, 0:512],
                                lhsT=wdtb[:, m * 128:(m + 1) * 128],
                                rhs=projb[:, 0:512], start=True, stop=True)
            p.step("tensor", mmdt)
            p.step("vector", lambda q, m=m: q.tensor_tensor(
                out=ddt[:, 0:512], in0=pmm[m % 2][:, 0:512],
                in1=bdt[:, m:m + 1].broadcast_to([128, 512]), op=ALU.add))

            p.step("scalar", lambda q, m=m: q.activation(out=dta[:], in_=ddt[:],
                                                         func=AF.Abs))
            p.step("scalar", lambda q, m=m: q.activation(out=dta[:], in_=dta[:],
                                                         func=AF.Exp, scale=-1.0))
            p.step("scalar", lambda q, m=m: q.activation(out=dta[:], in_=dta[:],
                                                         func=AF.Ln, bias=1.0))
            p.step("scalar", lambda q, m=m: q.activation(out=dtb_[:], in_=ddt[:],
                                                         func=AF.Relu))
            p.step("vector", lambda q, m=m: q.tensor_add(out=dta[:], in0=dta[:], in1=dtb_[:]))
            p.step("sync", lambda q, m=m, st=st: q.dma_start(
                out=dt_o.rearrange("(c p) t -> p c t", p=128)[:, m, st * 512:(st + 1) * 512],
                in_=dta[:]), dma=True)

    return p.finalize()


def run_A0(inputs, X):
    nc = _BUILD_CACHE.setdefault("A0", build_A0())
    w_in = np.ascontiguousarray(inputs["L0_attn_in_proj_w"].T)
    w_xp = np.ascontiguousarray(inputs["L0_attn_x_proj_w"].T)
    w_dt = np.ascontiguousarray(inputs["L0_attn_dt_proj_w"].T)
    base = {
        "lng1": _chunkvec(inputs["L0_attn_ln_mamba_g"], 8),
        "lnb1": _chunkvec(inputs["L0_attn_ln_mamba_b"], 8),
        "w_in": w_in,
        "b_in": _chunkvec(inputs["L0_attn_in_proj_b"], 32),
        "w_conv": np.ascontiguousarray(
            np.asarray(inputs["L0_attn_conv_w"], np.float32).reshape(16, 128, 4)
            .transpose(1, 0, 2).reshape(128, 64)),
        "b_conv": _chunkvec(inputs["L0_attn_conv_b"], 16),
        "w_xp": w_xp,
        "w_dt": w_dt,
        "b_dt": _chunkvec(inputs["L0_attn_dt_proj_b"], 16),
    }
    in_maps = []
    for c in range(NCORES):
        b, s = c // 2, c % 2
        if s == 0:
            seg = np.concatenate([np.zeros((8, DM), np.float32), X[b, 0:TH]], axis=0)
            hmask = np.zeros((128, 8), np.float32)
        else:
            seg = X[b, TH - 8: T]
            hmask = np.ones((128, 8), np.float32)
        m = dict(base)
        m["xT"] = np.ascontiguousarray(seg.T.astype(np.float32))
        m["hmask"] = hmask
        in_maps.append(m)
    res = run_bass_kernel_spmd(nc, in_maps, core_ids=list(range(NCORES)))
    U0 = np.empty((B, DM, T), np.float32)
    Z = np.empty((B, DI, T), np.float32)
    XC = np.empty((B, DI, T), np.float32)
    DT = np.empty((B, DI, T), np.float32)
    BC = np.empty((B, 32, T), np.float32)
    for c in range(NCORES):
        b, s = c // 2, c % 2
        r = res.results[c]
        U0[b, :, s * TH:(s + 1) * TH] = r["u0"]
        Z[b, :, s * TH:(s + 1) * TH] = r["z"]
        XC[b, :, s * TH:(s + 1) * TH] = r["xc"]
        DT[b, :, s * TH:(s + 1) * TH] = r["dt"]
        BC[b, :, s * TH:(s + 1) * TH] = r["bc"]
    return U0, Z, XC, DT, BC


def kernel(**inputs):
    raise NotImplementedError("phases under construction")



def build_A1(NT_=4, NN_=NS):
    p = Prog("A1")
    TT_ = 1024

    dts = p.din("dts", [1024, T])
    xcs = p.din("xcs", [1024, T])
    alog = p.din("alog", [128, 128])
    sel = p.din("sel", [16, 16 * 128])
    Bh = p.din("Bh", [16, T])
    Ch = p.din("Ch", [16, T])
    ys_o = p.dout("ys", [1024, T])

    dtb = p.sbuf("dtb", [128, 8 * TT_])
    xcb = p.sbuf("xcb", [128, 8 * TT_])
    ub = p.sbuf("ub", [128, 8 * TT_])
    yb = p.sbuf("yb", [128, 8 * TT_])
    brep = p.sbuf("brep", [128, TT_])
    crep = p.sbuf("crep", [128, TT_])
    dab = p.sbuf("dab", [128, TT_])
    hball = p.sbuf("hball", [128, 8 * TT_])
    acol = p.sbuf("acol", [128, 128])
    state = p.sbuf("state", [128, 128])
    bsml = p.sbuf("bsml", [16, TT_])
    csml = p.sbuf("csml", [16, TT_])
    selt = p.sbuf("selt", [16, 16 * 128])
    p.ones_row = p.sbuf("ones_row", [1, 128])
    prep1 = p.psum("prep1", [128, TT_])
    prep2 = p.psum("prep2", [128, TT_])

    p.step("vector", lambda q: q.memset(p.ones_row[:], 1.0))
    p.step("vector", lambda q: q.memset(state[:], 0.0))
    p.step("sync", lambda q: q.dma_start(out=acol[:], in_=alog), dma=True)
    p.step("sync", lambda q: q.dma_start(out=selt[:], in_=sel), dma=True)

    p.step("scalar", lambda q: q.activation(out=acol[:], in_=acol[:], func=AF.Exp))
    p.step("vector", lambda q: q.tensor_scalar_mul(out=acol[:], in0=acol[:], scalar1=-1.0))

    for it in range(NT_):
        t0 = it * TT_
        p.step("sync", lambda q, t0=t0: q.dma_start(
            out=dtb[:].rearrange("p (c t) -> p c t", c=8),
            in_=dts.rearrange("(c p) t -> p c t", p=128)[:, :, t0:t0 + TT_]), dma=True)
        p.step("sync", lambda q, t0=t0: q.dma_start(
            out=xcb[:].rearrange("p (c t) -> p c t", c=8),
            in_=xcs.rearrange("(c p) t -> p c t", p=128)[:, :, t0:t0 + TT_]), dma=True)
        p.step("sync", lambda q, t0=t0: q.dma_start(
            out=bsml[:], in_=Bh[:, t0:t0 + TT_]), dma=True)
        p.step("sync", lambda q, t0=t0: q.dma_start(
            out=csml[:], in_=Ch[:, t0:t0 + TT_]), dma=True)

        p.step("vector", lambda q: q.tensor_mul(out=ub[:], in0=dtb[:], in1=xcb[:]))

        for n in range(NN_):
            def repmm(q, n=n):
                lh = selt[:, n * 128:(n + 1) * 128]
                q.matmul(out=prep1[:, 0:512], lhsT=lh, rhs=bsml[:, 0:512],
                         start=True, stop=True)
                q.matmul(out=prep1[:, 512:1024], lhsT=lh, rhs=bsml[:, 512:1024],
                         start=True, stop=True)
                q.matmul(out=prep2[:, 0:512], lhsT=lh, rhs=csml[:, 0:512],
                         start=True, stop=True)
                return q.matmul(out=prep2[:, 512:1024], lhsT=lh, rhs=csml[:, 512:1024],
                                start=True, stop=True)
            p.step("tensor", repmm)
            p.step("vector", lambda q: q.tensor_copy(out=brep[:], in_=prep1[:]))
            p.step("vector", lambda q: q.tensor_copy(out=crep[:], in_=prep2[:]))

            p.step("vector", lambda q: q.tensor_tensor(
                out=xcb[:].rearrange("p (g t) -> p g t", g=8),
                in0=ub[:].rearrange("p (g t) -> p g t", g=8),
                in1=brep[:].unsqueeze(1).broadcast_to([128, 8, TT_]), op=ALU.mult))
            for g in range(8):
                col = g * 16 + n
                p.step("scalar", lambda q, g=g, col=col: q.activation(
                    out=dab[:], in_=dtb[:, g * TT_:(g + 1) * TT_], func=AF.Exp,
                    scale=acol[:, col:col + 1]))
                p.step("vector", lambda q, g=g, col=col: q.tensor_tensor_scan(
                    out=hball[:, g * TT_:(g + 1) * TT_], data0=dab[:],
                    data1=xcb[:, g * TT_:(g + 1) * TT_],
                    initial=state[:, col:col + 1], op0=ALU.mult, op1=ALU.add))

            def stcopy(q, n=n):
                for g in range(8):
                    col = g * 16 + n
                    last = q.tensor_copy(out=state[:, col:col + 1],
                                         in_=hball[:, g * TT_ + TT_ - 1: g * TT_ + TT_])
                return last
            p.step("vector", stcopy)
            p.step("gpsimd", lambda q: q.tensor_tensor(
                out=hball[:].rearrange("p (g t) -> p g t", g=8),
                in0=hball[:].rearrange("p (g t) -> p g t", g=8),
                in1=crep[:].unsqueeze(1).broadcast_to([128, 8, TT_]), op=ALU.mult))
            if n == 0:
                p.step("vector", lambda q: q.tensor_copy(out=yb[:], in_=hball[:]))
            else:
                p.step("vector", lambda q: q.tensor_add(out=yb[:], in0=yb[:],
                                                        in1=hball[:]))

        p.step("sync", lambda q, t0=t0: q.dma_start(
            out=ys_o.rearrange("(c p) t -> p c t", p=128)[:, :, t0:t0 + TT_],
            in_=yb[:].rearrange("p (c t) -> p c t", c=8)), dma=True)

    return p.finalize()


def run_A1(inputs, DT, XC, BC):
    nc = _BUILD_CACHE.setdefault("A1", build_A1())
    A_log = np.asarray(inputs["L0_attn_A_log"], np.float32)
    in_maps = []
    for c in range(NCORES):
        b, s = c // 2, c % 2
        sl = slice(s * 1024, (s + 1) * 1024)
        sel = np.zeros((16, 16 * 128), np.float32)
        for n in range(16):
            sel[n, n * 128:(n + 1) * 128] = 1.0
        in_maps.append({
            "sel": sel,
            "dts": np.ascontiguousarray(DT[b, sl]),
            "xcs": np.ascontiguousarray(XC[b, sl]),
            "alog": np.ascontiguousarray(
                A_log[sl].reshape(8, 128, 16).transpose(1, 0, 2).reshape(128, 128)),
            "Bh": np.ascontiguousarray(BC[b, 0:16]),
            "Ch": np.ascontiguousarray(BC[b, 16:32]),
        })
    res = run_bass_kernel_spmd(nc, in_maps, core_ids=list(range(NCORES)))
    YS = np.empty((B, DI, T), np.float32)
    for c in range(NCORES):
        b, s = c // 2, c % 2
        YS[b, s * 1024:(s + 1) * 1024] = res.results[c]["ys"]
    return YS



def ln_setup(p, maxw=512, share=None):
    p.ones_col = p.sbuf("ln_ones_col", [128, 1])
    p.ones_row = p.sbuf("ln_ones_row", [1, 128])
    p.ln_x2 = p.sbuf("ln_x2", [128, 8 * maxw])
    p.ln_sm = {k: p.sbuf("ln_sm_" + k, [1, 512]) for k in
               ("mu", "ex2", "var", "lnv", "rstd", "nmr")}
    p.ln_maxw = maxw
    if share is not None:
        p.ln_pst1, p.ln_pst2, p.ln_prepA, p.ln_prepB = share
    else:
        p.ln_pst1 = p.psum("ln_pst1", [1, 512])
        p.ln_pst2 = p.psum("ln_pst2", [1, 512])
        p.ln_prepA = p.psum("ln_prepA", [128, 512])
        p.ln_prepB = p.psum("ln_prepB", [128, 512])
    p.step("vector", lambda q: q.memset(p.ones_col[:], 1.0))
    p.step("vector", lambda q: q.memset(p.ones_row[:], 1.0))


def steps_ln(p, src_fn, dst_fn, w, nchunk=8, nfeat=DM, gain=None, bias=None,
             dst2_fn=None):
    sm = p.ln_sm
    x2c = lambda c: p.ln_x2[:, c * p.ln_maxw: c * p.ln_maxw + w]

    def sq(q):
        for c in range(nchunk):
            last = q.activation(out=x2c(c), in_=src_fn(c), func=AF.Square)
        return last
    p.step("scalar", sq)
    p.step("tensor", lambda q: emit_ln_stats(p, q, p.ln_pst1, p.ln_pst2,
                                             src_fn, x2c, w, nchunk))
    steps_ln_smalls(p, p.ln_pst1, p.ln_pst2, w, nfeat, sm)
    p.step("tensor", lambda q: emit_replicate2(p, q, p.ln_prepA, p.ln_prepB, sm, w))

    def mul_(q):
        for c in range(nchunk):
            last = q.tensor_tensor(out=x2c(c), in0=src_fn(c), in1=p.ln_prepA[:, 0:w],
                                   op=ALU.mult)
        return last
    p.step("vector", mul_)

    if gain is None:
        def add_(q):
            for c in range(nchunk):
                last = q.tensor_tensor(out=dst_fn(c), in0=x2c(c),
                                       in1=p.ln_prepB[:, 0:w], op=ALU.add)
            return last
        p.step("vector", add_)
        if dst2_fn is not None:
            def cp2(q):
                for c in range(nchunk):
                    last = q.tensor_copy(out=dst2_fn(c), in_=dst_fn(c))
                return last
            p.step("vector", cp2)
    else:
        def add_(q):
            for c in range(nchunk):
                last = q.tensor_tensor(out=x2c(c), in0=x2c(c),
                                       in1=p.ln_prepB[:, 0:w], op=ALU.add)
            return last
        p.step("vector", add_)

        def gb(q):
            for c in range(nchunk):
                last = q.scalar_tensor_tensor(
                    out=dst_fn(c), in0=x2c(c), scalar=gain[:, c:c + 1],
                    in1=bias[:, c:c + 1].broadcast_to([128, w]),
                    op0=ALU.mult, op1=ALU.add)
            return last
        p.step("vector", gb)
        if dst2_fn is not None:
            def cp2(q):
                for c in range(nchunk):
                    last = q.tensor_copy(out=dst2_fn(c), in_=dst_fn(c))
                return last
            p.step("vector", cp2)


def steps_stream_matmul(p, wdram, KC, MC, rhs_fn, evac_fn, tws, wst, wbf, pss,
                        mgroup=2):
    slabw = 128 * mgroup
    for si in range(0, MC, mgroup):
        r = (si // mgroup) % len(wst)
        p.step("sync", lambda q, si=si, r=r: q.dma_start(
            out=wst[r][:, 0:KC * slabw].rearrange("p (c n) -> p c n", c=KC),
            in_=wdram.rearrange("(c p) n -> p c n", p=128)
            [:, :, si * 128: si * 128 + slabw]), dma=True)
        p.step("scalar", lambda q, r=r: q.copy(out=wbf[r][:, 0:KC * slabw],
                                               in_=wst[r][:, 0:KC * slabw]))
        for mi in range(mgroup):
            m = si + mi
            if m >= MC:
                break
            for (t0, w) in tws:
                pb = pss[(m * len(tws) + tws.index((t0, w))) % len(pss)]

                def mmf(q, m=m, mi=mi, t0=t0, w=w, pb=pb, r=r):
                    for k in range(KC):
                        last = q.matmul(
                            out=pb[:, 0:w],
                            lhsT=wbf[r][:, k * slabw + mi * 128: k * slabw + mi * 128 + 128],
                            rhs=rhs_fn(k, t0, w),
                            start=(k == 0), stop=(k == KC - 1))
                    return last
                p.step("tensor", mmf)
                evac_fn(m, t0, w, pb)



def build_A2a():
    p = Prog("A2a")
    xs = p.din("xs", [DM, TH])
    ysd = p.din("ysd", [DI, TH])
    xcd = p.din("xcd", [DI, TH])
    zd = p.din("zd", [DI, TH])
    dsk = p.din("dsk", [128, 16])
    w_op = p.din("w_op", [DI, DM])
    lng = p.din("lng", [128, 8])
    lnb = p.din("lnb", [128, 8])
    w_rt = p.din("w_rt", [DM, 8])
    ident = p.din("ident", [128, 128])

    hres_o = p.dout("hres", [DM, TH])
    min_o = p.dout("min", [DM, TH], BF16)
    cfm_o = p.dout("cfm", [8, TH])

    hres = p.sbuf("hres_s", [128, 8 * TH])
    xst = p.sbuf("xst", [128, 8 * 256])
    ysb = p.sbuf("ysb", [128, 16 * 256])
    xcb = p.sbuf("xcb", [128, 16 * 256])
    ybf = p.sbuf("ybf", [128, 16 * 256], BF16)
    dskt = p.sbuf("dskt", [128, 16])
    wst = [p.sbuf(f"wst{i}", [128, 16 * 128]) for i in range(1)]
    wbf = [p.sbuf(f"wbf{i}", [128, 16 * 128], BF16) for i in range(1)]
    lg = p.sbuf("lg", [128, 8])
    lb = p.sbuf("lb", [128, 8])
    minb = p.sbuf("minb", [128, 8 * 512], BF16)
    minf = p.sbuf("minf", [128, 8 * 512])
    wrt = p.sbuf("wrt", [128, 8 * 8])
    identt = p.sbuf("identt", [128, 128])
    lgts = p.sbuf("lgts", [8, TH])
    lgtm = p.sbuf("lgtm", [128, 8])
    rmax = p.sbuf("rmax", [128, 1])
    rsum = p.sbuf("rsum", [128, 1])
    rrec = p.sbuf("rrec", [128, 1])
    probs = p.sbuf("probs", [128, 8])
    gtc = p.sbuf("gtc", [128, 64])
    cnt = p.sbuf("cnt", [128, 8])
    ind = p.sbuf("ind", [128, 8])
    wsum = p.sbuf("wsum", [128, 1])
    wrec = p.sbuf("wrec", [128, 1])
    ctm = p.sbuf("ctm", [128, 8])
    cfm = p.sbuf("cfm_t", [8, 128])
    pmm = [p.psum(f"pmm{i}", [128, 512]) for i in range(2)]
    prt = p.psum("prt", [8, 512])
    ptr = p.psum("ptr", [128, 128])
    ln_setup(p)

    TWS = [(t0, 512) for t0 in range(0, TH, 512)]

    def ld(dst, src):
        p.step("sync", lambda q, d=dst, s=src: q.dma_start(out=d, in_=s), dma=True)
    ld(dskt[:], dsk)
    ld(lg[:], lng)
    ld(lb[:], lnb)
    ld(wrt[:].rearrange("p (c n) -> p c n", c=8), w_rt.rearrange("(c p) n -> p c n", p=128))
    ld(identt[:], ident)

    for t0 in range(0, TH, 256):
        p.step("sync", lambda q, t0=t0: q.dma_start(
            out=ysb[:].rearrange("p (c t) -> p c t", c=16),
            in_=ysd.rearrange("(c p) t -> p c t", p=128)[:, :, t0:t0 + 256]), dma=True)
        p.step("sync", lambda q, t0=t0: q.dma_start(
            out=xcb[:].rearrange("p (c t) -> p c t", c=16),
            in_=xcd.rearrange("(c p) t -> p c t", p=128)[:, :, t0:t0 + 256]), dma=True)

        def ytail(q):
            for c in range(16):
                last = q.scalar_tensor_tensor(
                    out=ysb[:, c * 256:(c + 1) * 256],
                    in0=xcb[:, c * 256:(c + 1) * 256], scalar=dskt[:, c:c + 1],
                    in1=ysb[:, c * 256:(c + 1) * 256], op0=ALU.mult, op1=ALU.add)
            return last
        p.step("vector", ytail)
        p.step("sync", lambda q, t0=t0: q.dma_start(
            out=xcb[:].rearrange("p (c t) -> p c t", c=16),
            in_=zd.rearrange("(c p) t -> p c t", p=128)[:, :, t0:t0 + 256]), dma=True)

        def sz(q):
            for c in range(16):
                last = q.activation(out=xcb[:, c * 256:(c + 1) * 256],
                                    in_=xcb[:, c * 256:(c + 1) * 256], func=AF.Silu)
            return last
        p.step("scalar", sz)

        def ymul(q):
            for c in range(16):
                last = q.tensor_tensor(out=ybf[:, c * 256:(c + 1) * 256],
                                       in0=ysb[:, c * 256:(c + 1) * 256],
                                       in1=xcb[:, c * 256:(c + 1) * 256], op=ALU.mult)
            return last
        p.step("vector", ymul)
        p.step("sync", lambda q, t0=t0: q.dma_start(
            out=xst[:].rearrange("p (c t) -> p c t", c=8),
            in_=xs.rearrange("(c p) t -> p c t", p=128)[:, :, t0:t0 + 256]), dma=True)

        def evac(m, tt0, w_, pb, t0=t0):
            def f(q, m=m, pb=pb):
                return q.tensor_tensor(
                    out=hres[:, m * TH + t0: m * TH + t0 + 256],
                    in0=xst[:, m * 256:(m + 1) * 256], in1=pb[:, 0:256], op=ALU.add)
            p.step("vector", f)
        steps_stream_matmul(p, w_op, 16, 8,
                            lambda k, tt0, w_: ybf[:, k * 256: k * 256 + 256],
                            evac, [(0, 256)], wst, wbf, pmm, mgroup=1)

    for (t0, w) in TWS:
        src = lambda c, t0=t0: hres[:, c * TH + t0: c * TH + t0 + 512]
        dstb = lambda c: minb[:, c * 512:(c + 1) * 512]
        dstf = lambda c: minf[:, c * 512:(c + 1) * 512]
        steps_ln(p, src, dstb, 512, gain=lg, bias=lb, dst2_fn=dstf)

        def rt(q, t0=t0):
            for c in range(8):
                last = q.matmul(out=prt[:, 0:512], lhsT=wrt[:, c * 8:(c + 1) * 8],
                                rhs=minf[:, c * 512:(c + 1) * 512],
                                start=(c == 0), stop=(c == 7))
            return last
        p.step("tensor", rt)
        p.step("vector", lambda q, t0=t0: q.tensor_copy(out=lgts[:, t0:t0 + 512],
                                                        in_=prt[:, 0:512]))
        p.step("sync", lambda q, t0=t0: q.dma_start(
            out=min_o.rearrange("(c p) t -> p c t", p=128)[:, :, t0:t0 + 512],
            in_=minb[:].rearrange("p (c t) -> p c t", c=8)), dma=True)
    p.step("sync", lambda q: q.dma_start(
        out=hres_o.rearrange("(c p) t -> p c t", p=128),
        in_=hres[:].rearrange("p (c t) -> p c t", c=8)), dma=True)

    for ti in range(TH // 128):
        t0 = ti * 128
        p.step("tensor", lambda q, t0=t0: q.transpose(
            out=ptr[:, 0:8], in_=lgts[:, t0:t0 + 128], identity=identt[0:8, 0:8]))
        p.step("vector", lambda q: q.tensor_copy(out=lgtm[:], in_=ptr[0:128, 0:8]))
        p.step("vector", lambda q: q.reduce_max(out=rmax[:], in_=lgtm[:], axis=AX.X))
        p.step("vector", lambda q: q.tensor_scalar_mul(out=rmax[:], in0=rmax[:],
                                                       scalar1=-1.0))
        p.step("scalar", lambda q: q.activation(out=probs[:], in_=lgtm[:], func=AF.Exp,
                                                bias=rmax[:], accum_out=rsum[:]))
        p.step("vector", lambda q: q.reciprocal(out=rrec[:], in_=rsum[:]))
        p.step("scalar", lambda q: q.activation(out=probs[:], in_=probs[:],
                                                func=AF.Copy, scale=rrec[:]))
        p.step("vector", lambda q: q.tensor_tensor(
            out=gtc[:].rearrange("p (a b) -> p a b", a=8),
            in0=probs[:].unsqueeze(1).broadcast_to([128, 8, 8]),
            in1=probs[:].unsqueeze(2).broadcast_to([128, 8, 8]), op=ALU.is_gt))
        p.step("vector", lambda q: q.tensor_reduce(
            out=cnt[:], in_=gtc[:].rearrange("p (a b) -> p a b", a=8), axis=AX.X,
            op=ALU.add))
        p.step("vector", lambda q: q.tensor_single_scalar(
            out=ind[:], in_=cnt[:], scalar=1.5, op=ALU.is_lt))
        p.step("vector", lambda q: q.tensor_mul(out=ind[:], in0=ind[:], in1=probs[:]))
        p.step("vector", lambda q: q.tensor_reduce(out=wsum[:], in_=ind[:], axis=AX.X,
                                                   op=ALU.add))
        p.step("vector", lambda q: q.reciprocal(out=wrec[:], in_=wsum[:]))
        p.step("scalar", lambda q: q.activation(out=ctm[:], in_=ind[:], func=AF.Copy,
                                                scale=wrec[:]))
        p.step("tensor", lambda q: q.transpose(out=ptr[0:8, 0:128], in_=ctm[:],
                                               identity=identt[:]))
        p.step("vector", lambda q: q.tensor_copy(out=cfm[:], in_=ptr[0:8, 0:128]))
        p.step("sync", lambda q, t0=t0: q.dma_start(out=cfm_o[:, t0:t0 + 128],
                                                    in_=cfm[:]), dma=True)
    return p.finalize()


def run_A2a(inputs, X, YS, XC, Z):
    nc = _BUILD_CACHE.setdefault("A2a", build_A2a())
    base = {
        "dsk": _chunkvec(inputs["L0_attn_D_skip"], 16),
        "w_op": np.ascontiguousarray(np.asarray(inputs["L0_attn_out_proj_w"], np.float32).T),
        "lng": _chunkvec(inputs["L0_attn_ln_moe_g"], 8),
        "lnb": _chunkvec(inputs["L0_attn_ln_moe_b"], 8),
        "w_rt": np.ascontiguousarray(np.asarray(inputs["L0_attn_router_w"], np.float32).T),
        "ident": np.eye(128, dtype=np.float32),
    }
    in_maps = []
    for c in range(NCORES):
        b, s = c // 2, c % 2
        sl = slice(s * TH, (s + 1) * TH)
        m = dict(base)
        m["xs"] = np.ascontiguousarray(X[b][:, sl]) if X.shape[1] == DM else None
        m["ysd"] = np.ascontiguousarray(YS[b][:, sl])
        m["xcd"] = np.ascontiguousarray(XC[b][:, sl])
        m["zd"] = np.ascontiguousarray(Z[b][:, sl])
        in_maps.append(m)
    res = run_bass_kernel_spmd(nc, in_maps, core_ids=list(range(NCORES)))
    HR = np.empty((B, DM, T), np.float32)
    import ml_dtypes
    MIN = np.empty((B, DM, T), ml_dtypes.bfloat16)
    CFM = np.empty((B, 8, T), np.float32)
    for c in range(NCORES):
        b, s = c // 2, c % 2
        r = res.results[c]
        HR[b, :, s * TH:(s + 1) * TH] = r["hres"]
        MIN[b, :, s * TH:(s + 1) * TH] = r["min"]
        CFM[b, :, s * TH:(s + 1) * TH] = r["cfm"]
    return HR, MIN, CFM



def build_A2b():
    p = Prog("A2b")
    hres_i = p.din("hres", [DM, TH])
    min_i = p.din("min", [DM, TH], BF16)
    cfm_i = p.din("cfm", [8, TH])
    w1e = p.din("w1e", [8, DM, DI])
    b1e = p.din("b1e", [128, 8 * 16])
    w2e = p.din("w2e", [8, DI, DM])
    b2e = p.din("b2e", [128, 8 * 8])
    sel8 = p.din("sel8", [8, 8 * 128])
    xres_i = p.din("xres", [DM, TH])
    xo = p.dout("xo", [DM, TH])

    hres = p.sbuf("hres_s", [128, 8 * TH])
    minb = p.sbuf("minb", [128, 8 * TH], BF16)
    cfmt = p.sbuf("cfmt", [8, TH])
    selt = p.sbuf("selt", [8, 8 * 128])
    crep = p.sbuf("crep", [128, TH])
    h1cb = p.sbuf("h1cb", [128, 16 * TH], BF16)
    tmp5 = p.sbuf("tmp5", [128, 512])
    b1t = p.sbuf("b1t", [128, 8 * 16])
    b2t = p.sbuf("b2t", [128, 8 * 8])
    wst = [p.sbuf("wst0", [128, 16 * 128])]
    wbf = [p.sbuf("wbf0", [128, 16 * 128], BF16)]
    pmm = [p.psum(f"pmm{i}", [128, 512]) for i in range(2)]
    prep_c = p.psum("prep_c", [128, 512])

    def ld(dst, src):
        p.step("sync", lambda q, d=dst, s=src: q.dma_start(out=d, in_=s), dma=True)
    ld(hres[:].rearrange("p (c t) -> p c t", c=8),
       hres_i.rearrange("(c p) t -> p c t", p=128))
    ld(minb[:].rearrange("p (c t) -> p c t", c=8),
       min_i.rearrange("(c p) t -> p c t", p=128))
    ld(cfmt[:], cfm_i)
    ld(selt[:], sel8)
    for c8 in range(8):
        for tc in range(TH // 512):
            p.step("sync", lambda q, c8=c8, tc=tc: q.dma_start(
                out=tmp5[:],
                in_=xres_i.rearrange("(c p) t -> p c t", p=128)
                [:, c8, tc * 512:(tc + 1) * 512]), dma=True)
            p.step("vector", lambda q, c8=c8, tc=tc: q.tensor_tensor(
                out=hres[:, c8 * TH + tc * 512: c8 * TH + (tc + 1) * 512],
                in0=hres[:, c8 * TH + tc * 512: c8 * TH + (tc + 1) * 512],
                in1=tmp5[:], op=ALU.add))
    ld(b1t[:], b1e)
    ld(b2t[:], b2e)

    for e in range(8):
        for tc in range(TH // 512):
            p.step("tensor", lambda q, e=e, tc=tc: q.matmul(
                out=prep_c[:], lhsT=selt[:, e * 128:(e + 1) * 128],
                rhs=cfmt[:, tc * 512:(tc + 1) * 512], start=True, stop=True))
            p.step("vector", lambda q, tc=tc: q.tensor_copy(
                out=crep[:, tc * 512:(tc + 1) * 512], in_=prep_c[:]))

        def ev1(m, t0, w, pb, e=e):
            p.step("scalar", lambda q, m=m, pb=pb, e=e: q.activation(
                out=tmp5[:], in_=pb[:, 0:512], func=AF.Silu,
                bias=b1t[:, e * 16 + m: e * 16 + m + 1]))
            p.step("vector", lambda q, m=m, t0=t0: q.tensor_tensor(
                out=h1cb[:, m * TH + t0: m * TH + t0 + 512], in0=tmp5[:],
                in1=crep[:, t0:t0 + 512], op=ALU.mult))
        steps_stream_matmul(
            p, w1e[e], 8, 16,
            lambda k, t0, w: minb[:, k * TH + t0: k * TH + t0 + 512],
            ev1, [(t0, 512) for t0 in range(0, TH, 512)], wst, wbf, pmm, mgroup=1)

        def ev2(m, t0, w, pb, e=e):
            p.step("vector", lambda q, m=m, t0=t0, e=e: q.tensor_scalar(
                out=tmp5[:], in0=crep[:, t0:t0 + 512],
                scalar1=b2t[:, e * 8 + m: e * 8 + m + 1], scalar2=None, op0=ALU.mult))
            p.step("vector", lambda q, m=m, t0=t0, pb=pb: q.tensor_tensor(
                out=tmp5[:], in0=tmp5[:], in1=pb[:, 0:512], op=ALU.add))
            p.step("vector", lambda q, m=m, t0=t0: q.tensor_tensor(
                out=hres[:, m * TH + t0: m * TH + t0 + 512],
                in0=hres[:, m * TH + t0: m * TH + t0 + 512], in1=tmp5[:], op=ALU.add))
        steps_stream_matmul(
            p, w2e[e], 16, 8,
            lambda k, t0, w: h1cb[:, k * TH + t0: k * TH + t0 + 512],
            ev2, [(t0, 512) for t0 in range(0, TH, 512)], wst, wbf, pmm, mgroup=1)

    p.step("sync", lambda q: q.dma_start(
        out=xo.rearrange("(c p) t -> p c t", p=128),
        in_=hres[:].rearrange("p (c t) -> p c t", c=8)), dma=True)
    return p.finalize()


def run_A2b(inputs, HR, MIN, CFM, XRES):
    nc = _BUILD_CACHE.setdefault("A2b", build_A2b())
    w1 = np.asarray(inputs["L0_attn_w1"], np.float32)
    w2 = np.asarray(inputs["L0_attn_w2"], np.float32)
    b1 = np.asarray(inputs["L0_attn_b1"], np.float32)
    b2 = np.asarray(inputs["L0_attn_b2"], np.float32)
    sel8 = np.zeros((8, 8 * 128), np.float32)
    for e in range(8):
        sel8[e, e * 128:(e + 1) * 128] = 1.0
    base = {
        "w1e": np.ascontiguousarray(w1.transpose(0, 2, 1)),
        "w2e": np.ascontiguousarray(w2.transpose(0, 2, 1)),
        "b1e": np.ascontiguousarray(
            b1.reshape(8, 16, 128).transpose(2, 0, 1).reshape(128, 128)),
        "b2e": np.ascontiguousarray(
            b2.reshape(8, 8, 128).transpose(2, 0, 1).reshape(128, 64)),
        "sel8": sel8,
    }
    in_maps = []
    for c in range(NCORES):
        b, s = c // 2, c % 2
        sl = slice(s * TH, (s + 1) * TH)
        m = dict(base)
        m["hres"] = np.ascontiguousarray(HR[b][:, sl])
        m["min"] = np.ascontiguousarray(MIN[b][:, sl])
        m["cfm"] = np.ascontiguousarray(CFM[b][:, sl])
        m["xres"] = np.ascontiguousarray(XRES[b][:, sl])
        in_maps.append(m)
    res = run_bass_kernel_spmd(nc, in_maps, core_ids=list(range(NCORES)))
    XO = np.empty((B, DM, T), np.float32)
    for c in range(NCORES):
        b, s = c // 2, c % 2
        XO[b, :, s * TH:(s + 1) * TH] = res.results[c]["xo"]
    return XO



C_GELU = 0.7978845608028654
SQ_GELU = 0.21146087051789464


def build_F0():
    p = Prog("F0")
    xin = p.din("xin", [DM, TH])
    w_rt = p.din("w_rt", [DM, 8])
    w1 = p.din("w1", [DM, 4 * DM])
    b1c = p.din("b1c", [128, 32])
    w2 = p.din("w2", [4 * DM, DM])
    b2c = p.din("b2c", [128, 8])
    ident = p.din("ident", [128, 128])
    xo = p.dout("xo", [DM, TH])

    hres = p.sbuf("hres_s", [128, 8 * TH])
    ubf = p.sbuf("ubf", [128, 8 * 512], BF16)
    uf = p.sbuf("uf", [128, 8 * 512])
    h1cb = p.sbuf("h1cb", [128, 32 * 512], BF16)
    gx = p.sbuf("gx", [128, 512])
    gt = p.sbuf("gt", [128, 512])
    srow = p.sbuf("srow", [1, 512])
    srep = p.sbuf("srep", [128, 512])
    wrt = p.sbuf("wrt", [128, 8 * 8])
    b1t = p.sbuf("b1t", [128, 32])
    b2t = p.sbuf("b2t", [128, 8])
    identt = p.sbuf("identt", [128, 128])
    lgts = p.sbuf("lgts", [8, 512])
    lgtm = p.sbuf("lgtm", [128, 8])
    rmax = p.sbuf("rmax", [128, 1])
    rsum = p.sbuf("rsum", [128, 1])
    rrec = p.sbuf("rrec", [128, 1])
    probs = p.sbuf("probs", [128, 8])
    gtc = p.sbuf("gtc", [128, 64])
    cnt = p.sbuf("cnt", [128, 8])
    ind = p.sbuf("ind", [128, 8])
    wsum = p.sbuf("wsum", [128, 1])
    wrec = p.sbuf("wrec", [128, 1])
    stm = p.sbuf("stm", [128, 1])
    wst = [p.sbuf("wst0", [128, 32 * 128])]
    wbf = [p.sbuf("wbf0", [128, 32 * 128], BF16)]
    pmm = [p.psum(f"pmm{i}", [128, 512]) for i in range(2)]
    prt = p.psum("prt", [8, 512])
    ptr = p.psum("ptr", [128, 128])
    ln_setup(p)

    def ld(dst, src):
        p.step("sync", lambda q, d=dst, s=src: q.dma_start(out=d, in_=s), dma=True)
    ld(hres[:].rearrange("p (c t) -> p c t", c=8),
       xin.rearrange("(c p) t -> p c t", p=128))
    ld(wrt[:].rearrange("p (c n) -> p c n", c=8), w_rt.rearrange("(c p) n -> p c n", p=128))
    ld(b1t[:], b1c)
    ld(b2t[:], b2c)
    ld(identt[:], ident)

    for qtr in range(4):
        t0 = qtr * 512
        src_fn = lambda c, t0=t0: hres[:, c * TH + t0: c * TH + t0 + 512]
        dst_fn = lambda c: uf[:, c * 512:(c + 1) * 512]
        steps_ln(p, src_fn, dst_fn, 512)

        def castu(q):
            for c in range(8):
                last = q.tensor_copy(out=ubf[:, c * 512:(c + 1) * 512],
                                     in_=uf[:, c * 512:(c + 1) * 512])
            return last
        p.step("vector", castu)

        def rt(q):
            for c in range(8):
                last = q.matmul(out=prt[:, 0:512], lhsT=wrt[:, c * 8:(c + 1) * 8],
                                rhs=uf[:, c * 512:(c + 1) * 512],
                                start=(c == 0), stop=(c == 7))
            return last
        p.step("tensor", rt)
        p.step("vector", lambda q: q.tensor_copy(out=lgts[:], in_=prt[:, 0:512]))

        for ti in range(4):
            tt = ti * 128
            p.step("tensor", lambda q, tt=tt: q.transpose(
                out=ptr[:, 0:8], in_=lgts[:, tt:tt + 128], identity=identt[0:8, 0:8]))
            p.step("vector", lambda q: q.tensor_copy(out=lgtm[:], in_=ptr[0:128, 0:8]))
            p.step("vector", lambda q: q.reduce_max(out=rmax[:], in_=lgtm[:], axis=AX.X))
            p.step("vector", lambda q: q.tensor_scalar_mul(out=rmax[:], in0=rmax[:],
                                                           scalar1=-1.0))
            p.step("scalar", lambda q: q.activation(out=probs[:], in_=lgtm[:],
                                                    func=AF.Exp, bias=rmax[:],
                                                    accum_out=rsum[:]))
            p.step("vector", lambda q: q.reciprocal(out=rrec[:], in_=rsum[:]))
            p.step("scalar", lambda q: q.activation(out=probs[:], in_=probs[:],
                                                    func=AF.Copy, scale=rrec[:]))
            p.step("vector", lambda q: q.tensor_tensor(
                out=gtc[:].rearrange("p (a b) -> p a b", a=8),
                in0=probs[:].unsqueeze(1).broadcast_to([128, 8, 8]),
                in1=probs[:].unsqueeze(2).broadcast_to([128, 8, 8]), op=ALU.is_gt))
            p.step("vector", lambda q: q.tensor_reduce(
                out=cnt[:], in_=gtc[:].rearrange("p (a b) -> p a b", a=8), axis=AX.X,
                op=ALU.add))
            p.step("vector", lambda q: q.tensor_single_scalar(
                out=ind[:], in_=cnt[:], scalar=1.5, op=ALU.is_lt))
            p.step("vector", lambda q: q.tensor_mul(out=ind[:], in0=ind[:], in1=probs[:]))
            p.step("vector", lambda q: q.tensor_reduce(out=wsum[:], in_=ind[:],
                                                       axis=AX.X, op=ALU.add))
            p.step("vector", lambda q: q.reciprocal(out=wrec[:], in_=wsum[:]))
            p.step("vector", lambda q: q.tensor_mul(out=stm[:], in0=wsum[:], in1=wrec[:]))
            p.step("tensor", lambda q: q.transpose(out=ptr[0:1, 0:128], in_=stm[:],
                                                   identity=identt[:]))
            p.step("vector", lambda q, tt=tt: q.tensor_copy(out=srow[0:1, tt:tt + 128],
                                                            in_=ptr[0:1, 0:128]))
        p.step("tensor", lambda q: q.matmul(out=ptr[:, 0:128] if False else pmm[0][:, 0:512],
                                            lhsT=p.ones_row[0:1, :], rhs=srow[0:1, 0:512],
                                            start=True, stop=True))
        p.step("vector", lambda q: q.tensor_copy(out=srep[:], in_=pmm[0][:, 0:512]))

        def ev1(m, tt0, w, pb):
            p.step("vector", lambda q, m=m, pb=pb: q.tensor_tensor(
                out=gx[:], in0=pb[:, 0:512],
                in1=b1t[:, m:m + 1].broadcast_to([128, 512]), op=ALU.add))
            p.step("scalar", lambda q: q.activation(out=gt[:], in_=gx[:],
                                                    func=AF.Square, scale=SQ_GELU))
            p.step("vector", lambda q: q.scalar_tensor_tensor(
                out=gt[:], in0=gt[:], scalar=1.0, in1=gx[:], op0=ALU.add, op1=ALU.mult))
            p.step("scalar", lambda q: q.activation(out=gt[:], in_=gt[:],
                                                    func=AF.Sigmoid, scale=2 * C_GELU))
            p.step("vector", lambda q, m=m: q.tensor_mul(
                out=h1cb[:, m * 512:(m + 1) * 512], in0=gt[:], in1=gx[:]))
        steps_stream_matmul(p, w1, 8, 32,
                            lambda k, tt0, w: ubf[:, k * 512: k * 512 + 512],
                            ev1, [(0, 512)], wst, wbf, pmm, mgroup=1)

        def ev2(m, tt0, w, pb, t0=t0):
            p.step("vector", lambda q, m=m, pb=pb: q.tensor_tensor(
                out=gx[:], in0=pb[:, 0:512],
                in1=b2t[:, m:m + 1].broadcast_to([128, 512]), op=ALU.add))
            p.step("vector", lambda q: q.tensor_mul(out=gx[:], in0=gx[:], in1=srep[:]))
            p.step("vector", lambda q, m=m, t0=t0: q.tensor_tensor(
                out=hres[:, m * TH + t0: m * TH + t0 + 512],
                in0=hres[:, m * TH + t0: m * TH + t0 + 512], in1=gx[:], op=ALU.add))
        steps_stream_matmul(p, w2, 32, 8,
                            lambda k, tt0, w: h1cb[:, k * 512: k * 512 + 512],
                            ev2, [(0, 512)], wst, wbf, pmm, mgroup=1)

    p.step("sync", lambda q: q.dma_start(
        out=xo.rearrange("(c p) t -> p c t", p=128),
        in_=hres[:].rearrange("p (c t) -> p c t", c=8)), dma=True)
    return p.finalize()


def run_F0(inputs, X1):
    nc = _BUILD_CACHE.setdefault("F0", build_F0())
    base = {
        "w_rt": np.ascontiguousarray(np.asarray(inputs["L0_ffn_router_w"], np.float32)),
        "w1": np.ascontiguousarray(np.asarray(inputs["L0_ffn_w1"], np.float32)),
        "b1c": _chunkvec(inputs["L0_ffn_b1"], 32),
        "w2": np.ascontiguousarray(np.asarray(inputs["L0_ffn_w2"], np.float32)),
        "b2c": _chunkvec(inputs["L0_ffn_b2"], 8),
        "ident": np.eye(128, dtype=np.float32),
    }
    in_maps = []
    for c in range(NCORES):
        b, s = c // 2, c % 2
        m = dict(base)
        m["xin"] = np.ascontiguousarray(X1[b][:, s * TH:(s + 1) * TH])
        in_maps.append(m)
    res = run_bass_kernel_spmd(nc, in_maps, core_ids=list(range(NCORES)))
    XO = np.empty((B, DM, T), np.float32)
    for c in range(NCORES):
        b, s = c // 2, c % 2
        XO[b, :, s * TH:(s + 1) * TH] = res.results[c]["xo"]
    return XO



def build_B0():
    p = Prog("B0")
    xin = p.din("xin", [DM, TH])
    wqkv = p.din("wqkv", [DM, 3 * DM])
    bqkv = p.din("bqkv", [128, 24])
    wg = p.din("wg", [128, 8])
    bg = p.din("bg", [1, 1])
    qkv_o = p.dout("qkv", [3 * DM, TH], BF16)
    gate_o = p.dout("gate", [1, TH])

    hres = p.sbuf("hres_s", [128, 8 * TH])
    ubf = p.sbuf("ubf", [128, 8 * TH], BF16)
    uf = p.sbuf("uf", [128, 8 * 512])
    bq = p.sbuf("bq", [128, 24])
    wgt = p.sbuf("wgt", [128, 8])
    bgt = p.sbuf("bgt", [1, 1])
    ev = p.sbuf("ev", [128, 512], BF16)
    gat = p.sbuf("gat", [1, TH])
    wst = [p.sbuf("wst0", [128, 8 * 128])]
    wbf = [p.sbuf("wbf0", [128, 8 * 128], BF16)]
    pmm = [p.psum(f"pmm{i}", [128, 512]) for i in range(2)]
    pg = p.psum("pg", [1, 512])
    ln_setup(p)

    def ld(dst, src):
        p.step("sync", lambda q, d=dst, s=src: q.dma_start(out=d, in_=s), dma=True)
    ld(hres[:].rearrange("p (c t) -> p c t", c=8),
       xin.rearrange("(c p) t -> p c t", p=128))
    ld(bq[:], bqkv)
    ld(wgt[:], wg)
    ld(bgt[:], bg)

    TWS = [(t0, 512) for t0 in range(0, TH, 512)]
    for (t0, w) in TWS:
        src_fn = lambda c, t0=t0: hres[:, c * TH + t0: c * TH + t0 + 512]
        dst_fn = lambda c: uf[:, c * 512:(c + 1) * 512]
        steps_ln(p, src_fn, dst_fn, 512)

        def castu(q, t0=t0):
            for c in range(8):
                last = q.tensor_copy(out=ubf[:, c * TH + t0: c * TH + t0 + 512],
                                     in_=uf[:, c * 512:(c + 1) * 512])
            return last
        p.step("vector", castu)
        def gmm(q):
            for c in range(8):
                last = q.matmul(out=pg[:, 0:512], lhsT=wgt[:, c:c + 1],
                                rhs=uf[:, c * 512:(c + 1) * 512],
                                start=(c == 0), stop=(c == 7))
            return last
        p.step("tensor", gmm)
        p.step("scalar", lambda q, t0=t0: q.activation(
            out=gat[0:1, t0:t0 + 512], in_=pg[0:1, 0:512], func=AF.Sigmoid,
            bias=bgt[0:1, 0:1]))
    p.step("sync", lambda q: q.dma_start(out=gate_o, in_=gat[:]), dma=True)

    def ev1(m, t0, w, pb):
        p.step("vector", lambda q, m=m, pb=pb: q.tensor_tensor(
            out=ev[:], in0=pb[:, 0:512],
            in1=bq[:, m:m + 1].broadcast_to([128, 512]), op=ALU.add))
        p.step("sync", lambda q, m=m, t0=t0: q.dma_start(
            out=qkv_o.rearrange("(c p) t -> p c t", p=128)[:, m, t0:t0 + 512],
            in_=ev[:]), dma=True)
    steps_stream_matmul(p, wqkv, 8, 24,
                        lambda k, t0, w: ubf[:, k * TH + t0: k * TH + t0 + 512],
                        ev1, TWS, wst, wbf, pmm, mgroup=1)
    return p.finalize()


def run_B0(inputs, X2):
    nc = _BUILD_CACHE.setdefault("B0", build_B0())
    base = {
        "wqkv": np.ascontiguousarray(np.asarray(inputs["L1_attn_Wqkv"], np.float32)),
        "bqkv": _chunkvec(inputs["L1_attn_bqkv"], 24),
        "wg": _chunkvec(np.asarray(inputs["L1_attn_Wg"], np.float32)[:, 0], 8),
        "bg": np.asarray(inputs["L1_attn_bg"], np.float32).reshape(1, 1),
    }
    in_maps = []
    for c in range(NCORES):
        b, s = c // 2, c % 2
        m = dict(base)
        m["xin"] = np.ascontiguousarray(X2[b][:, s * TH:(s + 1) * TH])
        in_maps.append(m)
    res = run_bass_kernel_spmd(nc, in_maps, core_ids=list(range(NCORES)))
    import ml_dtypes
    QKV = np.empty((B, 3 * DM, T), ml_dtypes.bfloat16)
    GATE = np.empty((B, 1, T), np.float32)
    for c in range(NCORES):
        b, s = c // 2, c % 2
        QKV[b, :, s * TH:(s + 1) * TH] = res.results[c]["qkv"]
        GATE[b, :, s * TH:(s + 1) * TH] = res.results[c]["gate"]
    return QKV, GATE



def build_B1():
    p = Prog("B1")
    NH_ = 8
    NQB = T // 128
    SCALE = 0.125
    QF = p.din("QF", [NH_, 64, T], BF16)
    KF = p.din("KF", [NH_, 64, T], BF16)
    VT = p.din("VT", [NH_, T, 64], BF16)
    GT = p.din("GT", [1, T])
    mA2 = p.din("mA2", [128, 2 * 128])
    mB2 = p.din("mB2", [128, 2 * 128])
    mFull = p.din("mFull", [128, 128])
    ident = p.din("ident", [128, 128])
    augr = p.din("augr", [2, 64], BF16)
    at_o = p.dout("at", [NH_, 64, T], BF16)

    qf = p.sbuf("qf", [64, T], BF16)
    kaug = p.sbuf("kaug", [65, T], BF16)
    qaug = p.sbuf("qaug", [66, T], BF16)
    kgat = p.sbuf("kgat", [66, 64], BF16)
    vt = p.sbuf("vt", [128, 32 * 64], BF16)
    att = p.sbuf("att", [64, T], BF16)
    grep = p.sbuf("grep", [64, T])
    gtr = p.sbuf("gtr", [1, T])
    sa = p.sbuf("sa", [128, 640])
    sb = p.sbuf("sb", [128, 640])
    pt = p.sbuf("pt", [128, 5 * 128], BF16)
    ptg = p.sbuf("ptg", [64, 128], BF16)
    mcol = p.sbuf("mcol", [128, 32])
    mgcol = p.sbuf("mgcol", [128, 32])
    nm32 = p.sbuf("nm32", [32, 128], BF16)
    nmg32 = p.sbuf("nmg32", [32, 128], BF16)
    m32f = p.sbuf("m32f", [32, 128])
    srec = p.sbuf("srec", [1, 128])
    sgrec = p.sbuf("sgrec", [1, 128])
    rl = p.sbuf("rl", [64, 128])
    rg = p.sbuf("rg", [64, 128])
    ol = p.sbuf("ol", [64, 128])
    og = p.sbuf("og", [64, 128])
    dd = p.sbuf("dd", [64, 128])
    mskA = p.sbuf("mskA", [128, 2 * 128])
    mskB = p.sbuf("mskB", [128, 2 * 128])
    mskF = p.sbuf("mskF", [128, 128])
    onesb = p.sbuf("onesb", [128, 1], BF16)
    identt = p.sbuf("identt", [128, 128])
    p.ones_row = p.sbuf("ones_row", [1, 128])
    psA = p.psum("psA", [128, 640])
    psS = p.psum("psS", [1, 128])
    psSg = p.psum("psSg", [1, 128])
    psO = p.psum("psO", [64, 128])
    psOg = p.psum("psOg", [64, 128])
    psR = p.psum("psR", [64, 128])
    pstr = p.psum("pstr", [32, 128])

    p.step("vector", lambda q: q.memset(onesb[:], 1.0))
    p.step("vector", lambda q: q.memset(p.ones_row[:], 1.0))
    p.step("vector", lambda q: q.memset(qaug[:], 0.0))
    p.step("vector", lambda q: q.memset(kgat[:], 0.0))

    def ld(dst, src):
        p.step("sync", lambda q, d=dst, s=src: q.dma_start(out=d, in_=s), dma=True)
    ld(mskA[:], mA2)
    ld(mskB[:], mB2)
    ld(mskF[:], mFull)
    ld(identt[:], ident)
    ld(gtr[:], GT)
    for tc in range(T // 512):
        p.step("tensor", lambda q, tc=tc: q.matmul(
            out=psA[0:64, 0:512], lhsT=p.ones_row[0:1, 0:64],
            rhs=gtr[0:1, tc * 512:(tc + 1) * 512], start=True, stop=True))
        p.step("vector", lambda q, tc=tc: q.tensor_copy(
            out=grep[:, tc * 512:(tc + 1) * 512], in_=psA[0:64, 0:512]))

    for h in range(NH_):
        ld(qf[:], QF[h])
        ld(kaug[0:64, :], KF[h])
        p.step("vector", lambda q: q.memset(kaug[64:65, :], 1.0))
        ld(vt[:].rearrange("p (c n) -> p c n", c=32),
           VT[h].rearrange("(c p) n -> p c n", p=128))
        ld(qaug[0:64, :], QF[h])
        p.step("vector", lambda q: q.tensor_copy(out=kgat[0:64, :], in_=kaug[0:64, 0:64]))
        p.step("sync", lambda q: q.dma_start(out=kgat[64:66, :], in_=augr), dma=True)

        for qb in range(NQB):
            kbs = [min(max(qb - 2 + j, 0), NQB - 1) for j in range(5)]

            def mmA(q, qb=qb, kbs=kbs):
                for j, kb in enumerate(kbs):
                    last = q.matmul(out=psA[:, j * 128:(j + 1) * 128],
                                    lhsT=qf[:, qb * 128:(qb + 1) * 128],
                                    rhs=kaug[0:64, kb * 128:(kb + 1) * 128],
                                    start=True, stop=True)
                return last
            p.step("tensor", mmA)

            def mskev(q, qb=qb, kbs=kbs):
                for j, kb in enumerate(kbs):
                    if kb != qb - 2 + j:
                        msk = mskF[:]
                    elif j == 0:
                        msk = mskA[:, 0:128]
                    elif j == 4:
                        msk = mskA[:, 128:256]
                    else:
                        msk = None
                    if msk is None:
                        last = q.tensor_copy(out=sa[:, j * 128:(j + 1) * 128],
                                             in_=psA[:, j * 128:(j + 1) * 128])
                    else:
                        last = q.tensor_tensor(out=sa[:, j * 128:(j + 1) * 128],
                                               in0=psA[:, j * 128:(j + 1) * 128],
                                               in1=msk, op=ALU.add)
                return last
            p.step("vector", mskev)
            p.step("vector", lambda q, qb=qb: q.reduce_max(
                out=mcol[:, qb:qb + 1], in_=sa[:], axis=AX.X))
            p.step("tensor", lambda q, qb=qb: q.matmul(
                out=psA[:, 0:64], lhsT=qf[:, qb * 128:(qb + 1) * 128],
                rhs=kgat[0:64, 0:64], start=True, stop=True))
            p.step("vector", lambda q, qb=qb: q.reduce_max(
                out=mgcol[:, qb:qb + 1], in_=psA[:, 0:64], axis=AX.X))

        p.step("tensor", lambda q: q.transpose(out=pstr[:], in_=mcol[:],
                                               identity=identt[:]))
        p.step("vector", lambda q: q.tensor_copy(out=m32f[:], in_=pstr[:]))
        p.step("vector", lambda q: q.tensor_scalar_mul(out=nm32[:], in0=m32f[:],
                                                       scalar1=-1.0))
        p.step("tensor", lambda q: q.transpose(out=pstr[:], in_=mgcol[:],
                                               identity=identt[:]))
        p.step("vector", lambda q: q.tensor_copy(out=m32f[:], in_=pstr[:]))
        p.step("vector", lambda q: q.tensor_scalar_mul(out=nmg32[:], in0=m32f[:],
                                                       scalar1=-1.0))
        p.step("sync", lambda q: q.dma_start(
            out=qaug[64:65, :].rearrange("p (b t) -> p b t", b=32),
            in_=nm32[:].unsqueeze(1)), dma=True)
        p.step("sync", lambda q: q.dma_start(
            out=qaug[65:66, :].rearrange("p (b t) -> p b t", b=32),
            in_=nmg32[:].unsqueeze(1)), dma=True)

        for qb in range(NQB):
            kbs = [min(max(qb - 2 + j, 0), NQB - 1) for j in range(5)]

            def mmB(q, qb=qb, kbs=kbs):
                for j, kb in enumerate(kbs):
                    last = q.matmul(out=psA[:, j * 128:(j + 1) * 128],
                                    lhsT=kaug[0:65, kb * 128:(kb + 1) * 128],
                                    rhs=qaug[0:65, qb * 128:(qb + 1) * 128],
                                    start=True, stop=True)
                return last
            p.step("tensor", mmB)

            masks = []
            for j, kb in enumerate(kbs):
                if kb != qb - 2 + j:
                    masks.append(mskF[:])
                elif j == 0:
                    masks.append(mskB[:, 0:128])
                elif j == 4:
                    masks.append(mskB[:, 128:256])
                else:
                    masks.append(None)

            def maskst(q, masks=masks):
                last = None
                for j, msk in enumerate(masks):
                    if msk is not None:
                        last = q.tensor_tensor(
                            out=sb[:, j * 128:(j + 1) * 128],
                            in0=psA[:, j * 128:(j + 1) * 128], in1=msk, op=ALU.add)
                return last
            if any(m is not None for m in masks):
                p.step("vector", maskst)

            def exps(q, masks=masks):
                last = None
                runs = []
                j = 0
                while j < 5:
                    if masks[j] is None:
                        j2 = j
                        while j2 < 5 and masks[j2] is None:
                            j2 += 1
                        runs.append((j, j2, False))
                        j = j2
                    else:
                        j2 = j
                        while j2 < 5 and masks[j2] is not None:
                            j2 += 1
                        runs.append((j, j2, True))
                        j = j2
                for (a, b2, masked) in runs:
                    srcb = sb if masked else psA
                    last = q.activation(out=pt[:, a * 128: b2 * 128],
                                        in_=srcb[:, a * 128: b2 * 128],
                                        func=AF.Exp, scale=SCALE)
                return last
            p.step("scalar", exps)

            def pvs(q, kbs=kbs):
                for j in range(5):
                    q.matmul(out=psS[:], lhsT=onesb[:, 0:1],
                             rhs=pt[:, j * 128:(j + 1) * 128],
                             start=(j == 0), stop=(j == 4))
                for j, kb in enumerate(kbs):
                    last = q.matmul(out=psO[:], lhsT=vt[:, kb * 64:(kb + 1) * 64],
                                    rhs=pt[:, j * 128:(j + 1) * 128],
                                    start=(j == 0), stop=(j == 4))
                return last
            p.step("tensor", pvs)
            p.step("vector", lambda q: q.reciprocal(out=srec[:], in_=psS[0:1, :]))

            def mmBg(q, qb=qb):
                return q.matmul(out=psA[:, 512:640][0:64, :],
                                lhsT=kgat[:, :], rhs=qaug[:, qb * 128:(qb + 1) * 128],
                                start=True, stop=True)
            p.step("tensor", mmBg)
            p.step("scalar", lambda q: q.activation(out=ptg[:], in_=psA[0:64, 512:640],
                                                    func=AF.Exp, scale=SCALE))
            p.step("tensor", lambda q: q.matmul(out=psSg[:], lhsT=onesb[0:64, 0:1],
                                                rhs=ptg[:], start=True, stop=True))
            p.step("vector", lambda q: q.reciprocal(out=sgrec[:], in_=psSg[0:1, :]))
            p.step("tensor", lambda q: q.matmul(out=psOg[:], lhsT=vt[0:64, 0:64],
                                                rhs=ptg[:], start=True, stop=True))

            p.step("tensor", lambda q: q.matmul(out=psR[:], lhsT=p.ones_row[0:1, 0:64],
                                                rhs=srec[:], start=True, stop=True))
            p.step("vector", lambda q: q.tensor_copy(out=rl[:], in_=psR[:]))
            p.step("tensor", lambda q: q.matmul(out=psR[:], lhsT=p.ones_row[0:1, 0:64],
                                                rhs=sgrec[:], start=True, stop=True))
            p.step("vector", lambda q: q.tensor_copy(out=rg[:], in_=psR[:]))
            p.step("vector", lambda q: q.tensor_mul(out=ol[:], in0=psO[:], in1=rl[:]))
            p.step("vector", lambda q: q.tensor_mul(out=og[:], in0=psOg[:], in1=rg[:]))
            p.step("vector", lambda q: q.tensor_sub(out=dd[:], in0=og[:], in1=ol[:]))
            p.step("vector", lambda q, qb=qb: q.tensor_mul(
                out=dd[:], in0=dd[:], in1=grep[:, qb * 128:(qb + 1) * 128]))
            p.step("vector", lambda q, qb=qb: q.tensor_tensor(
                out=att[:, qb * 128:(qb + 1) * 128], in0=ol[:], in1=dd[:], op=ALU.add))

        p.step("sync", lambda q, h=h: q.dma_start(out=at_o[h], in_=att[:]), dma=True)
    return p.finalize()


def run_B1(inputs, QKV, GATE):
    nc = _BUILD_CACHE.setdefault("B1", build_B1())
    import ml_dtypes
    i0, i1 = np.tril_indices(128, -1)
    mA = np.zeros((128, 256), np.float32)
    m_j0 = np.zeros((128, 128), np.float32)
    m_j0[i0, i1] = -1e30
    m_j4 = np.ascontiguousarray(m_j0.T)
    mA[:, 0:128] = m_j0
    mA[:, 128:256] = m_j4
    mB = np.zeros((128, 256), np.float32)
    mB[:, 0:128] = m_j0.T
    mB[:, 128:256] = m_j4.T
    base = {
        "mA2": mA, "mB2": mB,
        "mFull": np.full((128, 128), -1e30, np.float32),
        "ident": np.eye(128, dtype=np.float32),
        "augr": np.concatenate([np.zeros((1, 64)), np.ones((1, 64))]).astype(ml_dtypes.bfloat16),
    }
    in_maps = []
    for c in range(NCORES):
        b, hh = c // 2, c % 2
        heads = range(hh * 8, hh * 8 + 8)
        qf = np.stack([QKV[b, h * 64:(h + 1) * 64, :] for h in heads])
        kf = np.stack([QKV[b, DM + h * 64: DM + (h + 1) * 64, :] for h in heads])
        vtm = np.stack([np.ascontiguousarray(
            QKV[b, 2 * DM + h * 64: 2 * DM + (h + 1) * 64, :].T) for h in heads])
        m = dict(base)
        m["QF"] = np.ascontiguousarray(qf)
        m["KF"] = np.ascontiguousarray(kf)
        m["VT"] = vtm
        m["GT"] = np.ascontiguousarray(GATE[b])
        in_maps.append(m)
    res = run_bass_kernel_spmd(nc, in_maps, core_ids=list(range(NCORES)))
    import ml_dtypes
    ATT = np.empty((B, DM, T), ml_dtypes.bfloat16)
    for c in range(NCORES):
        b, hh = c // 2, c % 2
        for j, h in enumerate(range(hh * 8, hh * 8 + 8)):
            ATT[b, h * 64:(h + 1) * 64, :] = res.results[c]["at"][j]
    return ATT



def build_B2():
    p = Prog("B2")
    xin = p.din("xin", [DM, TH])
    at_i = p.din("at", [DM, TH], BF16)
    wout = p.din("wout", [DM, DM])
    bout = p.din("bout", [128, 8])
    xo = p.dout("xo", [DM, TH])

    hres = p.sbuf("hres_s", [128, 8 * TH])
    atb = p.sbuf("atb", [128, 8 * TH], BF16)
    bo = p.sbuf("bo", [128, 8])
    wst = [p.sbuf("wst0", [128, 8 * 128])]
    wbf = [p.sbuf("wbf0", [128, 8 * 128], BF16)]
    pmm = [p.psum(f"pmm{i}", [128, 512]) for i in range(2)]

    def ld(dst, src):
        p.step("sync", lambda q, d=dst, s=src: q.dma_start(out=d, in_=s), dma=True)
    ld(hres[:].rearrange("p (c t) -> p c t", c=8), xin.rearrange("(c p) t -> p c t", p=128))
    ld(atb[:].rearrange("p (c t) -> p c t", c=8), at_i.rearrange("(c p) t -> p c t", p=128))
    ld(bo[:], bout)

    def ev(m, t0, w, pb):
        p.step("vector", lambda q, m=m, t0=t0, pb=pb: q.scalar_tensor_tensor(
            out=hres[:, m * TH + t0: m * TH + t0 + 512], in0=pb[:, 0:512],
            scalar=bo[:, m:m + 1], in1=hres[:, m * TH + t0: m * TH + t0 + 512],
            op0=ALU.add, op1=ALU.add))
    steps_stream_matmul(p, wout, 8, 8,
                        lambda k, t0, w: atb[:, k * TH + t0: k * TH + t0 + 512],
                        ev, [(t0, 512) for t0 in range(0, TH, 512)],
                        wst, wbf, pmm, mgroup=1)
    p.step("sync", lambda q: q.dma_start(
        out=xo.rearrange("(c p) t -> p c t", p=128),
        in_=hres[:].rearrange("p (c t) -> p c t", c=8)), dma=True)
    return p.finalize()


def run_B2(inputs, X2, ATT):
    nc = _BUILD_CACHE.setdefault("B2", build_B2())
    base = {
        "wout": np.ascontiguousarray(np.asarray(inputs["L1_attn_Wout"], np.float32)),
        "bout": _chunkvec(inputs["L1_attn_bout"], 8),
    }
    import ml_dtypes
    in_maps = []
    for c in range(NCORES):
        b, s = c // 2, c % 2
        m = dict(base)
        m["xin"] = np.ascontiguousarray(X2[b][:, s * TH:(s + 1) * TH])
        m["at"] = np.ascontiguousarray(np.asarray(ATT[b][:, s * TH:(s + 1) * TH],
                                                  ml_dtypes.bfloat16))
        in_maps.append(m)
    res = run_bass_kernel_spmd(nc, in_maps, core_ids=list(range(NCORES)))
    XO = np.empty((B, DM, T), np.float32)
    for c in range(NCORES):
        b, s = c // 2, c % 2
        XO[b, :, s * TH:(s + 1) * TH] = res.results[c]["xo"]
    return XO



def build_C0():
    p = Prog("C0")
    xin = p.din("xin", [DM, TH])
    w_gt = p.din("w_gt", [DM, 16])
    gbrep = p.din("gbrep", [128, 16])
    w13 = p.din("w13", [16, DM, DM])
    w2e = p.din("w2e", [16, 512, DM])
    w13s = p.din("w13s", [DM, DM])
    w2s = p.din("w2s", [512, DM])
    sel16 = p.din("sel16", [16, 16 * 128])
    ident = p.din("ident", [128, 128])
    xo = p.dout("xo", [DM, TH])

    hres = p.sbuf("hres_s", [128, 8 * TH])
    ubf = p.sbuf("ubf", [128, 8 * TH], BF16)
    uf = p.sbuf("uf", [128, 8 * 256])
    gsb = p.sbuf("gsb", [128, 4 * TH], BF16)
    h1cb = p.sbuf("h1cb", [128, 4 * TH], BF16)
    crep = p.sbuf("crep", [128, TH])
    scfm = p.sbuf("scfm", [16, TH])
    cfm = p.sbuf("cfm", [16, TH])
    selt = p.sbuf("selt", [16, 16 * 128])
    gbt = p.sbuf("gbt", [128, 16])
    identt = p.sbuf("identt", [128, 128])
    stm = p.sbuf("stm", [128, 16])
    btm = p.sbuf("btm", [128, 16])
    gtc = p.sbuf("gtc", [128, 256])
    cnt = p.sbuf("cnt", [128, 16])
    ind = p.sbuf("ind", [128, 16])
    wsum = p.sbuf("wsum", [128, 1])
    wrec = p.sbuf("wrec", [128, 1])
    ctm = p.sbuf("ctm", [128, 16])
    tmp5 = p.sbuf("tmp5", [128, 512])
    wst = [p.sbuf("wst0", [128, 8 * 128])]
    wbf = [p.sbuf("wbf0", [128, 8 * 128], BF16)]
    pmm = [p.psum(f"pmm{i}", [128, 512]) for i in range(2)]
    prt = p.psum("prt", [16, 512])
    prc = p.psum("prc", [128, 512])
    ptr = prc
    ln_setup(p, maxw=256)

    def ld(dst, src):
        p.step("sync", lambda q, d=dst, s=src: q.dma_start(out=d, in_=s), dma=True)
    ld(hres[:].rearrange("p (c t) -> p c t", c=8), xin.rearrange("(c p) t -> p c t", p=128))
    ld(selt[:], sel16)
    ld(gbt[:], gbrep)
    ld(identt[:], ident)
    wgt = p.sbuf("wgt", [128, 8 * 16])
    ld(wgt[:].rearrange("p (c n) -> p c n", c=8), w_gt.rearrange("(c p) n -> p c n", p=128))

    for (t0, w) in [(t0, 256) for t0 in range(0, TH, 256)]:
        src_fn = lambda c, t0=t0: hres[:, c * TH + t0: c * TH + t0 + 256]
        dst_fn = lambda c: uf[:, c * 256:(c + 1) * 256]
        steps_ln(p, src_fn, dst_fn, 256)

        def castu(q, t0=t0):
            for c in range(8):
                last = q.tensor_copy(out=ubf[:, c * TH + t0: c * TH + t0 + 256],
                                     in_=uf[:, c * 256:(c + 1) * 256])
            return last
        p.step("vector", castu)

        def gmm(q):
            for c in range(8):
                last = q.matmul(out=prt[:, 0:256], lhsT=wgt[:, c * 16:(c + 1) * 16],
                                rhs=uf[:, c * 256:(c + 1) * 256],
                                start=(c == 0), stop=(c == 7))
            return last
        p.step("tensor", gmm)
        p.step("scalar", lambda q, t0=t0: q.activation(
            out=scfm[:, t0:t0 + 256], in_=prt[0:16, 0:256], func=AF.Sigmoid))

    for ti in range(TH // 128):
        t0 = ti * 128
        p.step("tensor", lambda q, t0=t0: q.transpose(
            out=ptr[:, 0:16], in_=scfm[:, t0:t0 + 128], identity=identt[0:16, 0:16]))
        p.step("vector", lambda q: q.tensor_copy(out=stm[:], in_=ptr[0:128, 0:16]))
        p.step("vector", lambda q: q.tensor_add(out=btm[:], in0=stm[:], in1=gbt[:]))
        p.step("vector", lambda q: q.tensor_tensor(
            out=gtc[:].rearrange("p (a b) -> p a b", a=16),
            in0=btm[:].unsqueeze(1).broadcast_to([128, 16, 16]),
            in1=btm[:].unsqueeze(2).broadcast_to([128, 16, 16]), op=ALU.is_gt))
        p.step("vector", lambda q: q.tensor_reduce(
            out=cnt[:], in_=gtc[:].rearrange("p (a b) -> p a b", a=16), axis=AX.X,
            op=ALU.add))
        p.step("vector", lambda q: q.tensor_single_scalar(
            out=ind[:], in_=cnt[:], scalar=3.5, op=ALU.is_lt))
        p.step("vector", lambda q: q.tensor_mul(out=ind[:], in0=ind[:], in1=stm[:]))
        p.step("vector", lambda q: q.tensor_reduce(out=wsum[:], in_=ind[:],
                                                   axis=AX.X, op=ALU.add))
        p.step("vector", lambda q: q.reciprocal(out=wrec[:], in_=wsum[:]))
        p.step("scalar", lambda q: q.activation(out=ctm[:], in_=ind[:], func=AF.Copy,
                                                scale=wrec[:]))
        p.step("vector", lambda q: q.tensor_scalar_mul(out=ctm[:], in0=ctm[:],
                                                       scalar1=2.5))
        p.step("tensor", lambda q: q.transpose(out=ptr[0:16, 0:128], in_=ctm[:],
                                               identity=identt[:]))
        p.step("vector", lambda q, t0=t0: q.tensor_copy(out=cfm[:, t0:t0 + 128],
                                                        in_=ptr[0:16, 0:128]))

    TWS = [(t0, 512) for t0 in range(0, TH, 512)]
    for e in range(17):
        shared = (e == 16)
        if not shared:
            for tc in range(TH // 512):
                p.step("tensor", lambda q, e=e, tc=tc: q.matmul(
                    out=prc[:], lhsT=selt[:, e * 128:(e + 1) * 128],
                    rhs=cfm[:, tc * 512:(tc + 1) * 512], start=True, stop=True))
                p.step("vector", lambda q, tc=tc: q.tensor_copy(
                    out=crep[:, tc * 512:(tc + 1) * 512], in_=prc[:]))

        wsrc = w13s if shared else w13[e]

        def ev1(m, t0, w, pb, shared=shared):
            if m < 4:
                p.step("scalar", lambda q, m=m, t0=t0, pb=pb: q.activation(
                    out=gsb[:, m * TH + t0: m * TH + t0 + 512], in_=pb[:, 0:512],
                    func=AF.Silu))
            else:
                if shared:
                    p.step("vector", lambda q, m=m, t0=t0, pb=pb: q.tensor_tensor(
                        out=h1cb[:, (m - 4) * TH + t0: (m - 4) * TH + t0 + 512],
                        in0=gsb[:, (m - 4) * TH + t0: (m - 4) * TH + t0 + 512],
                        in1=pb[:, 0:512], op=ALU.mult))
                else:
                    p.step("vector", lambda q, m=m, t0=t0, pb=pb: q.tensor_tensor(
                        out=tmp5[:], in0=gsb[:, (m - 4) * TH + t0: (m - 4) * TH + t0 + 512],
                        in1=pb[:, 0:512], op=ALU.mult))
                    p.step("vector", lambda q, m=m, t0=t0: q.tensor_tensor(
                        out=h1cb[:, (m - 4) * TH + t0: (m - 4) * TH + t0 + 512],
                        in0=tmp5[:], in1=crep[:, t0:t0 + 512], op=ALU.mult))
        steps_stream_matmul(p, wsrc, 8, 8,
                            lambda k, t0, w: ubf[:, k * TH + t0: k * TH + t0 + 512],
                            ev1, TWS, wst, wbf, pmm, mgroup=1)

        wsrc2 = w2s if shared else w2e[e]

        def ev2(m, t0, w, pb):
            p.step("vector", lambda q, m=m, t0=t0, pb=pb: q.tensor_tensor(
                out=hres[:, m * TH + t0: m * TH + t0 + 512],
                in0=hres[:, m * TH + t0: m * TH + t0 + 512], in1=pb[:, 0:512],
                op=ALU.add))
        steps_stream_matmul(p, wsrc2, 4, 8,
                            lambda k, t0, w: h1cb[:, k * TH + t0: k * TH + t0 + 512],
                            ev2, TWS, wst, wbf, pmm, mgroup=1)

    p.step("sync", lambda q: q.dma_start(
        out=xo.rearrange("(c p) t -> p c t", p=128),
        in_=hres[:].rearrange("p (c t) -> p c t", c=8)), dma=True)
    return p.finalize()


def run_C0(inputs, X3):
    nc = _BUILD_CACHE.setdefault("C0", build_C0())
    w13 = np.asarray(inputs["L1_ffn_w13"], np.float32)
    w2 = np.asarray(inputs["L1_ffn_w2"], np.float32)
    sel16 = np.zeros((16, 16 * 128), np.float32)
    for e in range(16):
        sel16[e, e * 128:(e + 1) * 128] = 1.0
    base = {
        "w_gt": np.ascontiguousarray(np.asarray(inputs["L1_ffn_gate_w"], np.float32).T),
        "gbrep": np.ascontiguousarray(np.broadcast_to(
            np.asarray(inputs["L1_ffn_gate_b"], np.float32), (128, 16))),
        "w13": np.ascontiguousarray(w13.transpose(0, 2, 1)),
        "w2e": np.ascontiguousarray(w2.transpose(0, 2, 1)),
        "w13s": np.ascontiguousarray(np.asarray(inputs["L1_ffn_w13_s"], np.float32).T),
        "w2s": np.ascontiguousarray(np.asarray(inputs["L1_ffn_w2_s"], np.float32).T),
        "sel16": sel16,
        "ident": np.eye(128, dtype=np.float32),
    }
    in_maps = []
    for c in range(NCORES):
        b, s = c // 2, c % 2
        m = dict(base)
        m["xin"] = np.ascontiguousarray(X3[b][:, s * TH:(s + 1) * TH])
        in_maps.append(m)
    res = run_bass_kernel_spmd(nc, in_maps, core_ids=list(range(NCORES)))
    XO = np.empty((B, DM, T), np.float32)
    for c in range(NCORES):
        b, s = c // 2, c % 2
        XO[b, :, s * TH:(s + 1) * TH] = res.results[c]["xo"]
    return XO



def build_D0a():
    p = Prog("D0a")
    TE2 = TH + 256
    NTI = TE2 // 128
    xin = p.din("xin", [DM, TE2])
    w_ret = p.din("w_ret", [DM, DM])
    brep = p.din("brep", [128, DM])
    ltb = p.din("ltb", [128, 128], BF16)
    dcol = p.din("dcol", [128, 1], BF16)
    hmask = p.din("hmask", [128, 2])
    ret_o = p.dout("ret", [DM, TH])

    uf = p.sbuf("uf", [128, 8 * 512])
    ubf = p.sbuf("ubf", [128, 8 * TE2], BF16)
    sigf = p.sbuf("sigf", [128, 1024])
    sigb = p.sbuf("sigb", [128, 1024], BF16)
    stater = p.sbuf("stater", [1, 1024])
    bret = p.sbuf("bret", [128, DM])
    ltbt = p.sbuf("ltbt", [128, 128], BF16)
    dcolt = p.sbuf("dcolt", [128, 1], BF16)
    hmt = p.sbuf("hmt", [128, 2])
    retf = p.sbuf("retf", [128, 8 * 128])
    wst = [p.sbuf("wst0", [128, 8 * 128])]
    wbf = [p.sbuf("wbf0", [128, 8 * 128], BF16)]
    psg = [p.psum(f"psg{i}", [128, 512]) for i in range(2)]
    psr = p.psum("psr", [128, 128])
    ln_setup(p)

    def ld(dst, src):
        p.step("sync", lambda q, d=dst, s=src: q.dma_start(out=d, in_=s), dma=True)
    ld(bret[:], brep)
    ld(ltbt[:], ltb)
    ld(dcolt[:], dcol)
    ld(hmt[:], hmask)
    p.step("vector", lambda q: q.memset(stater[:], 0.0))

    for t0 in range(0, TE2, 512):
        w = min(512, TE2 - t0)
        src_fn = lambda c, t0=t0, w=w: None
        p.step("sync", lambda q, t0=t0, w=w: q.dma_start(
            out=uf[:].rearrange("p (c t) -> p c t", c=8)[:, :, 0:w],
            in_=xin.rearrange("(c p) t -> p c t", p=128)[:, :, t0:t0 + w]), dma=True)
        srcf = lambda c, w=w: uf[:, c * 512: c * 512 + w]
        dstf = lambda c, t0=t0, w=w: ubf[:, c * TE2 + t0: c * TE2 + t0 + w]
        steps_ln(p, srcf, dstf, w)

    wrb = p.sbuf("wrb", [128, 8 * DM], BF16)
    for c8 in range(8):
        p.step("sync", lambda q, c8=c8: q.dma_start(
            out=wst[0][:].rearrange("p (a n) -> p a n", a=1),
            in_=w_ret.rearrange("(c p) n -> p c n", p=128)[:, c8:c8 + 1, :]), dma=True)
        p.step("scalar", lambda q, c8=c8: q.copy(
            out=wrb[:, c8 * DM:(c8 + 1) * DM], in_=wst[0][:, 0:DM]))

    for ti in range(NTI):
        t0 = ti * 128
        for half in range(2):
            def mmsig(q, t0=t0, half=half):
                for c in range(8):
                    last = q.matmul(
                        out=psg[half][:, 0:512],
                        lhsT=ubf[:, c * TE2 + t0: c * TE2 + t0 + 128],
                        rhs=wrb[:, c * DM + half * 512: c * DM + half * 512 + 512],
                        start=(c == 0), stop=(c == 7))
                return last
            p.step("tensor", mmsig)
            p.step("vector", lambda q, half=half: q.tensor_tensor(
                out=sigf[:, half * 512:(half + 1) * 512], in0=psg[half][:, 0:512],
                in1=bret[:, half * 512:(half + 1) * 512], op=ALU.add))
        p.step("scalar", lambda q: q.activation(out=sigf[:], in_=sigf[:],
                                                func=AF.Sigmoid))
        if ti < 2:
            p.step("vector", lambda q, ti=ti: q.tensor_scalar(
                out=sigf[:], in0=sigf[:], scalar1=hmt[:, ti:ti + 1], scalar2=None,
                op0=ALU.mult))
        p.step("vector", lambda q: q.tensor_tensor(
            out=sigf[0:1, :], in0=sigf[0:1, :], in1=stater[:], op=ALU.add))
        p.step("vector", lambda q: q.tensor_copy(out=sigb[:], in_=sigf[:]))
        for fc in range(8):
            p.step("tensor", lambda q, fc=fc: q.matmul(
                out=psr[:], lhsT=sigb[:, fc * 128:(fc + 1) * 128], rhs=ltbt[:],
                start=True, stop=True))
            p.step("vector", lambda q, fc=fc: q.tensor_copy(
                out=retf[:, fc * 128:(fc + 1) * 128], in_=psr[:]))
        def mmst(q):
            q.matmul(out=psg[0][0:1, 0:512], lhsT=dcolt[:], rhs=sigb[:, 0:512],
                     start=True, stop=True)
            return q.matmul(out=psg[1][0:1, 0:512], lhsT=dcolt[:],
                            rhs=sigb[:, 512:1024], start=True, stop=True)
        p.step("tensor", mmst)
        p.step("vector", lambda q: q.tensor_copy(out=stater[0:1, 0:512],
                                                 in_=psg[0][0:1, 0:512]))
        p.step("vector", lambda q: q.tensor_copy(out=stater[0:1, 512:1024],
                                                 in_=psg[1][0:1, 0:512]))
        if ti >= 2:
            p.step("sync", lambda q, ti=ti: q.dma_start(
                out=ret_o.rearrange("(c p) t -> p c t", p=128)
                [:, :, (ti - 2) * 128:(ti - 1) * 128],
                in_=retf[:].rearrange("p (c t) -> p c t", c=8)), dma=True)
    return p.finalize()


def run_D0a(inputs, X4):
    nc = _BUILD_CACHE.setdefault("D0a", build_D0a())
    lt = np.zeros((128, 128), np.float32)
    for s_ in range(128):
        for t_ in range(s_, 128):
            lt[s_, t_] = 0.5 ** (t_ - s_ + 1)
    import ml_dtypes
    dcol = (0.5 ** (128 - np.arange(128, dtype=np.float32))).reshape(128, 1)
    base = {
        "w_ret": np.ascontiguousarray(np.asarray(inputs["L2_attn_W_ret"], np.float32)),
        "brep": np.ascontiguousarray(np.broadcast_to(
            np.asarray(inputs["L2_attn_b_ret"], np.float32), (128, DM))),
        "ltb": lt.astype(ml_dtypes.bfloat16),
        "dcol": dcol.astype(ml_dtypes.bfloat16),
    }
    in_maps = []
    for c in range(NCORES):
        b, s = c // 2, c % 2
        m = dict(base)
        if s == 0:
            seg = np.concatenate([np.zeros((DM, 256), np.float32),
                                  X4[b][:, 0:TH]], axis=1)
            m["hmask"] = np.zeros((128, 2), np.float32)
        else:
            seg = X4[b][:, TH - 256: T]
            m["hmask"] = np.ones((128, 2), np.float32)
        m["xin"] = np.ascontiguousarray(seg)
        in_maps.append(m)
    res = run_bass_kernel_spmd(nc, in_maps, core_ids=list(range(NCORES)))
    RET = np.empty((B, DM, T), np.float32)
    for c in range(NCORES):
        b, s = c // 2, c % 2
        RET[b, :, s * TH:(s + 1) * TH] = res.results[c]["ret"]
    return RET



def build_D0b():
    p = Prog("D0b")
    xin = p.din("xin", [DM, TH])
    ret_i = p.din("ret", [DM, TH])
    l1g = p.din("l1g", [128, 8])
    l1b = p.din("l1b", [128, 8])
    w1 = p.din("w1", [DM, 4 * DM])
    b1c = p.din("b1c", [128, 32])
    w2 = p.din("w2", [4 * DM, DM])
    b2c = p.din("b2c", [128, 8])
    l2g = p.din("l2g", [128, 8])
    l2b = p.din("l2b", [128, 8])
    xo = p.dout("xo", [DM, TH])

    xt4 = p.sbuf("xt4", [128, 8 * 512])
    h2f = p.sbuf("h2f", [128, 8 * TH])
    h2b = p.sbuf("h2b", [128, 8 * 512], BF16)
    uf = p.sbuf("uf", [128, 8 * 512])
    h1cb = p.sbuf("h1cb", [128, 32 * 512], BF16)
    gx = p.sbuf("gx", [128, 512])
    gt = p.sbuf("gt", [128, 512])
    lg1 = p.sbuf("lg1", [128, 8])
    lb1 = p.sbuf("lb1", [128, 8])
    lg2 = p.sbuf("lg2", [128, 8])
    lb2 = p.sbuf("lb2", [128, 8])
    b1t = p.sbuf("b1t", [128, 32])
    b2t = p.sbuf("b2t", [128, 8])
    wst = [p.sbuf("wst0", [128, 32 * 128])]
    wbf = [p.sbuf("wbf0", [128, 32 * 128], BF16)]
    pmm = [p.psum(f"pmm{i}", [128, 512]) for i in range(2)]
    ln_setup(p)

    def ld(dst, src):
        p.step("sync", lambda q, d=dst, s=src: q.dma_start(out=d, in_=s), dma=True)
    ld(lg1[:], l1g)
    ld(lb1[:], l1b)
    ld(lg2[:], l2g)
    ld(lb2[:], l2b)
    ld(b1t[:], b1c)
    ld(b2t[:], b2c)

    for t0 in range(0, TH, 512):
        p.step("sync", lambda q, t0=t0: q.dma_start(
            out=xt4[:].rearrange("p (c t) -> p c t", c=8),
            in_=xin.rearrange("(c p) t -> p c t", p=128)[:, :, t0:t0 + 512]), dma=True)
        srcf = lambda c: xt4[:, c * 512:(c + 1) * 512]
        dstf = lambda c: uf[:, c * 512:(c + 1) * 512]
        steps_ln(p, srcf, dstf, 512)
        def addret(q, t0=t0):
            return q.dma_start(
                out=p.ln_x2[:].rearrange("p (c t) -> p c t", c=8),
                in_=ret_i.rearrange("(c p) t -> p c t", p=128)[:, :, t0:t0 + 512])
        p.step("sync", addret, dma=True)

        def vsum(q, t0=t0):
            for c in range(8):
                last = q.tensor_tensor(out=uf[:, c * 512:(c + 1) * 512],
                                       in0=uf[:, c * 512:(c + 1) * 512],
                                       in1=p.ln_x2[:, c * 512:(c + 1) * 512],
                                       op=ALU.add)
            return last
        p.step("vector", vsum)
        srcf2 = lambda c: uf[:, c * 512:(c + 1) * 512]
        dstf2 = lambda c, t0=t0: h2f[:, c * TH + t0: c * TH + t0 + 512]
        steps_ln(p, srcf2, dstf2, 512, gain=lg1, bias=lb1)

    for qtr in range(4):
        t0 = qtr * 512

        def casth(q, t0=t0):
            for c in range(8):
                last = q.tensor_copy(out=h2b[:, c * 512:(c + 1) * 512],
                                     in_=h2f[:, c * TH + t0: c * TH + t0 + 512])
            return last
        p.step("vector", casth)

        def ev1(m, tt0, w, pb):
            p.step("vector", lambda q, m=m, pb=pb: q.tensor_tensor(
                out=gx[:], in0=pb[:, 0:512],
                in1=b1t[:, m:m + 1].broadcast_to([128, 512]), op=ALU.add))
            p.step("scalar", lambda q: q.activation(out=gt[:], in_=gx[:],
                                                    func=AF.Square, scale=SQ_GELU))
            p.step("vector", lambda q: q.scalar_tensor_tensor(
                out=gt[:], in0=gt[:], scalar=1.0, in1=gx[:], op0=ALU.add, op1=ALU.mult))
            p.step("scalar", lambda q: q.activation(out=gt[:], in_=gt[:],
                                                    func=AF.Sigmoid, scale=2 * C_GELU))
            p.step("vector", lambda q, m=m: q.tensor_mul(
                out=h1cb[:, m * 512:(m + 1) * 512], in0=gt[:], in1=gx[:]))
        steps_stream_matmul(p, w1, 8, 32,
                            lambda k, tt0, w: h2b[:, k * 512: k * 512 + 512],
                            ev1, [(0, 512)], wst, wbf, pmm, mgroup=1)

        def ev2(m, tt0, w, pb, t0=t0):
            p.step("vector", lambda q, m=m, t0=t0, pb=pb: q.scalar_tensor_tensor(
                out=h2f[:, m * TH + t0: m * TH + t0 + 512], in0=pb[:, 0:512],
                scalar=b2t[:, m:m + 1], in1=h2f[:, m * TH + t0: m * TH + t0 + 512],
                op0=ALU.add, op1=ALU.add))
        steps_stream_matmul(p, w2, 32, 8,
                            lambda k, tt0, w: h1cb[:, k * 512: k * 512 + 512],
                            ev2, [(0, 512)], wst, wbf, pmm, mgroup=1)

    for t0 in range(0, TH, 512):
        srcf = lambda c, t0=t0: h2f[:, c * TH + t0: c * TH + t0 + 512]
        dstf = lambda c: uf[:, c * 512:(c + 1) * 512]
        steps_ln(p, srcf, dstf, 512, gain=lg2, bias=lb2)
        p.step("sync", lambda q, t0=t0: q.dma_start(
            out=xt4[:].rearrange("p (c t) -> p c t", c=8),
            in_=xin.rearrange("(c p) t -> p c t", p=128)[:, :, t0:t0 + 512]), dma=True)

        def fin(q, t0=t0):
            for c in range(8):
                last = q.tensor_tensor(out=xt4[:, c * 512:(c + 1) * 512],
                                       in0=xt4[:, c * 512:(c + 1) * 512],
                                       in1=uf[:, c * 512:(c + 1) * 512], op=ALU.add)
            return last
        p.step("vector", fin)
        p.step("sync", lambda q, t0=t0: q.dma_start(
            out=xo.rearrange("(c p) t -> p c t", p=128)[:, :, t0:t0 + 512],
            in_=xt4[:].rearrange("p (c t) -> p c t", c=8)), dma=True)
    return p.finalize()


def run_D0b(inputs, X4, RET):
    nc = _BUILD_CACHE.setdefault("D0b", build_D0b())
    base = {
        "l1g": _chunkvec(inputs["L2_attn_ln1_scale"], 8),
        "l1b": _chunkvec(inputs["L2_attn_ln1_bias"], 8),
        "w1": np.ascontiguousarray(np.asarray(inputs["L2_attn_W1"], np.float32)),
        "b1c": _chunkvec(inputs["L2_attn_b1"], 32),
        "w2": np.ascontiguousarray(np.asarray(inputs["L2_attn_W2"], np.float32)),
        "b2c": _chunkvec(inputs["L2_attn_b2"], 8),
        "l2g": _chunkvec(inputs["L2_attn_ln2_scale"], 8),
        "l2b": _chunkvec(inputs["L2_attn_ln2_bias"], 8),
    }
    in_maps = []
    for c in range(NCORES):
        b, s = c // 2, c % 2
        m = dict(base)
        m["xin"] = np.ascontiguousarray(X4[b][:, s * TH:(s + 1) * TH])
        m["ret"] = np.ascontiguousarray(RET[b][:, s * TH:(s + 1) * TH])
        in_maps.append(m)
    res = run_bass_kernel_spmd(nc, in_maps, core_ids=list(range(NCORES)))
    XO = np.empty((B, DM, T), np.float32)
    for c in range(NCORES):
        b, s = c // 2, c % 2
        XO[b, :, s * TH:(s + 1) * TH] = res.results[c]["xo"]
    return XO



def build_E0():
    p = Prog("E0")
    xin = p.din("xin", [DM, TH])
    wsim = p.din("wsim", [DM, 16])
    gates = p.din("gates", [1, 16])
    wq_all = p.din("wq_all", [DM, DM])
    wk_all = p.din("wk_all", [DM, DM])
    wv_all = p.din("wv_all", [DM, DM])
    pat = p.din("pat", [16, 8 * 128])
    fpat = p.din("fpat", [128, 64], BF16)
    ident = p.din("ident", [128, 128])
    qm_o = p.dout("qm", [64, TH], BF16)
    km_o = p.dout("km", [64, TH], BF16)
    vm_o = p.dout("vm", [64, TH], BF16)
    rw_o = p.dout("rw", [16, TH])

    vbf = p.sbuf("vbf", [128, 8 * TH], BF16)
    uf = p.sbuf("uf", [128, 8 * 512])
    acc = p.sbuf("acc", [64, TH])
    accb = p.sbuf("accb", [64, TH], BF16)
    rwr = p.sbuf("rwr", [128, 512])
    lgts = p.sbuf("lgts", [16, TH])
    rwf = p.sbuf("rwf", [16, TH])
    wsimt = p.sbuf("wsimt", [128, 8 * 16])
    wsq = p.sbuf("wsq", [128, 8 * 16])
    patt = p.sbuf("patt", [16, 8 * 128])
    identt = p.sbuf("identt", [128, 128])
    ssrow = p.sbuf("ssrow", [1, 512])
    rn = p.sbuf("rn", [1, 512])
    rsimr = p.sbuf("rsimr", [1, 16])
    rsimc = p.sbuf("rsimc", [16, 1])
    sgr = p.sbuf("sgr", [1, 16])
    sgc = p.sbuf("sgc", [16, 1])
    lgf = p.sbuf("lgf", [16, 512])
    lgtm = p.sbuf("lgtm", [128, 16])
    pos = p.sbuf("pos", [128, 16])
    gated = p.sbuf("gated", [128, 16])
    expg = p.sbuf("expg", [128, 16])
    num = p.sbuf("num", [128, 16])
    den = p.sbuf("den", [128, 1])
    inact = p.sbuf("inact", [128, 1])
    rac = p.sbuf("rac", [128, 1])
    rwact = p.sbuf("rwact", [128, 16])
    gtc = p.sbuf("gtc", [128, 256])
    cnt = p.sbuf("cnt", [128, 16])
    ind2 = p.sbuf("ind2", [128, 16])
    rwtm = p.sbuf("rwtm", [128, 16])
    tmp6 = p.sbuf("tmp6", [128, 512], BF16)
    fpt = p.sbuf("fpt", [128, 64], BF16)
    wst = [p.sbuf("wst0", [128, 8 * 128])]
    wbf = [p.sbuf("wbf0", [128, 8 * 128], BF16)]
    pmm0 = p.psum("pmm0", [128, 512])
    pmm = [pmm0, pmm0]
    prt = p.psum("prt", [16, 512])
    prw = p.psum("prw", [128, 512])
    psX = [p.psum(f"psX{i}", [128, 512]) for i in range(4)]
    ln_setup(p, share=(psX[0][0:1, 0:512], psX[1][0:1, 0:512], psX[2], psX[3]))

    def ld(dst, src):
        p.step("sync", lambda q, d=dst, s=src: q.dma_start(out=d, in_=s), dma=True)
    ld(wsimt[:].rearrange("p (c n) -> p c n", c=8),
       wsim.rearrange("(c p) n -> p c n", p=128))
    ld(patt[:], pat)
    ld(fpt[:], fpat)
    ld(identt[:], ident)
    ld(sgr[:], gates)
    p.step("scalar", lambda q: q.activation(out=sgr[:], in_=sgr[:], func=AF.Sigmoid))
    p.step("tensor", lambda q: q.transpose(out=prt[0:16, 0:1], in_=sgr[:],
                                           identity=identt[0:1, 0:1]))
    p.step("vector", lambda q: q.tensor_copy(out=sgc[:], in_=prt[0:16, 0:1]))
    p.step("scalar", lambda q: q.activation(out=wsq[:], in_=wsimt[:], func=AF.Square))
    def simn(q):
        for c in range(8):
            last = q.matmul(out=prt[0:1, 0:16], lhsT=p.ones_col[:, 0:1],
                            rhs=wsq[:, c * 16:(c + 1) * 16],
                            start=(c == 0), stop=(c == 7))
        return last
    p.step("tensor", simn)
    p.step("scalar", lambda q: q.activation(out=rsimr[:], in_=prt[0:1, 0:16], func=AF.Ln))
    p.step("scalar", lambda q: q.activation(out=rsimr[:], in_=rsimr[:], func=AF.Exp,
                                            scale=0.5))
    p.step("vector", lambda q: q.tensor_scalar_max(out=rsimr[:], in0=rsimr[:],
                                                   scalar1=1e-12))
    p.step("vector", lambda q: q.reciprocal(out=rsimr[:], in_=rsimr[:]))
    p.step("tensor", lambda q: q.transpose(out=prt[0:16, 0:1], in_=rsimr[:],
                                           identity=identt[0:1, 0:1]))
    p.step("vector", lambda q: q.tensor_copy(out=rsimc[:], in_=prt[0:16, 0:1]))

    uf2 = p.sbuf("uf2", [128, 8 * 512])
    for t0 in range(0, TH, 512):
        p.step("sync", lambda q, t0=t0: q.dma_start(
            out=uf[:].rearrange("p (c t) -> p c t", c=8),
            in_=xin.rearrange("(c p) t -> p c t", p=128)[:, :, t0:t0 + 512]), dma=True)
        srcf = lambda c: uf[:, c * 512:(c + 1) * 512]
        dstf = lambda c: uf2[:, c * 512:(c + 1) * 512]
        steps_ln(p, srcf, dstf, 512)

        def castv(q, t0=t0):
            for c in range(8):
                last = q.tensor_copy(out=vbf[:, c * TH + t0: c * TH + t0 + 512],
                                     in_=uf2[:, c * 512:(c + 1) * 512])
            return last
        p.step("vector", castv)
        def sqv(q):
            for c in range(8):
                last = q.activation(out=uf[:, c * 512:(c + 1) * 512],
                                    in_=uf2[:, c * 512:(c + 1) * 512], func=AF.Square)
            return last
        p.step("scalar", sqv)

        def ssum(q):
            for c in range(8):
                last = q.matmul(out=prw[0:1, 0:512], lhsT=p.ones_col[:, 0:1],
                                rhs=uf[:, c * 512:(c + 1) * 512],
                                start=(c == 0), stop=(c == 7))
            return last
        p.step("tensor", ssum)
        p.step("vector", lambda q: q.tensor_copy(out=ssrow[:], in_=prw[0:1, 0:512]))
        p.step("scalar", lambda q: q.activation(out=rn[:], in_=ssrow[:], func=AF.Ln))
        p.step("scalar", lambda q: q.activation(out=rn[:], in_=rn[:], func=AF.Exp,
                                                scale=0.5))
        p.step("vector", lambda q: q.tensor_scalar_max(out=rn[:], in0=rn[:],
                                                       scalar1=1e-12))
        p.step("vector", lambda q: q.reciprocal(out=rn[:], in_=rn[:]))
        def lmm(q):
            for c in range(8):
                last = q.matmul(out=prt[:, 0:512], lhsT=wsimt[:, c * 16:(c + 1) * 16],
                                rhs=uf2[:, c * 512:(c + 1) * 512],
                                start=(c == 0), stop=(c == 7))
            return last
        p.step("tensor", lmm)
        p.step("vector", lambda q: q.tensor_copy(out=lgf[:], in_=prt[0:16, 0:512]))
        p.step("tensor", lambda q: q.matmul(out=prw[0:16, 0:512],
                                            lhsT=p.ones_row[0:1, 0:16], rhs=rn[:],
                                            start=True, stop=True))
        p.step("vector", lambda q: q.tensor_tensor(out=lgf[:], in0=lgf[:],
                                                   in1=prw[0:16, 0:512], op=ALU.mult))
        p.step("vector", lambda q, t0=t0: q.scalar_tensor_tensor(
            out=lgts[:, t0:t0 + 512], in0=lgf[:], scalar=rsimc[:],
            in1=sgc[:].broadcast_to([16, 512]), op0=ALU.mult, op1=ALU.subtract))

    for ti in range(TH // 128):
        t0 = ti * 128
        p.step("tensor", lambda q, t0=t0: q.transpose(
            out=prw[:, 0:16], in_=lgts[:, t0:t0 + 128], identity=identt[0:16, 0:16]))
        p.step("vector", lambda q: q.tensor_copy(out=lgtm[:], in_=prw[0:128, 0:16]))
        p.step("vector", lambda q: q.tensor_single_scalar(out=pos[:], in_=lgtm[:],
                                                          scalar=0.0, op=ALU.is_gt))
        p.step("vector", lambda q: q.tensor_scalar_max(out=gated[:], in0=lgtm[:],
                                                       scalar1=0.0))
        p.step("scalar", lambda q: q.activation(out=expg[:], in_=gated[:], func=AF.Exp))
        p.step("vector", lambda q: q.tensor_mul(out=num[:], in0=pos[:], in1=expg[:]))
        p.step("vector", lambda q: q.tensor_reduce(out=den[:], in_=num[:], axis=AX.X,
                                                   op=ALU.add))
        p.step("vector", lambda q: q.tensor_single_scalar(out=inact[:], in_=den[:],
                                                          scalar=0.5, op=ALU.is_lt))
        p.step("vector", lambda q: q.tensor_add(out=den[:], in0=den[:], in1=inact[:]))
        p.step("vector", lambda q: q.reciprocal(out=rac[:], in_=den[:]))
        p.step("scalar", lambda q: q.activation(out=rwact[:], in_=num[:], func=AF.Copy,
                                                scale=rac[:]))
        p.step("vector", lambda q: q.tensor_tensor(
            out=gtc[:].rearrange("p (a b) -> p a b", a=16),
            in0=lgtm[:].unsqueeze(1).broadcast_to([128, 16, 16]),
            in1=lgtm[:].unsqueeze(2).broadcast_to([128, 16, 16]), op=ALU.is_gt))
        p.step("vector", lambda q: q.tensor_reduce(
            out=cnt[:], in_=gtc[:].rearrange("p (a b) -> p a b", a=16), axis=AX.X,
            op=ALU.add))
        p.step("vector", lambda q: q.tensor_single_scalar(out=ind2[:], in_=cnt[:],
                                                          scalar=1.5, op=ALU.is_lt))
        p.step("vector", lambda q: q.tensor_scalar_mul(out=ind2[:], in0=ind2[:],
                                                       scalar1=0.5))
        p.step("vector", lambda q: q.scalar_tensor_tensor(
            out=rwtm[:], in0=ind2[:], scalar=inact[:], in1=rwact[:],
            op0=ALU.mult, op1=ALU.add))
        p.step("tensor", lambda q: q.transpose(out=prw[0:16, 0:128], in_=rwtm[:],
                                               identity=identt[:]))
        p.step("vector", lambda q, t0=t0: q.tensor_copy(out=rwf[:, t0:t0 + 128],
                                                        in_=prw[0:16, 0:128]))
    p.step("sync", lambda q: q.dma_start(out=rw_o, in_=rwf[:]), dma=True)

    for wsrc, dsto in ((wq_all, qm_o), (wk_all, km_o), (wv_all, vm_o)):
        def ev2(m, t0, w, pb):
            p.step("tensor", lambda q, m=m, t0=t0: q.matmul(
                out=prw[:], lhsT=patt[:, m * 128:(m + 1) * 128],
                rhs=rwf[:, t0:t0 + 512], start=True, stop=True))
            p.step("vector", lambda q: q.tensor_copy(out=rwr[:], in_=prw[:]))
            p.step("vector", lambda q, pb=pb: q.tensor_tensor(
                out=tmp6[:], in0=pb[:, 0:512], in1=rwr[:], op=ALU.mult))
            p.step("tensor", lambda q, m=m, t0=t0: q.matmul(
                out=psX[t0 // 512][0:64, :], lhsT=fpt[:], rhs=tmp6[:],
                start=(m == 0), stop=(m == 7)))
            if m == 7:
                p.step("vector", lambda q, t0=t0: q.tensor_copy(
                    out=accb[:, t0:t0 + 512], in_=psX[t0 // 512][0:64, :]))
        steps_stream_matmul(p, wsrc, 8, 8,
                            lambda k, t0, w: vbf[:, k * TH + t0: k * TH + t0 + 512],
                            ev2, [(t0, 512) for t0 in range(0, TH, 512)],
                            wst, wbf, pmm, mgroup=1)
        p.step("sync", lambda q, dsto=dsto: q.dma_start(out=dsto, in_=accb[:]), dma=True)
    return p.finalize()


def run_E0(inputs, X5):
    nc = _BUILD_CACHE.setdefault("E0", build_E0())
    qp = np.asarray(inputs["L2_ffn_q_proj"], np.float32)
    kp = np.asarray(inputs["L2_ffn_k_proj"], np.float32)
    vp = np.asarray(inputs["L2_ffn_v_proj"], np.float32)
    pat = np.zeros((16, 8 * 128), np.float32)
    for ch in range(8):
        for j in range(128):
            pat[2 * ch + j // 64, ch * 128 + j] = 1.0
    base = {
        "wsim": np.ascontiguousarray(np.asarray(inputs["L2_ffn_sim_matrix"], np.float32)),
        "gates": np.asarray(inputs["L2_ffn_gates"], np.float32).reshape(1, 16),
        "wq_all": np.ascontiguousarray(qp.transpose(1, 0, 2).reshape(DM, DM)),
        "wk_all": np.ascontiguousarray(kp.transpose(1, 0, 2).reshape(DM, DM)),
        "wv_all": np.ascontiguousarray(vp.transpose(1, 0, 2).reshape(DM, DM)),
        "pat": pat,
        "ident": np.eye(128, dtype=np.float32),
    }
    import ml_dtypes
    fpat = np.zeros((128, 64), np.float32)
    for j in range(128):
        fpat[j, j % 64] = 1.0
    base["fpat"] = fpat.astype(ml_dtypes.bfloat16)
    in_maps = []
    for c in range(NCORES):
        b, s = c // 2, c % 2
        m = dict(base)
        m["xin"] = np.ascontiguousarray(X5[b][:, s * TH:(s + 1) * TH])
        in_maps.append(m)
    res = run_bass_kernel_spmd(nc, in_maps, core_ids=list(range(NCORES)))
    QM = np.empty((B, 64, T), ml_dtypes.bfloat16)
    KM = np.empty((B, 64, T), ml_dtypes.bfloat16)
    VM = np.empty((B, 64, T), ml_dtypes.bfloat16)
    RW = np.empty((B, 16, T), np.float32)
    for c in range(NCORES):
        b, s = c // 2, c % 2
        r = res.results[c]
        QM[b, :, s * TH:(s + 1) * TH] = r["qm"]
        KM[b, :, s * TH:(s + 1) * TH] = r["km"]
        VM[b, :, s * TH:(s + 1) * TH] = r["vm"]
        RW[b, :, s * TH:(s + 1) * TH] = r["rw"]
    return QM, KM, VM, RW



def build_E1():
    p = Prog("E1")
    NJ = 16
    SCALE = 0.125
    QF = p.din("QF", [64, NJ * 128], BF16)
    KM = p.din("KM", [64, T], BF16)
    VT = p.din("VT", [T, 64], BF16)
    mA1 = p.din("mA1", [128, 128])
    mA2_ = p.din("mA2x", [128, 128])
    mB1 = p.din("mB1", [128, 128])
    mB2_ = p.din("mB2x", [128, 128])
    ident = p.din("ident", [128, 128])
    o_o = p.dout("o", [64, NJ * 128])

    kaug = p.sbuf("kaug", [65, T], BF16)
    qaug = p.sbuf("qaug", [65, NJ * 128], BF16)
    vt = p.sbuf("vt", [128, 32 * 64], BF16)
    sa = p.sbuf("sa", [128, 34 * 128])
    pt = p.sbuf("pt", [128, 512], BF16)
    mcol = p.sbuf("mcol", [128, NJ])
    m16f = p.sbuf("m16f", [16, 128])
    nm16 = p.sbuf("nm16", [16, 128], BF16)
    srec = p.sbuf("srec", [1, 128])
    rep = p.sbuf("rep", [64, 128])
    oo = p.sbuf("oo", [64, NJ * 128])
    msA1 = p.sbuf("msA1", [128, 128])
    msA2 = p.sbuf("msA2", [128, 128])
    msB1 = p.sbuf("msB1", [128, 128])
    msB2 = p.sbuf("msB2", [128, 128])
    identt = p.sbuf("identt", [128, 128])
    onesb = p.sbuf("onesb", [128, 1], BF16)
    p.ones_row = p.sbuf("ones_row", [1, 128])
    psA = p.psum("psA", [128, 512])
    psB = p.psum("psB", [128, 512])
    psS = p.psum("psS", [1, 128])
    psO = p.psum("psO", [64, 128])
    psR = p.psum("psR", [64, 128])
    pstr = p.psum("pstr", [16, 128])

    p.step("vector", lambda q: q.memset(onesb[:], 1.0))
    p.step("vector", lambda q: q.memset(p.ones_row[:], 1.0))
    p.step("vector", lambda q: q.memset(qaug[:], 0.0))

    def ld(dst, src):
        p.step("sync", lambda q, d=dst, s=src: q.dma_start(out=d, in_=s), dma=True)
    ld(msA1[:], mA1)
    ld(msA2[:], mA2_)
    ld(msB1[:], mB1)
    ld(msB2[:], mB2_)
    ld(identt[:], ident)
    ld(qaug[0:64, :], QF)
    ld(kaug[0:64, :], KM)
    p.step("vector", lambda q: q.memset(kaug[64:65, :], 1.0))
    ld(vt[:].rearrange("p (c n) -> p c n", c=32), VT.rearrange("(c p) n -> p c n", p=128))

    for j in range(NJ):
        nkb = 2 * j + 2
        ngr = (nkb * 128 + 511) // 512

        for g in range(ngr):
            w = min(512, nkb * 128 - g * 512)
            p.step("tensor", lambda q, j=j, g=g, w=w: q.matmul(
                out=psA[:, 0:w], lhsT=qaug[0:64, j * 128:(j + 1) * 128],
                rhs=kaug[0:64, g * 512: g * 512 + w], start=True, stop=True))

            def evA(q, j=j, g=g, w=w, nkb=nkb):
                done = None
                for kb in range(g * 4, min(nkb, g * 4 + (w + 127) // 128)):
                    off = kb * 128 - g * 512
                    if kb == nkb - 2:
                        done = q.tensor_tensor(
                            out=sa[:, kb * 128:(kb + 1) * 128],
                            in0=psA[:, off:off + 128], in1=msA1[:], op=ALU.add)
                    elif kb == nkb - 1:
                        done = q.tensor_tensor(
                            out=sa[:, kb * 128:(kb + 1) * 128],
                            in0=psA[:, off:off + 128], in1=msA2[:], op=ALU.add)
                    else:
                        done = q.tensor_copy(out=sa[:, kb * 128:(kb + 1) * 128],
                                             in_=psA[:, off:off + 128])
                return done
            p.step("vector", evA)
        p.step("vector", lambda q, j=j, nkb=nkb: q.reduce_max(
            out=mcol[:, j:j + 1], in_=sa[:, 0:nkb * 128], axis=AX.X))

    p.step("tensor", lambda q: q.transpose(out=pstr[:], in_=mcol[:],
                                           identity=identt[:]))
    p.step("vector", lambda q: q.tensor_copy(out=m16f[:], in_=pstr[:]))
    p.step("vector", lambda q: q.tensor_scalar_mul(out=nm16[:], in0=m16f[:],
                                                   scalar1=-1.0))
    p.step("sync", lambda q: q.dma_start(
        out=qaug[64:65, :].rearrange("p (b t) -> p b t", b=NJ),
        in_=nm16[:].unsqueeze(1)), dma=True)

    for j in range(NJ):
        nkb = 2 * j + 2
        for g in range((nkb + 3) // 4):
            kbs = list(range(g * 4, min(nkb, g * 4 + 4)))

            def mmB(q, j=j, kbs=kbs):
                for i, kb in enumerate(kbs):
                    last = q.matmul(out=psB[:, i * 128:(i + 1) * 128],
                                    lhsT=kaug[0:65, kb * 128:(kb + 1) * 128],
                                    rhs=qaug[0:65, j * 128:(j + 1) * 128],
                                    start=True, stop=True)
                return last
            p.step("tensor", mmB)

            for i, kb in enumerate(kbs):
                if kb == nkb - 2:
                    p.step("vector", lambda q, i=i: q.tensor_tensor(
                        out=sa[:, 0:128], in0=psB[:, i * 128:(i + 1) * 128],
                        in1=msB1[:], op=ALU.add))
                    p.step("scalar", lambda q, i=i: q.activation(
                        out=pt[:, i * 128:(i + 1) * 128], in_=sa[:, 0:128],
                        func=AF.Exp, scale=SCALE))
                elif kb == nkb - 1:
                    p.step("vector", lambda q, i=i: q.tensor_tensor(
                        out=sa[:, 0:128], in0=psB[:, i * 128:(i + 1) * 128],
                        in1=msB2[:], op=ALU.add))
                    p.step("scalar", lambda q, i=i: q.activation(
                        out=pt[:, i * 128:(i + 1) * 128], in_=sa[:, 0:128],
                        func=AF.Exp, scale=SCALE))
                else:
                    p.step("scalar", lambda q, i=i: q.activation(
                        out=pt[:, i * 128:(i + 1) * 128],
                        in_=psB[:, i * 128:(i + 1) * 128], func=AF.Exp, scale=SCALE))

            def sums(q, g=g, kbs=kbs, nkb=nkb):
                for i, kb in enumerate(kbs):
                    last = q.matmul(out=psS[:], lhsT=onesb[:, 0:1],
                                    rhs=pt[:, i * 128:(i + 1) * 128],
                                    start=(kb == 0), stop=(kb == nkb - 1))
                return last
            p.step("tensor", sums)

            def pv(q, g=g, kbs=kbs, nkb=nkb):
                for i, kb in enumerate(kbs):
                    last = q.matmul(out=psO[:], lhsT=vt[:, kb * 64:(kb + 1) * 64],
                                    rhs=pt[:, i * 128:(i + 1) * 128],
                                    start=(kb == 0), stop=(kb == nkb - 1))
                return last
            p.step("tensor", pv)

        p.step("vector", lambda q: q.reciprocal(out=srec[:], in_=psS[0:1, :]))
        p.step("tensor", lambda q: q.matmul(out=psR[:], lhsT=p.ones_row[0:1, 0:64],
                                            rhs=srec[:], start=True, stop=True))
        p.step("vector", lambda q: q.tensor_copy(out=rep[:], in_=psR[:]))
        p.step("vector", lambda q, j=j: q.tensor_mul(
            out=oo[:, j * 128:(j + 1) * 128], in0=psO[:], in1=rep[:]))
    p.step("sync", lambda q: q.dma_start(out=o_o, in_=oo[:]), dma=True)
    return p.finalize()


def run_E1(inputs, QM, KM, VM):
    nc = _BUILD_CACHE.setdefault("E1", build_E1())
    import ml_dtypes
    i0, i1 = np.tril_indices(128, -1)
    causA = np.zeros((128, 128), np.float32)
    causA[np.triu_indices(128, 1)] = -1e30
    causB = np.ascontiguousarray(causA.T)
    full = np.full((128, 128), -1e30, np.float32)
    zero = np.zeros((128, 128), np.float32)
    in_maps = []
    for c in range(NCORES):
        b, par = c // 2, c % 2
        qblocks = [2 * j + par for j in range(16)]
        qf = np.concatenate([np.asarray(QM[b][:, qb * 128:(qb + 1) * 128])
                             for qb in qblocks], axis=1)
        if par == 0:
            m_a1, m_a2 = causA, full
            m_b1, m_b2 = causB, full
        else:
            m_a1, m_a2 = zero, causA
            m_b1, m_b2 = zero, causB
        in_maps.append({
            "QF": np.ascontiguousarray(qf.astype(ml_dtypes.bfloat16)),
            "KM": np.ascontiguousarray(np.asarray(KM[b], ml_dtypes.bfloat16)),
            "VT": np.ascontiguousarray(np.asarray(VM[b], np.float32).T.astype(ml_dtypes.bfloat16)),
            "mA1": m_a1, "mA2x": m_a2, "mB1": m_b1, "mB2x": m_b2,
            "ident": np.eye(128, dtype=np.float32),
        })
    res = run_bass_kernel_spmd(nc, in_maps, core_ids=list(range(NCORES)))
    AT2 = np.empty((B, 64, T), np.float32)
    for c in range(NCORES):
        b, par = c // 2, c % 2
        o = res.results[c]["o"]
        for j in range(16):
            qb = 2 * j + par
            AT2[b, :, qb * 128:(qb + 1) * 128] = o[:, j * 128:(j + 1) * 128]
    return AT2



def build_E2():
    p = Prog("E2")
    xin = p.din("xin", [DM, TH])
    at_i = p.din("at2", [64, TH], BF16)
    rw_i = p.din("rw", [16, TH])
    wo = p.din("wo", [DM, DM])
    pat = p.din("pat", [16, 8 * 128])
    pat2 = p.din("pat2", [64, 128], BF16)
    xo = p.dout("xo", [DM, TH])

    hres = p.sbuf("hres_s", [128, 8 * TH])
    atb = p.sbuf("atb", [64, TH], BF16)
    rwf = p.sbuf("rwf", [16, TH])
    patt = p.sbuf("patt", [16, 8 * 128])
    patt2 = p.sbuf("patt2", [64, 128], BF16)
    mixb = p.sbuf("mixb", [128, 8 * 512], BF16)
    rwr = p.sbuf("rwr", [128, 512])
    wst = [p.sbuf("wst0", [128, 8 * 128])]
    wbf = [p.sbuf("wbf0", [128, 8 * 128], BF16)]
    pmm = [p.psum(f"pmm{i}", [128, 512]) for i in range(2)]
    prw = p.psum("prw", [128, 512])
    par_ = p.psum("par", [128, 512])

    def ld(dst, src):
        p.step("sync", lambda q, d=dst, s=src: q.dma_start(out=d, in_=s), dma=True)
    ld(hres[:].rearrange("p (c t) -> p c t", c=8), xin.rearrange("(c p) t -> p c t", p=128))
    ld(atb[:], at_i)
    ld(rwf[:], rw_i)
    ld(patt[:], pat)
    ld(patt2[:], pat2)

    for tc in range(TH // 512):
        t0 = tc * 512
        for ch in range(8):
            p.step("tensor", lambda q, ch=ch, t0=t0: q.matmul(
                out=par_[:], lhsT=patt2[:], rhs=atb[:, t0:t0 + 512],
                start=True, stop=True))
            p.step("tensor", lambda q, ch=ch, t0=t0: q.matmul(
                out=prw[:], lhsT=patt[:, ch * 128:(ch + 1) * 128],
                rhs=rwf[:, t0:t0 + 512], start=True, stop=True))
            p.step("vector", lambda q: q.tensor_copy(out=rwr[:], in_=prw[:]))
            p.step("vector", lambda q, ch=ch: q.tensor_tensor(
                out=mixb[:, ch * 512:(ch + 1) * 512], in0=par_[:], in1=rwr[:],
                op=ALU.mult))

        def ev(m, tt0, w, pb, t0=t0):
            p.step("vector", lambda q, m=m, pb=pb, t0=t0: q.tensor_tensor(
                out=hres[:, m * TH + t0: m * TH + t0 + 512],
                in0=hres[:, m * TH + t0: m * TH + t0 + 512], in1=pb[:, 0:512],
                op=ALU.add))
        steps_stream_matmul(p, wo, 8, 8,
                            lambda k, tt0, w: mixb[:, k * 512: k * 512 + 512],
                            ev, [(0, 512)], wst, wbf, pmm, mgroup=1)

    p.step("sync", lambda q: q.dma_start(
        out=xo.rearrange("(c p) t -> p c t", p=128),
        in_=hres[:].rearrange("p (c t) -> p c t", c=8)), dma=True)
    return p.finalize()


def run_E2(inputs, X5, AT2, RW):
    nc = _BUILD_CACHE.setdefault("E2", build_E2())
    op = np.asarray(inputs["L2_ffn_o_proj"], np.float32)
    pat = np.zeros((16, 8 * 128), np.float32)
    for ch in range(8):
        for j in range(128):
            pat[2 * ch + j // 64, ch * 128 + j] = 1.0
    pat2 = np.zeros((64, 128), np.float32)
    for j in range(128):
        pat2[j % 64, j] = 1.0
    base = {
        "wo": np.ascontiguousarray(op.reshape(DM, DM)),
        "pat": pat,
    }
    import ml_dtypes
    base["pat2"] = pat2.astype(ml_dtypes.bfloat16)
    in_maps = []
    for c in range(NCORES):
        b, s = c // 2, c % 2
        m = dict(base)
        m["xin"] = np.ascontiguousarray(X5[b][:, s * TH:(s + 1) * TH])
        m["at2"] = np.ascontiguousarray(
            np.asarray(AT2[b][:, s * TH:(s + 1) * TH], ml_dtypes.bfloat16))
        m["rw"] = np.ascontiguousarray(RW[b][:, s * TH:(s + 1) * TH])
        in_maps.append(m)
    res = run_bass_kernel_spmd(nc, in_maps, core_ids=list(range(NCORES)))
    XO = np.empty((B, DM, T), np.float32)
    for c in range(NCORES):
        b, s = c // 2, c % 2
        XO[b, :, s * TH:(s + 1) * TH] = res.results[c]["xo"]
    return XO



def build_G0():
    p = Prog("G0")
    xin = p.din("xin", [DM, TH])
    wqkv = p.din("wqkv", [DM, 3 * DM])
    bqkv = p.din("bqkv", [128, 24])
    q2lp = p.din("q2lp", [64, 64], BF16)
    kv2lp = p.din("kv2lp", [64, 64], BF16)
    bq2l = p.din("bq2l", [64, 1])
    bkv2l = p.din("bkv2l", [64, 1])
    wqkp = p.din("wqkp", [64, 64], BF16)
    bwqk = p.din("bwqk", [64, 1])
    cos64 = p.din("cos64", [64, TH])
    sin64 = p.din("sin64", [64, TH])
    qfk_o = p.dout("qfk", [16, 64, TH], BF16)
    kl_o = p.dout("kl", [16, 64, TH], BF16)
    vl_o = p.dout("vl", [16, 64, TH], BF16)

    ubf = p.sbuf("ubf", [128, 8 * TH], BF16)
    uf = p.sbuf("uf", [128, 8 * 512])
    qkvb = p.sbuf("qkvb", [128, 512], BF16)
    bq = p.sbuf("bq", [128, 24])
    q2t = p.sbuf("q2t", [128, 64], BF16)
    kvt = p.sbuf("kvt", [128, 64], BF16)
    wqt = p.sbuf("wqt", [64, 64], BF16)
    bq2 = p.sbuf("bq2", [64, 1])
    bkv = p.sbuf("bkv", [64, 1])
    bwq = p.sbuf("bwq", [64, 1])
    cst = p.sbuf("cst", [64, TH])
    snt = p.sbuf("snt", [64, TH])
    lbuf = p.sbuf("lbuf", [64, 512])
    rb = p.sbuf("rb", [64, 512])
    scr = p.sbuf("scr", [64, 512])
    latb = p.sbuf("latb", [64, 512], BF16)
    wst = [p.sbuf("wst0", [128, 8 * 128])]
    wbf = [p.sbuf("wbf0", [128, 8 * 128], BF16)]
    pmm = [p.psum(f"pmm{i}", [128, 512]) for i in range(2)]
    pla = p.psum("pla", [64, 512])
    ln_setup(p)

    def ld(dst, src):
        p.step("sync", lambda q, d=dst, s=src: q.dma_start(out=d, in_=s), dma=True)
    ld(bq[:], bqkv)
    ld(q2t[0:64, :], q2lp)
    ld(q2t[64:128, :], q2lp)
    ld(kvt[0:64, :], kv2lp)
    ld(kvt[64:128, :], kv2lp)
    ld(wqt[:], wqkp)
    ld(bq2[:], bq2l)
    ld(bkv[:], bkv2l)
    ld(bwq[:], bwqk)
    ld(cst[:], cos64)
    ld(snt[:], sin64)

    TWS = [(t0, 512) for t0 in range(0, TH, 512)]
    for (t0, w) in TWS:
        p.step("sync", lambda q, t0=t0: q.dma_start(
            out=uf[:].rearrange("p (c t) -> p c t", c=8),
            in_=xin.rearrange("(c p) t -> p c t", p=128)[:, :, t0:t0 + 512]), dma=True)
        srcf = lambda c: uf[:, c * 512:(c + 1) * 512]
        dstf = lambda c, t0=t0: ubf[:, c * TH + t0: c * TH + t0 + 512]
        steps_ln(p, srcf, dstf, 512)

    def rope(which, t0):
        p.step("vector", lambda q, t0=t0: q.tensor_tensor(
            out=rb[0:16, :], in0=lbuf[0:16, :], in1=cst[0:16, t0:t0 + 512],
            op=ALU.mult))
        p.step("vector", lambda q, t0=t0: q.tensor_tensor(
            out=scr[0:16, :], in0=lbuf[32:48, :], in1=snt[32:48, t0:t0 + 512],
            op=ALU.mult))
        p.step("vector", lambda q: q.tensor_sub(out=rb[0:16, :], in0=rb[0:16, :],
                                                in1=scr[0:16, :]))
        p.step("vector", lambda q, t0=t0: q.tensor_tensor(
            out=rb[32:48, :], in0=lbuf[32:48, :], in1=cst[32:48, t0:t0 + 512],
            op=ALU.mult))
        p.step("vector", lambda q, t0=t0: q.tensor_tensor(
            out=scr[32:48, :], in0=lbuf[0:16, :], in1=snt[0:16, t0:t0 + 512],
            op=ALU.mult))
        p.step("vector", lambda q: q.tensor_tensor(out=rb[32:48, :], in0=rb[32:48, :],
                                                   in1=scr[32:48, :], op=ALU.add))

    def ev(m, t0, w, pb):
        kind = m // 8
        p.step("vector", lambda q, m=m, pb=pb: q.tensor_tensor(
            out=qkvb[:], in0=pb[:, 0:512],
            in1=bq[:, m:m + 1].broadcast_to([128, 512]), op=ALU.add))
        for sub in range(2):
            head = (m % 8) * 2 + sub
            lt = q2t if kind == 0 else kvt
            bias = bq2 if kind == 0 else bkv
            p.step("tensor", lambda q, sub=sub, lt=lt: q.matmul(
                out=pla[:], lhsT=lt[sub * 64:(sub + 1) * 64, :],
                rhs=qkvb[sub * 64:(sub + 1) * 64, :], start=True, stop=True))
            p.step("vector", lambda q, bias=bias: q.tensor_tensor(
                out=lbuf[:], in0=pla[:], in1=bias[:].broadcast_to([64, 512]),
                op=ALU.add))
            if kind == 2:
                p.step("vector", lambda q: q.tensor_copy(out=latb[:], in_=lbuf[:]))
                p.step("sync", lambda q, head=head, t0=t0: q.dma_start(
                    out=vl_o[head][:, t0:t0 + 512], in_=latb[:]), dma=True)
            else:
                rope(kind, t0)
                p.step("vector", lambda q: q.tensor_copy(out=latb[:], in_=rb[:]))
                if kind == 1:
                    p.step("sync", lambda q, head=head, t0=t0: q.dma_start(
                        out=kl_o[head][:, t0:t0 + 512], in_=latb[:]), dma=True)
                else:
                    p.step("tensor", lambda q: q.matmul(
                        out=pla[:], lhsT=wqt[:], rhs=latb[:], start=True, stop=True))
                    p.step("vector", lambda q: q.tensor_tensor(
                        out=lbuf[:], in0=pla[:], in1=bwq[:].broadcast_to([64, 512]),
                        op=ALU.add))
                    p.step("vector", lambda q: q.tensor_copy(out=latb[:], in_=lbuf[:]))
                    p.step("sync", lambda q, head=head, t0=t0: q.dma_start(
                        out=qfk_o[head][:, t0:t0 + 512], in_=latb[:]), dma=True)
    steps_stream_matmul(p, wqkv, 8, 24,
                        lambda k, t0, w: ubf[:, k * TH + t0: k * TH + t0 + 512],
                        ev, TWS, wst, wbf, pmm, mgroup=1)
    return p.finalize()


def _pad_lat(w32):
    raise NotImplementedError


def run_G0(inputs, X6):
    nc = _BUILD_CACHE.setdefault("G0", build_G0())
    import ml_dtypes

    def padmap(l):
        return (l // 2) if l % 2 == 0 else 32 + l // 2

    q2l = np.asarray(inputs["L3_attn_q2l_w"], np.float32)
    kv2l = np.asarray(inputs["L3_attn_kv2l_w"], np.float32)
    q2lp = np.zeros((64, 64), np.float32)
    kv2lp = np.zeros((64, 64), np.float32)
    for l in range(32):
        q2lp[:, padmap(l)] = q2l[l]
        kv2lp[:, padmap(l)] = kv2l[l]
    bq2 = np.zeros((64, 1), np.float32)
    bkv = np.zeros((64, 1), np.float32)
    for l in range(32):
        bq2[padmap(l), 0] = np.asarray(inputs["L3_attn_q2l_b"], np.float32)[l]
        bkv[padmap(l), 0] = np.asarray(inputs["L3_attn_kv2l_b"], np.float32)[l]
    wqk = np.asarray(inputs["L3_attn_wqk_w"], np.float32)
    wqkp = np.zeros((64, 64), np.float32)
    for li in range(32):
        for lo in range(32):
            wqkp[padmap(li), padmap(lo)] = wqk[lo, li]
    bwq = np.zeros((64, 1), np.float32)
    for l in range(32):
        bwq[padmap(l), 0] = np.asarray(inputs["L3_attn_wqk_b"], np.float32)[l]
    inv = 1.0 / 10000.0 ** (np.arange(0, 32, 2, dtype=np.float32) / 32.0)
    tt = np.arange(T, dtype=np.float32)
    fr = np.outer(inv, tt)
    cosf = np.cos(fr).astype(np.float32)
    sinf = np.sin(fr).astype(np.float32)
    base = {
        "wqkv": np.ascontiguousarray(np.asarray(inputs["L3_attn_c_attn_w"], np.float32).T),
        "bqkv": _chunkvec(inputs["L3_attn_c_attn_b"], 24),
        "q2lp": q2lp.astype(ml_dtypes.bfloat16),
        "kv2lp": kv2lp.astype(ml_dtypes.bfloat16),
        "bq2l": bq2, "bkv2l": bkv,
        "wqkp": wqkp.astype(ml_dtypes.bfloat16), "bwqk": bwq,
    }
    in_maps = []
    for c in range(NCORES):
        b, s = c // 2, c % 2
        m = dict(base)
        m["xin"] = np.ascontiguousarray(X6[b][:, s * TH:(s + 1) * TH])
        cos = np.zeros((64, TH), np.float32)
        sin = np.zeros((64, TH), np.float32)
        sl = slice(s * TH, (s + 1) * TH)
        cos[0:16] = cosf[:, sl]; cos[32:48] = cosf[:, sl]
        sin[0:16] = sinf[:, sl]; sin[32:48] = sinf[:, sl]
        m["cos64"] = cos
        m["sin64"] = sin
        in_maps.append(m)
    res = run_bass_kernel_spmd(nc, in_maps, core_ids=list(range(NCORES)))
    import ml_dtypes
    QFK = np.empty((B, 16, 64, T), ml_dtypes.bfloat16)
    KL = np.empty((B, 16, 64, T), ml_dtypes.bfloat16)
    VL = np.empty((B, 16, 64, T), ml_dtypes.bfloat16)
    for c in range(NCORES):
        b, s = c // 2, c % 2
        r = res.results[c]
        QFK[b, :, :, s * TH:(s + 1) * TH] = r["qfk"]
        KL[b, :, :, s * TH:(s + 1) * TH] = r["kl"]
        VL[b, :, :, s * TH:(s + 1) * TH] = r["vl"]
    return QFK, KL, VL



def build_G1():
    p = Prog("G1")
    NH_ = 8
    NQB = T // 128
    SCALE = float(32.0 ** -0.5)
    QF = p.din("QF", [NH_, 64, T], BF16)
    KF = p.din("KF", [NH_, 64, T], BF16)
    VT = p.din("VT", [NH_, T, 64], BF16)
    l2op = p.din("l2op", [64, 64], BF16)
    bl2o = p.din("bl2o", [64, 1])
    mA = p.din("mA", [128, 128])
    mB = p.din("mB", [128, 128])
    ident = p.din("ident", [128, 128])
    y_o = p.dout("y", [NH_, 64, T], BF16)

    kaug = p.sbuf("kaug", [65, T], BF16)
    qaug = p.sbuf("qaug", [65, T], BF16)
    vt = p.sbuf("vt", [128, 32 * 64], BF16)
    yh = p.sbuf("yh", [64, T], BF16)
    sa = p.sbuf("sa", [128, 32 * 128])
    sb = p.sbuf("sb", [128, 128])
    pt = p.sbuf("pt", [128, 512], BF16)
    mcol = p.sbuf("mcol", [128, 32])
    m32f = p.sbuf("m32f", [32, 128])
    nm32 = p.sbuf("nm32", [32, 128], BF16)
    srec = p.sbuf("srec", [1, 128])
    rep = p.sbuf("rep", [64, 128])
    ylatb = p.sbuf("ylatb", [64, 128], BF16)
    msA = p.sbuf("msA", [128, 128])
    msB = p.sbuf("msB", [128, 128])
    identt = p.sbuf("identt", [128, 128])
    onesb = p.sbuf("onesb", [128, 1], BF16)
    l2t = p.sbuf("l2t", [64, 64], BF16)
    bl2 = p.sbuf("bl2", [64, 1])
    p.ones_row = p.sbuf("ones_row", [1, 128])
    psA = p.psum("psA", [128, 512])
    psB = p.psum("psB", [128, 512])
    psS = p.psum("psS", [1, 128])
    psO = p.psum("psO", [64, 128])
    psR = p.psum("psR", [64, 128])
    pstr = p.psum("pstr", [32, 128])

    p.step("vector", lambda q: q.memset(onesb[:], 1.0))
    p.step("vector", lambda q: q.memset(p.ones_row[:], 1.0))
    p.step("vector", lambda q: q.memset(qaug[:], 0.0))

    def ld(dst, src):
        p.step("sync", lambda q, d=dst, s=src: q.dma_start(out=d, in_=s), dma=True)
    ld(msA[:], mA)
    ld(msB[:], mB)
    ld(identt[:], ident)
    ld(l2t[:], l2op)
    ld(bl2[:], bl2o)

    for h in range(NH_):
        ld(qaug[0:64, :], QF[h])
        ld(kaug[0:64, :], KF[h])
        p.step("vector", lambda q: q.memset(kaug[64:65, :], 1.0))
        ld(vt[:].rearrange("p (c n) -> p c n", c=32),
           VT[h].rearrange("(c p) n -> p c n", p=128))

        for qb in range(NQB):
            nkb = qb + 1
            ngr = (nkb * 128 + 511) // 512
            for g in range(ngr):
                w = min(512, nkb * 128 - g * 512)
                p.step("tensor", lambda q, qb=qb, g=g, w=w: q.matmul(
                    out=psA[:, 0:w], lhsT=qaug[0:64, qb * 128:(qb + 1) * 128],
                    rhs=kaug[0:64, g * 512: g * 512 + w], start=True, stop=True))

                def evA(q, g=g, w=w, nkb=nkb):
                    done = None
                    nk_here = (w + 127) // 128
                    for i in range(nk_here):
                        kb = g * 4 + i
                        if kb == nkb - 1:
                            done = q.tensor_tensor(
                                out=sa[:, kb * 128:(kb + 1) * 128],
                                in0=psA[:, i * 128:(i + 1) * 128], in1=msA[:],
                                op=ALU.add)
                        else:
                            done = q.tensor_copy(out=sa[:, kb * 128:(kb + 1) * 128],
                                                 in_=psA[:, i * 128:(i + 1) * 128])
                    return done
                p.step("vector", evA)
            p.step("vector", lambda q, qb=qb, nkb=nkb: q.reduce_max(
                out=mcol[:, qb:qb + 1], in_=sa[:, 0:nkb * 128], axis=AX.X))

        p.step("tensor", lambda q: q.transpose(out=pstr[:], in_=mcol[:],
                                               identity=identt[:]))
        p.step("vector", lambda q: q.tensor_copy(out=m32f[:], in_=pstr[:]))
        p.step("vector", lambda q: q.tensor_scalar_mul(out=nm32[:], in0=m32f[:],
                                                       scalar1=-1.0))
        p.step("sync", lambda q: q.dma_start(
            out=qaug[64:65, :].rearrange("p (b t) -> p b t", b=32),
            in_=nm32[:].unsqueeze(1)), dma=True)

        for qb in range(NQB):
            nkb = qb + 1
            for g in range((nkb + 3) // 4):
                kbs = list(range(g * 4, min(nkb, g * 4 + 4)))

                def mmB(q, qb=qb, kbs=kbs):
                    for i, kb in enumerate(kbs):
                        last = q.matmul(out=psB[:, i * 128:(i + 1) * 128],
                                        lhsT=kaug[0:65, kb * 128:(kb + 1) * 128],
                                        rhs=qaug[0:65, qb * 128:(qb + 1) * 128],
                                        start=True, stop=True)
                    return last
                p.step("tensor", mmB)

                has_mask = (kbs[-1] == nkb - 1)
                nfull = len(kbs) - (1 if has_mask else 0)
                if has_mask:
                    p.step("vector", lambda q, i=len(kbs) - 1: q.tensor_tensor(
                        out=sb[:], in0=psB[:, i * 128:(i + 1) * 128],
                        in1=msB[:], op=ALU.add))

                def exps(q, nfull=nfull, has_mask=has_mask, nk=len(kbs)):
                    last = None
                    if nfull:
                        last = q.activation(out=pt[:, 0:nfull * 128],
                                            in_=psB[:, 0:nfull * 128], func=AF.Exp,
                                            scale=SCALE)
                    if has_mask:
                        last = q.activation(out=pt[:, (nk - 1) * 128: nk * 128],
                                            in_=sb[:], func=AF.Exp, scale=SCALE)
                    return last
                p.step("scalar", exps)

                def pvs(q, kbs=kbs, nkb=nkb):
                    for i, kb in enumerate(kbs):
                        q.matmul(out=psS[:], lhsT=onesb[:, 0:1],
                                 rhs=pt[:, i * 128:(i + 1) * 128],
                                 start=(kb == 0), stop=(kb == nkb - 1))
                    for i, kb in enumerate(kbs):
                        last = q.matmul(out=psO[:], lhsT=vt[:, kb * 64:(kb + 1) * 64],
                                        rhs=pt[:, i * 128:(i + 1) * 128],
                                        start=(kb == 0), stop=(kb == nkb - 1))
                    return last
                p.step("tensor", pvs)

            p.step("vector", lambda q: q.reciprocal(out=srec[:], in_=psS[0:1, :]))
            p.step("tensor", lambda q: q.matmul(out=psR[:], lhsT=p.ones_row[0:1, 0:64],
                                                rhs=srec[:], start=True, stop=True))
            p.step("vector", lambda q: q.tensor_copy(out=rep[:], in_=psR[:]))
            p.step("vector", lambda q: q.tensor_mul(out=ylatb[:], in0=psO[:], in1=rep[:]))
            p.step("tensor", lambda q: q.matmul(out=psR[:], lhsT=l2t[:], rhs=ylatb[:],
                                                start=True, stop=True))
            p.step("vector", lambda q, qb=qb: q.tensor_tensor(
                out=yh[:, qb * 128:(qb + 1) * 128], in0=psR[:],
                in1=bl2[:].broadcast_to([64, 128]), op=ALU.add))
        p.step("sync", lambda q, h=h: q.dma_start(out=y_o[h], in_=yh[:]), dma=True)
    return p.finalize()


def run_G1(inputs, QFK, KL, VL):
    nc = _BUILD_CACHE.setdefault("G1", build_G1())
    import ml_dtypes

    def padmap(l):
        return (l // 2) if l % 2 == 0 else 32 + l // 2

    l2o = np.asarray(inputs["L3_attn_l2o_w"], np.float32)
    l2op = np.zeros((64, 64), np.float32)
    for l in range(32):
        l2op[padmap(l), :] = l2o[:, l]
    bl2 = np.asarray(inputs["L3_attn_l2o_b"], np.float32).reshape(64, 1)
    causA = np.zeros((128, 128), np.float32)
    causA[np.triu_indices(128, 1)] = -1e30
    causB = np.ascontiguousarray(causA.T)
    base = {
        "l2op": l2op.astype(ml_dtypes.bfloat16),
        "bl2o": bl2,
        "mA": causA, "mB": causB,
        "ident": np.eye(128, dtype=np.float32),
    }
    in_maps = []
    for c in range(NCORES):
        b, hh = c // 2, c % 2
        heads = slice(hh * 8, hh * 8 + 8)
        m = dict(base)
        m["QF"] = np.ascontiguousarray(QFK[b][heads])
        m["KF"] = np.ascontiguousarray(KL[b][heads])
        m["VT"] = np.ascontiguousarray(np.transpose(np.asarray(VL[b][heads]),
                                                    (0, 2, 1)))
        in_maps.append(m)
    res = run_bass_kernel_spmd(nc, in_maps, core_ids=list(range(NCORES)))
    Y3 = np.empty((B, DM, T), ml_dtypes.bfloat16)
    for c in range(NCORES):
        b, hh = c // 2, c % 2
        for j, h in enumerate(range(hh * 8, hh * 8 + 8)):
            Y3[b, h * 64:(h + 1) * 64, :] = res.results[c]["y"][j]
    return Y3


def run_G2(inputs, X6, Y3):
    nc = _BUILD_CACHE.setdefault("B2", build_B2())
    base = {
        "wout": np.ascontiguousarray(np.asarray(inputs["L3_attn_cproj_w"], np.float32).T),
        "bout": _chunkvec(inputs["L3_attn_cproj_b"], 8),
    }
    import ml_dtypes
    in_maps = []
    for c in range(NCORES):
        b, s = c // 2, c % 2
        m = dict(base)
        m["xin"] = np.ascontiguousarray(X6[b][:, s * TH:(s + 1) * TH])
        m["at"] = np.ascontiguousarray(np.asarray(Y3[b][:, s * TH:(s + 1) * TH],
                                                  ml_dtypes.bfloat16))
        in_maps.append(m)
    res = run_bass_kernel_spmd(nc, in_maps, core_ids=list(range(NCORES)))
    XO = np.empty((B, DM, T), np.float32)
    for c in range(NCORES):
        b, s = c // 2, c % 2
        XO[b, :, s * TH:(s + 1) * TH] = res.results[c]["xo"]
    return XO



RNE_C = 8388608.0

def build_H0():
    p = Prog("H0")
    TE3 = TH + 2
    xin = p.din("xin", [DM, TE3])
    w1T = p.din("w1T", [DM, 4 * DM])
    w2T = p.din("w2T", [4 * DM, DM])
    convw = p.din("convw", [128, 32 * 3])
    convb = p.din("convb", [128, 32])
    ident = p.din("ident", [128, 128])
    xo = p.dout("xo", [DM, TH])
    dbg = p.dout("dbg", [128, 80])

    W = 256
    RW_ = 254
    uf = p.sbuf("uf", [128, 8 * W])
    uf2 = p.sbuf("uf2", [128, 8 * W])
    xqb = p.sbuf("xqb", [128, 8 * W], BF16)
    y1buf = p.sbuf("y1buf", [128, 32 * W])
    hbuf = p.sbuf("hbuf", [128, 32 * RW_])
    htree = p.sbuf("htree", [128, 16 * RW_])
    hqb = p.sbuf("hqb", [128, 32 * RW_], BF16)
    xt7 = p.sbuf("xt7", [128, 8 * RW_])
    cwt = p.sbuf("cwt", [128, 32 * 3])
    cbt = p.sbuf("cbt", [128, 32])
    identt = p.sbuf("identt", [128, 128])
    wst = [p.sbuf("wst0", [128, 32 * 128])]
    wqb = [p.sbuf("wqb0", [128, 32 * 128], BF16)]
    wabs = p.sbuf("wabs", [128, 8 * 512])
    wqpc = p.sbuf("wqpc", [128, 8 * 512], BF16)
    amaxt = p.sbuf("amaxt", [128, 2])
    sxt = p.sbuf("sxt", [128, 2])
    rmt = p.sbuf("rmt", [128, 2])
    trow = p.sbuf("trow", [2, 128])
    srow = p.sbuf("srow", [1, W])
    rmrow = p.sbuf("rmrow", [1, W])
    srep = p.sbuf("srep", [128, W])
    rmrep = p.sbuf("rmrep", [128, W])
    sm1 = p.sbuf("sm1", [1, 512])
    tot = p.sbuf("tot", [1, 2])
    swc = p.sbuf("swc", [128, 2])
    mvc = p.sbuf("mvc", [128, 2])
    p.ones_row = p.sbuf("ones_row", [1, 128])
    p.ones_col = p.sbuf("ones_col", [128, 1])
    pmm = [p.psum(f"pmm{i}", [128, 512]) for i in range(2)]
    ptr = p.psum("ptr", [128, 128])
    ln_setup(p, maxw=256,
             share=None)

    def ld(dst, src):
        p.step("sync", lambda q, d=dst, s=src: q.dma_start(out=d, in_=s), dma=True)
    ld(cwt[:], convw)
    ld(cbt[:], convb)
    ld(identt[:], ident)

    for wi, (wd, csz, ncols) in enumerate(((w1T, 8, 4 * DM), (w2T, 32, DM))):
        npieces = (csz * ncols) // (8 * 512)
        piece_cols = 8 * 512 // csz
        first = True
        for pc in range(npieces):
            p.step("sync", lambda q, wd=wd, csz=csz, pc=pc, piece_cols=piece_cols:
                   q.dma_start(
                       out=wabs[:].rearrange("p (c n) -> p c n", c=csz)[:, :, 0:piece_cols],
                       in_=wd.rearrange("(c p) n -> p c n", p=128)
                       [:, :, pc * piece_cols:(pc + 1) * piece_cols]), dma=True)
            p.step("scalar", lambda q: q.activation(out=wabs[:], in_=wabs[:],
                                                    func=AF.Abs))

            def accs(q, first=first):
                for c in range(8):
                    last = q.matmul(out=p.ln_pst1[0:1, 0:512], lhsT=p.ones_col[:, 0:1],
                                    rhs=wabs[:, c * 512:(c + 1) * 512],
                                    start=(first and c == 0), stop=False)
                return last
            p.step("tensor", accs)
            first = False
        p.step("vector", lambda q: q.tensor_copy(out=sm1[:], in_=p.ln_pst1[0:1, 0:512]))
        p.step("vector", lambda q, wi=wi: q.tensor_reduce(out=tot[0:1, wi:wi + 1],
                                                          in_=sm1[:], axis=AX.X,
                                                          op=ALU.add))
    p.step("vector", lambda q: q.tensor_scalar_mul(out=tot[0:1, 0:1], in0=tot[0:1, 0:1],
                                                   scalar1=1.0 / (4 * DM * DM)))
    p.step("vector", lambda q: q.tensor_scalar_mul(out=tot[0:1, 1:2], in0=tot[0:1, 1:2],
                                                   scalar1=1.0 / (4 * DM * DM)))
    p.step("vector", lambda q: q.tensor_scalar_max(out=tot[:], in0=tot[:], scalar1=1e-5))
    p.step("tensor", lambda q: q.matmul(out=ptr[:, 0:2], lhsT=p.ones_row[0:1, :],
                                        rhs=tot[:], start=True, stop=True))
    p.step("vector", lambda q: q.tensor_copy(out=mvc[:], in_=ptr[:, 0:2]))
    p.step("vector", lambda q: q.reciprocal(out=swc[:], in_=mvc[:]))

    w1q_d = p.nc.dram_tensor("w1q_d", [DM, 4 * DM], BF16, kind="Internal").ap()
    w2q_d = p.nc.dram_tensor("w2q_d", [4 * DM, DM], BF16, kind="Internal").ap()
    for wi, (wd, wq_d, csz) in enumerate(((w1T, w1q_d, 8), (w2T, w2q_d, 32))):
        piece_cols = 8 * 512 // csz
        npieces = (csz * (4 * DM if wi == 0 else DM)) // (8 * 512)
        for pc in range(npieces):
            p.step("sync", lambda q, wd=wd, csz=csz, pc=pc, piece_cols=piece_cols:
                   q.dma_start(
                       out=wabs[:].rearrange("p (c n) -> p c n", c=csz)[:, :, 0:piece_cols],
                       in_=wd.rearrange("(c p) n -> p c n", p=128)
                       [:, :, pc * piece_cols:(pc + 1) * piece_cols]), dma=True)
            p.step("vector", lambda q, wi=wi: q.tensor_scalar(
                out=wabs[:], in0=wabs[:], scalar1=swc[:, wi:wi + 1], scalar2=None,
                op0=ALU.mult))
            p.step("vector", lambda q: q.tensor_scalar_add(out=wabs[:], in0=wabs[:],
                                                           scalar1=RNE_C))
            p.step("vector", lambda q: q.tensor_scalar_add(out=wabs[:], in0=wabs[:],
                                                           scalar1=-RNE_C))
            p.step("vector", lambda q: q.tensor_scalar_min(out=wabs[:], in0=wabs[:],
                                                           scalar1=1.0))
            p.step("vector", lambda q: q.tensor_scalar_max(out=wqpc[:], in0=wabs[:],
                                                           scalar1=-1.0))
            p.step("sync", lambda q, wq_d=wq_d, csz=csz, pc=pc, piece_cols=piece_cols:
                   q.dma_start(
                       out=wq_d.rearrange("(c p) n -> p c n", p=128)
                       [:, :, pc * piece_cols:(pc + 1) * piece_cols],
                       in_=wqpc[:].rearrange("p (c n) -> p c n", c=csz)[:, :, 0:piece_cols]),
                   dma=True)

    def quant_slab(r, ncol, wi):
        p.step("vector", lambda q, r=r, ncol=ncol, wi=wi: q.tensor_scalar(
            out=wst[r][:, 0:ncol], in0=wst[r][:, 0:ncol],
            scalar1=swc[:, wi:wi + 1], scalar2=None, op0=ALU.mult))
        p.step("vector", lambda q, r=r, ncol=ncol: q.tensor_scalar_add(
            out=wst[r][:, 0:ncol], in0=wst[r][:, 0:ncol], scalar1=RNE_C))
        p.step("vector", lambda q, r=r, ncol=ncol: q.tensor_scalar_add(
            out=wst[r][:, 0:ncol], in0=wst[r][:, 0:ncol], scalar1=-RNE_C))
        p.step("vector", lambda q, r=r, ncol=ncol: q.tensor_scalar_min(
            out=wst[r][:, 0:ncol], in0=wst[r][:, 0:ncol], scalar1=1.0))
        p.step("vector", lambda q, r=r, ncol=ncol: q.tensor_scalar_max(
            out=wqb[r][:, 0:ncol], in0=wst[r][:, 0:ncol], scalar1=-1.0))

    starts = [254 * k for k in range(8)] + [TE3 - W]
    for k, S in enumerate(starts):
        outsl = (0, 254) if k < 8 else (238, 254)
        p.step("sync", lambda q, S=S: q.dma_start(
            out=uf[:].rearrange("p (c t) -> p c t", c=8),
            in_=xin.rearrange("(c p) t -> p c t", p=128)[:, :, S:S + W]), dma=True)
        srcf = lambda c: uf[:, c * W:(c + 1) * W]
        dstf = lambda c: uf2[:, c * W:(c + 1) * W]
        steps_ln(p, srcf, dstf, W)

        p.step("scalar", lambda q: q.activation(out=uf[:], in_=uf2[:], func=AF.Abs))
        p.step("vector", lambda q: q.tensor_tensor(out=uf[:, 0:4 * W], in0=uf[:, 0:4 * W],
                                                   in1=uf[:, 4 * W:8 * W], op=ALU.max))
        p.step("vector", lambda q: q.tensor_tensor(out=uf[:, 0:2 * W], in0=uf[:, 0:2 * W],
                                                   in1=uf[:, 2 * W:4 * W], op=ALU.max))
        p.step("vector", lambda q: q.tensor_tensor(out=uf[:, 0:W], in0=uf[:, 0:W],
                                                   in1=uf[:, W:2 * W], op=ALU.max))
        for tr in range(2):
            p.step("tensor", lambda q, tr=tr: q.transpose(
                out=ptr[:], in_=uf[:, tr * 128:(tr + 1) * 128], identity=identt[:]))
            p.step("vector", lambda q, tr=tr: q.reduce_max(
                out=amaxt[:, tr:tr + 1], in_=ptr[:], axis=AX.X))
        p.step("vector", lambda q: q.tensor_scalar_max(out=amaxt[:], in0=amaxt[:],
                                                       scalar1=1e-5))
        p.step("vector", lambda q: q.reciprocal(out=sxt[:], in_=amaxt[:]))
        p.step("vector", lambda q: q.tensor_scalar_mul(out=sxt[:], in0=sxt[:],
                                                       scalar1=127.0))
        p.step("vector", lambda q: q.reciprocal(out=rmt[:], in_=sxt[:]))
        p.step("vector", lambda q: q.tensor_scalar(out=rmt[:], in0=rmt[:],
                                                   scalar1=mvc[:, 0:1], scalar2=None,
                                                   op0=ALU.mult))
        def mkrow(src_, dstrow):
            p.step("tensor", lambda q, src_=src_: q.transpose(
                out=ptr[0:2, 0:128], in_=src_, identity=identt[:]))
            p.step("vector", lambda q: q.tensor_copy(out=trow[:], in_=ptr[0:2, 0:128]))
            p.step("sync", lambda q, dstrow=dstrow: q.dma_start(
                out=dstrow.rearrange("p (b t) -> p b t", b=2),
                in_=trow[:].unsqueeze(1)), dma=True)
        mkrow(sxt[:], srow[:])
        mkrow(rmt[:], rmrow[:])
        p.step("tensor", lambda q: q.matmul(out=pmm[0][:, 0:W], lhsT=p.ones_row[0:1, :],
                                            rhs=srow[:], start=True, stop=True))
        p.step("vector", lambda q: q.tensor_copy(out=srep[:], in_=pmm[0][:, 0:W]))
        p.step("tensor", lambda q: q.matmul(out=pmm[0][:, 0:W], lhsT=p.ones_row[0:1, :],
                                            rhs=rmrow[:], start=True, stop=True))
        p.step("vector", lambda q: q.tensor_copy(out=rmrep[:], in_=pmm[0][:, 0:W]))

        def qx(q):
            for c in range(8):
                last = q.tensor_tensor(out=uf[:, c * W:(c + 1) * W],
                                       in0=uf2[:, c * W:(c + 1) * W], in1=srep[:],
                                       op=ALU.mult)
            return last
        p.step("vector", qx)
        p.step("vector", lambda q: q.tensor_scalar_add(out=uf[:], in0=uf[:],
                                                       scalar1=RNE_C))
        p.step("vector", lambda q: q.tensor_scalar_add(out=uf[:], in0=uf[:],
                                                       scalar1=-RNE_C))
        p.step("vector", lambda q: q.tensor_scalar_min(out=uf[:], in0=uf[:],
                                                       scalar1=127.0))
        p.step("vector", lambda q: q.tensor_scalar_max(out=xqb[:], in0=uf[:],
                                                       scalar1=-128.0))

        if k == 0:
            def dmp1(q):
                q.tensor_copy(out=uf2[:, 0:2], in_=mvc[:])
                q.tensor_copy(out=uf2[:, 2:4], in_=amaxt[:])
                q.tensor_copy(out=uf2[:, 4:6], in_=sxt[:])
                q.tensor_copy(out=uf2[:, 6:8], in_=rmt[:])
                q.tensor_copy(out=uf2[:, 8:16], in_=srep[:, 0:8])
                return q.tensor_copy(out=uf2[:, 16:24], in_=rmrep[:, 0:8])
            def dmp1b(q):
                q.tensor_copy(out=xt7[:, 0:2], in_=mvc[:])
                q.tensor_copy(out=xt7[:, 2:4], in_=amaxt[:])
                q.tensor_copy(out=xt7[:, 4:6], in_=sxt[:])
                q.tensor_copy(out=xt7[:, 6:8], in_=rmt[:])
                q.tensor_copy(out=xt7[:, 8:16], in_=srep[:, 0:8])
                last = q.tensor_copy(out=xt7[:, 16:24], in_=rmrep[:, 0:8])
                return q.tensor_copy(out=xt7[:, 24:32], in_=xqb[:, 0:8])
            p.step("vector", dmp1b)
        for m in range(32):
            p.step("sync", lambda q, m=m: q.dma_start(
                out=wqb[0][:, 0:8 * 128].rearrange("p (c n) -> p c n", c=8),
                in_=w1q_d.rearrange("(c p) n -> p c n", p=128)
                [:, :, m * 128:(m + 1) * 128]), dma=True)

            def mm1(q, m=m):
                for kk in range(8):
                    last = q.matmul(out=pmm[m % 2][:, 0:W],
                                    lhsT=wqb[0][:, kk * 128:(kk + 1) * 128],
                                    rhs=xqb[:, kk * W:(kk + 1) * W],
                                    start=(kk == 0), stop=(kk == 7))
                return last
            p.step("tensor", mm1)
            p.step("vector", lambda q, m=m: q.tensor_tensor(
                out=y1buf[:, m * W:(m + 1) * W], in0=pmm[m % 2][:, 0:W], in1=rmrep[:],
                op=ALU.mult))

        for c in range(32):
            p.step("vector", lambda q, c=c: q.tensor_scalar(
                out=hbuf[:, c * RW_:(c + 1) * RW_], in0=y1buf[:, c * W: c * W + RW_],
                scalar1=cwt[:, c * 3: c * 3 + 1], scalar2=None, op0=ALU.mult))
            def c2(q, c=c):
                q.scalar_tensor_tensor(
                    out=hbuf[:, c * RW_:(c + 1) * RW_],
                    in0=y1buf[:, c * W + 1: c * W + 1 + RW_],
                    scalar=cwt[:, c * 3 + 1: c * 3 + 2],
                    in1=hbuf[:, c * RW_:(c + 1) * RW_], op0=ALU.mult, op1=ALU.add)
                return q.scalar_tensor_tensor(
                    out=hbuf[:, c * RW_:(c + 1) * RW_],
                    in0=y1buf[:, c * W + 2: c * W + 2 + RW_],
                    scalar=cwt[:, c * 3 + 2: c * 3 + 3],
                    in1=hbuf[:, c * RW_:(c + 1) * RW_], op0=ALU.mult, op1=ALU.add)
            p.step("vector", c2)
            p.step("scalar", lambda q, c=c: q.activation(
                out=hbuf[:, c * RW_:(c + 1) * RW_], in_=hbuf[:, c * RW_:(c + 1) * RW_],
                func=AF.Silu, bias=cbt[:, c:c + 1]))

        if k == 0:
            def dmp2(q):
                q.tensor_copy(out=xt7[:, 32:40], in_=y1buf[:, 0:8])
                last = q.tensor_copy(out=xt7[:, 40:48], in_=hbuf[:, 0:8])
                return last
            p.step("vector", dmp2)
        p.step("scalar", lambda q: q.activation(out=y1buf[:, 0:32 * RW_], in_=hbuf[:],
                                                func=AF.Abs))
        p.step("vector", lambda q: q.tensor_tensor(
            out=htree[:], in0=y1buf[:, 0:16 * RW_], in1=y1buf[:, 16 * RW_:32 * RW_],
            op=ALU.max))
        p.step("vector", lambda q: q.tensor_tensor(
            out=htree[:, 0:8 * RW_], in0=htree[:, 0:8 * RW_],
            in1=htree[:, 8 * RW_:16 * RW_], op=ALU.max))
        p.step("vector", lambda q: q.tensor_tensor(
            out=htree[:, 0:4 * RW_], in0=htree[:, 0:4 * RW_],
            in1=htree[:, 4 * RW_:8 * RW_], op=ALU.max))
        p.step("vector", lambda q: q.tensor_tensor(
            out=htree[:, 0:2 * RW_], in0=htree[:, 0:2 * RW_],
            in1=htree[:, 2 * RW_:4 * RW_], op=ALU.max))
        p.step("vector", lambda q: q.tensor_tensor(
            out=htree[:, 0:RW_], in0=htree[:, 0:RW_], in1=htree[:, RW_:2 * RW_],
            op=ALU.max))
        for tr in range(2):
            wtr = min(128, RW_ - tr * 128)
            p.step("tensor", lambda q, tr=tr, wtr=wtr: q.transpose(
                out=ptr[0:wtr, 0:128], in_=htree[:, tr * 128: tr * 128 + wtr],
                identity=identt[:]))
            p.step("vector", lambda q, tr=tr, wtr=wtr: q.reduce_max(
                out=amaxt[0:wtr, tr:tr + 1], in_=ptr[0:wtr, 0:128], axis=AX.X))
        p.step("vector", lambda q: q.tensor_scalar_max(out=amaxt[:], in0=amaxt[:],
                                                       scalar1=1e-5))
        p.step("vector", lambda q: q.reciprocal(out=sxt[:], in_=amaxt[:]))
        p.step("vector", lambda q: q.tensor_scalar_mul(out=sxt[:], in0=sxt[:],
                                                       scalar1=127.0))
        p.step("vector", lambda q: q.reciprocal(out=rmt[:], in_=sxt[:]))
        p.step("vector", lambda q: q.tensor_scalar(out=rmt[:], in0=rmt[:],
                                                   scalar1=mvc[:, 1:2], scalar2=None,
                                                   op0=ALU.mult))
        mkrow(sxt[:], srow[:])
        mkrow(rmt[:], rmrow[:])
        p.step("tensor", lambda q: q.matmul(out=pmm[0][:, 0:RW_],
                                            lhsT=p.ones_row[0:1, :],
                                            rhs=srow[0:1, 0:RW_], start=True, stop=True))
        p.step("vector", lambda q: q.tensor_copy(out=srep[:, 0:RW_],
                                                 in_=pmm[0][:, 0:RW_]))
        p.step("tensor", lambda q: q.matmul(out=pmm[0][:, 0:RW_],
                                            lhsT=p.ones_row[0:1, :],
                                            rhs=rmrow[0:1, 0:RW_], start=True, stop=True))
        p.step("vector", lambda q: q.tensor_copy(out=rmrep[:, 0:RW_],
                                                 in_=pmm[0][:, 0:RW_]))

        def qh(q):
            for c in range(32):
                last = q.tensor_tensor(out=y1buf[:, c * RW_:(c + 1) * RW_],
                                       in0=hbuf[:, c * RW_:(c + 1) * RW_],
                                       in1=srep[:, 0:RW_], op=ALU.mult)
            return last
        p.step("vector", qh)
        p.step("vector", lambda q: q.tensor_scalar_add(
            out=y1buf[:, 0:32 * RW_], in0=y1buf[:, 0:32 * RW_], scalar1=RNE_C))
        p.step("vector", lambda q: q.tensor_scalar_add(
            out=y1buf[:, 0:32 * RW_], in0=y1buf[:, 0:32 * RW_], scalar1=-RNE_C))
        p.step("vector", lambda q: q.tensor_scalar_min(
            out=y1buf[:, 0:32 * RW_], in0=y1buf[:, 0:32 * RW_], scalar1=127.0))
        p.step("vector", lambda q: q.tensor_scalar_max(
            out=hqb[:], in0=y1buf[:, 0:32 * RW_], scalar1=-128.0))
        if k == 0:
            def dmp3(q):
                q.tensor_copy(out=xt7[:, 48:56], in_=hqb[:, 0:8])
                q.tensor_copy(out=xt7[:, 56:58], in_=sxt[:])
                last = q.tensor_copy(out=xt7[:, 58:60], in_=rmt[:])
                return last
            p.step("vector", dmp3)
            p.step("sync", lambda q: q.dma_start(out=dbg[:, 0:60], in_=xt7[:, 0:60]),
                   dma=True)

        p.step("sync", lambda q, S=S: q.dma_start(
            out=xt7[:].rearrange("p (c t) -> p c t", c=8),
            in_=xin.rearrange("(c p) t -> p c t", p=128)[:, :, S + 1:S + 1 + RW_]),
            dma=True)
        for m in range(8):
            p.step("sync", lambda q, m=m: q.dma_start(
                out=wqb[0][:].rearrange("p (c n) -> p c n", c=32),
                in_=w2q_d.rearrange("(c p) n -> p c n", p=128)
                [:, :, m * 128:(m + 1) * 128]), dma=True)

            def mm2(q, m=m):
                for kk in range(32):
                    last = q.matmul(out=pmm[m % 2][:, 0:RW_],
                                    lhsT=wqb[0][:, kk * 128:(kk + 1) * 128],
                                    rhs=hqb[:, kk * RW_:(kk + 1) * RW_],
                                    start=(kk == 0), stop=(kk == 31))
                return last
            p.step("tensor", mm2)
            p.step("vector", lambda q, m=m: q.tensor_tensor(
                out=uf[:, 0:RW_], in0=pmm[m % 2][:, 0:RW_], in1=rmrep[:, 0:RW_],
                op=ALU.mult))
            p.step("vector", lambda q, m=m: q.tensor_tensor(
                out=uf[:, 0:RW_], in0=uf[:, 0:RW_],
                in1=xt7[:, m * RW_:(m + 1) * RW_], op=ALU.add))
            a, bnd = outsl
            p.step("sync", lambda q, m=m, S=S, a=a, bnd=bnd: q.dma_start(
                out=xo.rearrange("(c p) t -> p c t", p=128)
                [:, m, S + a: S + bnd],
                in_=uf[:, a:bnd]), dma=True)

    return p.finalize()


def run_H0(inputs, X7):
    nc = _BUILD_CACHE.setdefault("H0", build_H0())
    cw = np.asarray(inputs["L3_ffn_conv_w"], np.float32)[:, 0, :]
    base = {
        "w1T": np.ascontiguousarray(np.asarray(inputs["L3_ffn_w1"], np.float32).T),
        "w2T": np.ascontiguousarray(np.asarray(inputs["L3_ffn_w2"], np.float32).T),
        "convw": np.ascontiguousarray(
            cw.reshape(32, 128, 3).transpose(1, 0, 2).reshape(128, 96)),
        "convb": _chunkvec(inputs["L3_ffn_conv_b"], 32),
        "ident": np.eye(128, dtype=np.float32),
    }
    in_maps = []
    for c in range(NCORES):
        b, s = c // 2, c % 2
        m = dict(base)
        if s == 0:
            seg = np.concatenate([np.zeros((DM, 1), np.float32), X7[b][:, 0:TH + 1]],
                                 axis=1)
        else:
            seg = np.concatenate([X7[b][:, TH - 1: T],
                                  np.zeros((DM, 1), np.float32)], axis=1)
        m["xin"] = np.ascontiguousarray(seg)
        in_maps.append(m)
    res = run_bass_kernel_spmd(nc, in_maps, core_ids=list(range(NCORES)))
    XO = np.empty((B, DM, T), np.float32)
    for c in range(NCORES):
        b, s = c // 2, c % 2
        XO[b, :, s * TH:(s + 1) * TH] = res.results[c]["xo"]
    return XO



def kernel(**inputs):
    inputs = {k: np.asarray(v) for k, v in inputs.items()}
    X = np.ascontiguousarray(
        np.transpose(np.asarray(inputs["L0_attn_x"], np.float32), (0, 2, 1)))
    U0, Z, XC, DT, BC = run_A0(inputs, np.transpose(X, (0, 2, 1)))
    YS = run_A1(inputs, DT, XC, BC)
    HR, MIN, CFM = run_A2a(inputs, U0, YS, XC, Z)
    X1 = run_A2b(inputs, HR, MIN, CFM, X)
    del U0, Z, XC, DT, BC, YS, HR, MIN, CFM
    X2 = run_F0(inputs, X1)
    del X1
    QKV, GATE = run_B0(inputs, X2)
    ATT = run_B1(inputs, QKV, GATE)
    X3 = run_B2(inputs, X2, ATT)
    del QKV, GATE, ATT, X2
    X4 = run_C0(inputs, X3)
    del X3
    RET = run_D0a(inputs, X4)
    X5 = run_D0b(inputs, X4, RET)
    del RET, X4
    QM, KM, VM, RW = run_E0(inputs, X5)
    AT2 = run_E1(inputs, QM, KM, VM)
    X6 = run_E2(inputs, X5, AT2, RW)
    del QM, KM, VM, RW, AT2, X5
    QFK, KL, VL = run_G0(inputs, X6)
    Y3 = run_G1(inputs, QFK, KL, VL)
    X7 = run_G2(inputs, X6, Y3)
    del QFK, KL, VL, Y3, X6
    X8 = run_H0(inputs, X7)
    del X7
    return np.ascontiguousarray(np.transpose(X8, (0, 2, 1)))
